# version 08
# baseline (speedup 1.0000x reference)
import contextlib
import numpy as np
import ml_dtypes
import concourse.bass as bass
import concourse.mybir as mybir
from concourse.bass_utils import run_bass_kernel_spmd

F32 = mybir.dt.float32
BF16 = mybir.dt.bfloat16
AF = mybir.ActivationFunctionType
ALU = mybir.AluOpType

D = 1024
S = 2048
SEQ_PER_CORE = 4
NCORES = 8
NH = 16
DH = 64
NEG = -30000.0


class Sched:

    ENGINES = ("sync", "tensor", "scalar", "vector", "gpsimd")

    def __init__(self, nc):
        self.nc = nc
        self.items = []
        self.counts = {}
        self.sems = {}
        self.fence_waits = {}
        self.emitted_upto = 0
        self.e_waited = {e: {} for e in self.ENGINES}
        self.e_selfcnt = {e: {} for e in self.ENGINES}

    def add_sem(self, name, handle):
        self.sems[name] = handle
        self.counts[name] = 0

    def op(self, engine, emit, waits=(), incs=(), hard_waits=(), nofence=False):
        after = {}
        w = {} if nofence else dict(self.fence_waits)
        for semname, val in waits:
            w[semname] = max(w.get(semname, 0), val)
        hw = {}
        for semname, val in hard_waits:
            hw[semname] = max(hw.get(semname, 0), val)
            w.pop(semname, None) if False else None
        inc_list = []
        for semname, amt in incs:
            self.counts[semname] += amt
            after[semname] = self.counts[semname]
            inc_list.append((semname, amt))
        self.items.append((engine, emit, sorted(w.items()), inc_list,
                           sorted(hw.items())))
        return after

    def val(self, semname):
        return self.counts[semname]

    FENCE_EXEMPT = ("ld_prep", "ldt_prep", "st_prep")

    def fence(self):
        self.fence_waits = {k: v for k, v in self.counts.items()
                            if k not in self.FENCE_EXEMPT}
        for s_, v in getattr(self, "sticky_waits", {}).items():
            self.fence_waits[s_] = max(self.fence_waits.get(s_, 0), v)

    def fence_include_val(self, sem, val):
        if not hasattr(self, "sticky_waits"):
            self.sticky_waits = {}
        self.sticky_waits[sem] = max(self.sticky_waits.get(sem, 0), val)
        self.fence_waits[sem] = max(self.fence_waits.get(sem, 0), val)

    def emit(self, block):
        new_items = self.items[self.emitted_upto:]
        self.emitted_upto = len(self.items)
        per_engine = {e: [] for e in self.ENGINES}
        for it in new_items:
            per_engine[it[0]].append(it)
        sems = self.sems

        def mk_body(items, waited, selfcnt):
            def body(eng):
                for (_e, emit, waits, incs, hard) in items:
                    for semname, val in waits:
                        if val <= waited.get(semname, 0) or val <= selfcnt.get(semname, 0):
                            continue
                        eng.wait_ge(sems[semname], val)
                        waited[semname] = val
                    for semname, val in hard:
                        if val <= waited.get(semname, 0):
                            continue
                        eng.wait_ge(sems[semname], val)
                        waited[semname] = val
                    instr = emit(eng)
                    for semname, amt in incs:
                        if instr is None:
                            eng.sem_inc(sems[semname], amt)
                        else:
                            instr = instr.then_inc(sems[semname], amt)
                        selfcnt[semname] = selfcnt.get(semname, 0) + amt
            return body

        for ename in self.ENGINES:
            if per_engine[ename]:
                getattr(block, ename)(mk_body(
                    per_engine[ename], self.e_waited[ename], self.e_selfcnt[ename]))


def host_constants():
    c = {}
    c["c_ident_f32"] = np.eye(128, dtype=np.float32)
    c["c_ident_bf"] = np.eye(128).astype(ml_dtypes.bfloat16)
    c["c_ones_row"] = np.ones((1, 512), dtype=np.float32)
    mask = np.zeros((4, 128, 512), dtype=np.float32)
    for r in range(4):
        for i in range(128):
            mask[r, i, : min(r * 128 + i, 512)] = NEG
    c["c_maskT"] = mask.astype(ml_dtypes.bfloat16)
    c["c_iota8"] = np.tile(np.arange(8, dtype=np.float32), (128, 1))
    inv_freq = 1.0 / (10000.0 ** (np.arange(0, DH, 2, dtype=np.float32) / DH))
    t = np.arange(S, dtype=np.float32)
    freqs = np.outer(t, inv_freq)
    emb = np.concatenate([freqs, freqs], axis=1)
    cos = np.cos(emb).T.astype(np.float32)
    sin = np.sin(emb).T.astype(np.float32)
    c["c_rope_cos"] = np.concatenate([cos, cos], axis=0).astype(ml_dtypes.bfloat16)
    c["c_rope_sin"] = np.concatenate([sin, sin], axis=0).astype(ml_dtypes.bfloat16)
    R = np.zeros((128, 128), dtype=np.float32)
    for h0 in (0, 64):
        for d_ in range(32):
            R[h0 + d_, h0 + 32 + d_] = -1.0
            R[h0 + 32 + d_, h0 + d_] = 1.0
    c["c_R_rot_T"] = R.T.copy().astype(ml_dtypes.bfloat16)
    return c


CONST_SPECS = {
    "c_ident_f32": ([128, 128], F32),
    "c_ident_bf": ([128, 128], BF16),
    "c_ones_row": ([1, 512], F32),
    "c_maskT": ([4, 128, 512], BF16),
    "c_iota8": ([128, 8], F32),
    "c_rope_cos": ([128, S], BF16),
    "c_rope_sin": ([128, S], BF16),
    "c_R_rot_T": ([128, 128], BF16),
}


def emit_prep_T(sch, stg2, src_bf, dst_T, nrow, ncol, cast_ld, state):
    pend = state.get("pend")
    for cc in range(ncol // 128):
        for rr in range(0, nrow, 512):
            rw = min(512, nrow - rr)
            par = state["i"] % 2
            ld = sch.op("sync", lambda e, cc=cc, rr=rr, rw=rw, par=par: e.dma_start(
                out=stg2[par][:, 0:rw],
                in_=src_bf[rr:rr + rw, cc * 128:(cc + 1) * 128],
                transpose=True),
                waits=[("st_prep", state.get(par, 0)), ("ld_prep", cast_ld)],
                incs=[("ldt_prep", 16)], nofence=True)
            if pend is not None:
                (pcc, prr, prw, ppar, pld, pdst) = pend
                st = sch.op("gpsimd", lambda e, pcc=pcc, prr=prr, prw=prw, ppar=ppar,
                            pdst=pdst: e.dma_start(
                    out=pdst[pcc * 128:(pcc + 1) * 128, prr:prr + prw],
                    in_=stg2[ppar][:, 0:prw]),
                    waits=[("ldt_prep", pld)], incs=[("st_prep", 16)], nofence=True)
                state[ppar] = st["st_prep"]
            pend = (cc, rr, rw, par, ld["ldt_prep"], dst_T)
            state["i"] += 1
    state["pend"] = pend


def flush_prep_T(sch, stg2, state):
    pend = state.pop("pend", None)
    if pend is not None:
        (pcc, prr, prw, ppar, pld, pdst) = pend
        st = sch.op("gpsimd", lambda e: e.dma_start(
            out=pdst[pcc * 128:(pcc + 1) * 128, prr:prr + prw],
            in_=stg2[ppar][:, 0:prw]),
            waits=[("ldt_prep", pld)], incs=[("st_prep", 16)], nofence=True)
        state[ppar] = st["st_prep"]


def declare_const_dram(nc, names):
    return {name: nc.dram_tensor(name, CONST_SPECS[name][0], CONST_SPECS[name][1],
                                 kind="ExternalInput").ap()
            for name in names}


def np_split_tokens(x):
    return [np.ascontiguousarray(
        x[i * SEQ_PER_CORE:(i + 1) * SEQ_PER_CORE].reshape(-1, D))
        for i in range(NCORES)]


class AttnProgram:

    def __init__(self, nc, layer, sch=None, pre=""):
        self.nc = nc
        self.layer = layer
        self.pre = pre
        self.CONSTS = ("c_ident_f32", "c_maskT") if layer == 0 else (
            "c_ident_f32", "c_maskT", "c_rope_cos", "c_rope_sin", "c_R_rot_T")
        self.sch = sch if sch is not None else Sched(nc)
        self.sb = {}

    def sbuf(self, ctx, name, shape, dt, **kw):
        t = ctx.enter_context(self.nc.sbuf_tensor(self.pre + name, shape, dt, **kw))
        self.sb[name] = t
        return t

    def build(self):
        nc, sch = self.nc, self.sch
        ctx = contextlib.ExitStack()
        cdram = declare_const_dram(nc, self.CONSTS)
        x_in = nc.dram_tensor("x", [SEQ_PER_CORE * S, D], F32, kind="ExternalInput").ap()
        y_out = nc.dram_tensor("y", [SEQ_PER_CORE * S, D], F32, kind="ExternalOutput").ap()
        with ctx:
            block = ctx.enter_context(nc.Block())
            for s_ in ("ld_sync", "ld_gp", "ld_prep", "ldt_prep", "st_prep",
                        "pe", "act", "dve", "pool", "st"):
                sch.add_sem(s_, ctx.enter_context(nc.semaphore(s_)))
            self.prep_stg = [ctx.enter_context(nc.sbuf_tensor(
                self.pre + f"prep_stg{i}", [128, 512], BF16,
                is_transpose_dest=True)) for i in range(2)]
            with contextlib.ExitStack() as pctx:
                self.emit_body(pctx, x_in, y_out, cdram)
                st_total = sch.val("st")
                sch.op("gpsimd", lambda e: e.nop(), waits=[("st", st_total)])
                sch.emit(block)
        return nc

    def declare_w(self):
        if getattr(self, "_declared", False):
            return
        self._declared = True
        nc, pre = self.nc, self.pre
        self.w_dram = {wn: nc.dram_tensor(pre + wn, [D, D], F32,
                                          kind="ExternalInput").ap()
                       for wn in ("Wq", "Wk", "Wv", "Wo")}
        if self.layer == 0:
            self.b_dram = {bn: nc.dram_tensor(pre + bn, [D], F32,
                                              kind="ExternalInput").ap()
                           for bn in ("bq", "bk", "bv", "bo")}
            self.w_src = self.w_dram
        else:
            self.wT_dram = {wn: nc.dram_tensor(pre + "T_" + wn, [D, D], BF16,
                                               kind="Internal").ap()
                            for wn in ("Wq", "Wk", "Wv", "Wo")}
            self.w_bf = {wn: nc.dram_tensor(pre + "bf_" + wn, [D, D], BF16,
                                            kind="Internal").ap()
                         for wn in ("Wq", "Wk", "Wv", "Wo")}
            self.w_src = self.wT_dram

    def emit_prep(self):
        self.declare_w()
        sch = self.sch
        self.prep_emitted = True
        cast_ld = 0
        for wn in ("Wq", "Wk", "Wv", "Wo"):
            cl = sch.op("gpsimd", lambda e, wn=wn: e.dma_start(
                out=self.w_bf[wn], in_=self.w_dram[wn]), incs=[("ld_prep", 16)],
                nofence=True)
            cast_ld = cl["ld_prep"]
        state = {"i": 0, 0: sch.val("st_prep"), 1: sch.val("st_prep")}
        for wn in ("Wq", "Wk", "Wv", "Wo"):
            emit_prep_T(sch, self.prep_stg, self.w_bf[wn], self.wT_dram[wn],
                        D, D, cast_ld, state)
        flush_prep_T(sch, self.prep_stg, state)
        self.prep_st_total = sch.val("st_prep")

    def emit_body(self, ctx, x_in, y_out, cdram):
        nc, sch, sb = self.nc, self.sch, self.sb
        pre = self.pre
        self.declare_w()
        w_dram = self.w_dram
        if self.layer == 0:
            b_dram = self.b_dram
        if True:
            for name in self.CONSTS:
                shape, dt = CONST_SPECS[name]
                dram = cdram[name]
                if name == "c_maskT":
                    t = self.sbuf(ctx, name + "_sb", [128, 4 * 512], dt)
                    self.sb[name] = t
                    for r in range(4):
                        sch.op("sync", lambda e, t=t, r=r, dram=dram: e.dma_start(
                            out=t[:, r * 512:(r + 1) * 512], in_=dram[r]),
                            incs=[("ld_sync", 16)])
                else:
                    t = self.sbuf(ctx, name + "_sb", list(shape), dt)
                    self.sb[name] = t
                    sch.op("sync", lambda e, t=t, dram=dram: e.dma_start(
                        out=t[:], in_=dram), incs=[("ld_sync", 16)])

            ones_col = self.sbuf(ctx, "ones_col", [128, 1], F32)
            onesr = self.sbuf(ctx, "onesr", [1, 128], F32)
            sch.op("vector", lambda e: e.memset(ones_col[:], 1.0), incs=[("dve", 1)])
            sch.op("vector", lambda e: e.memset(onesr[:], 1.0), incs=[("dve", 1)])

            x_res = [self.sbuf(ctx, f"x_res{c}", [128, S], F32) for c in range(8)]
            xn = [self.sbuf(ctx, f"xn{c}", [128, S], BF16) for c in range(8)]
            K_fm = [self.sbuf(ctx, f"K{c}", [128, S], BF16) for c in range(8)]
            V_tm = self.sbuf(ctx, "V", [128, 16 * NH * 65], BF16)
            Qb = self.sbuf(ctx, "Qb", [128, 8 * 512], BF16, is_transpose_dest=True)
            ctxf = [self.sbuf(ctx, f"ctx{c}", [128, 512], BF16) for c in range(8)]
            PT = [self.sbuf(ctx, f"PT{p}", [128, 512], BF16) for p in range(2)]
            x2 = self.sbuf(ctx, "ln_x2", [128, 1024], F32)
            if self.layer == 0:
                rb_t = self.sbuf(ctx, "rb", [64, 512], F32)
                rb = rb_t[:, :]
            else:
                rb = x2[0:64, 512:1024]
            rows = self.sbuf(ctx, "ln_rows", [1, 1536], F32)
            w_och = [self.sbuf(ctx, f"w_och{p}", [128, 8, 128], BF16) for p in range(2)]
            if self.layer == 0:
                bias_col = {bn: self.sbuf(ctx, f"{bn}_col", [128, 8], F32)
                            for bn in ("bq", "bk", "bo")}
                bvb = self.sbuf(ctx, "bvb", [128, D], BF16)
            else:
                bias_col = None
                bvb = None
            brow = rows
            stage = x2

            ps_stats = ctx.enter_context(nc.psum_tensor(self.pre + "ps_stats", [33, 512], F32))
            ps_mm = [ctx.enter_context(nc.psum_tensor(self.pre + f"ps_mm{i}", [128, 512], F32))
                     for i in range(2)]
            ps_S = [ctx.enter_context(nc.psum_tensor(self.pre + f"ps_S{i}", [128, 512], F32))
                    for i in range(2)]
            ps_ctx = ctx.enter_context(nc.psum_tensor(self.pre + "ps_ctx", [128, 512], F32))
            ps_t = ctx.enter_context(nc.psum_tensor(self.pre + "ps_t", [128, 512], F32))
            ps_t2 = ctx.enter_context(nc.psum_tensor(self.pre + "ps_t2", [128, 512], F32))
            self.ps_tp = [ps_t, ps_t2]
            self.ps_mm, self.ps_S, self.ps_ctx, self.ps_t = ps_mm, ps_S, ps_ctx, ps_t
            self._stats = ps_stats

            if self.layer == 0:
                for bn in ("bq", "bk", "bo"):
                    for c in range(8):
                        sch.op("sync", lambda e, bn=bn, c=c: e.dma_start(
                            out=bias_col[bn][:, c:c + 1],
                            in_=b_dram[bn][c * 128:(c + 1) * 128]
                            .rearrange("(p o) -> p o", o=1)),
                            incs=[("ld_sync", 16)])
                sch.fence()
                sch.op("vector", lambda e: e.tensor_scalar_mul(
                    bias_col["bq"][:], bias_col["bq"][:], 0.125), incs=[("dve", 1)])
                for half in range(2):
                    ldh = sch.op("sync", lambda e, half=half: e.dma_start(
                        out=brow[0:1, 0:512],
                        in_=b_dram["bv"][half * 512:(half + 1) * 512]
                        .rearrange("(o d) -> o d", o=1)),
                        incs=[("ld_sync", 16)])
                    pe = sch.op("tensor", lambda e: e.matmul(
                        ps_t[:, 0:512], onesr[0:1, :], brow[0:1, 0:512],
                        start=True, stop=True),
                        waits=[("ld_sync", ldh["ld_sync"])], incs=[("pe", 1)])
                    sch.op("scalar", lambda e, half=half: e.copy(
                        bvb[:, half * 512:(half + 1) * 512], ps_t[:, 0:512]),
                        waits=[("pe", pe["pe"])], incs=[("act", 1)])
                    sch.fence()
            else:
                if not getattr(self, "prep_emitted", False):
                    self.emit_prep()
                sch.fence()
                sch.fence_include_val("st_prep", self.prep_st_total)

            for sq in range(SEQ_PER_CORE):
                self.seq(sq, x_in, y_out,
                         x_res, xn, K_fm, V_tm, Qb, ctxf, PT, rows, rb, x2,
                         w_och, bias_col, bvb, stage)


    def ln(self, x_res, xn, t0, W, ps_stats, ps_abc):
        sch, sb = self.sch, self.sb
        ones_col, onesr = sb["ones_col"], sb["onesr"]
        x2, rows = sb["ln_x2"], sb["ln_rows"]
        for c in range(8):
            sch.op("tensor", lambda e, c=c: e.matmul(
                ps_stats[0:1, 0:W], ones_col[:], x_res[c][:, t0:t0 + W],
                start=(c == 0), stop=(c == 7)),
                waits=[("dve", sch.val("dve"))] if c == 0 else (),
                incs=[("pe", 1)] if c == 7 else ())
        pe_entry = sch.val("pe")
        dve_entry = sch.val("dve")
        pe_gate = {}
        for c in range(8):
            half = c % 2
            w = ([("pe", pe_gate[half])] if half in pe_gate
                 else [("pe", pe_entry), ("dve", dve_entry)])
            a = sch.op("scalar", lambda e, c=c, half=half: e.square(
                x2[:, half * 512:half * 512 + W], x_res[c][:, t0:t0 + W]),
                waits=w, incs=[("act", 1)])
            p = sch.op("tensor", lambda e, c=c, half=half: e.matmul(
                ps_stats[32:33, 0:W], ones_col[:], x2[:, half * 512:half * 512 + W],
                start=(c == 0), stop=(c == 7)),
                waits=[("act", a["act"])], incs=[("pe", 1)])
            pe_gate[half] = p["pe"]
        pe_done = sch.val("pe")
        mean_r = rows[0:1, 0:W]
        a_r = rows[0:1, 512:512 + W]
        b_r = rows[0:1, 1024:1024 + W]
        sch.op("vector", lambda e: e.tensor_scalar_mul(
            mean_r, ps_stats[0:1, 0:W], 1.0 / D),
            waits=[("pe", pe_done)], incs=[("dve", 1)])
        sch.op("vector", lambda e: e.tensor_scalar_mul(
            b_r, ps_stats[32:33, 0:W], 1.0 / D), incs=[("dve", 1)])
        sch.op("vector", lambda e: e.scalar_tensor_tensor(
            a_r, mean_r, -1.0, mean_r, ALU.mult, ALU.mult), incs=[("dve", 1)])
        sch.op("vector", lambda e: e.tensor_add(b_r, b_r, a_r), incs=[("dve", 1)])
        v = sch.op("vector", lambda e: e.tensor_scalar_add(b_r, b_r, 1e-5),
                   incs=[("dve", 1)])
        sv = sch.op("scalar", lambda e: e.activation(a_r, b_r, AF.Sqrt),
                    waits=[("dve", v["dve"])], incs=[("act", 1)])
        sch.op("vector", lambda e: e.reciprocal(a_r, a_r),
               waits=[("act", sv["act"])], incs=[("dve", 1)])
        v = sch.op("vector", lambda e: e.scalar_tensor_tensor(
            b_r, mean_r, -1.0, a_r, ALU.mult, ALU.mult), incs=[("dve", 1)])
        dve_rows = v["dve"]
        pa = sch.op("tensor", lambda e: e.matmul(
            ps_abc[0][:, 0:W], onesr[0:1, :], a_r, start=True, stop=True),
            waits=[("dve", dve_rows)], incs=[("pe", 1)])
        pb = sch.op("tensor", lambda e: e.matmul(
            ps_abc[1][:, 0:W], onesr[0:1, :], b_r, start=True, stop=True),
            incs=[("pe", 1)])
        pe_bc = pb["pe"]
        gate = {}
        for c in range(8):
            half = c % 2
            w = [("pe", pe_bc)] + ([("dve", gate[half])] if half in gate else [])
            sch.op("vector", lambda e, c=c, half=half: e.tensor_mul(
                x2[:, half * 512:half * 512 + W], x_res[c][:, t0:t0 + W],
                ps_abc[0][:, 0:W]), waits=w, incs=[("dve", 1)])
            g = sch.op("vector", lambda e, c=c, half=half: e.tensor_tensor(
                xn[c][:, t0:t0 + W], x2[:, half * 512:half * 512 + W],
                ps_abc[1][:, 0:W], ALU.add), incs=[("dve", 1)])
            gate[half] = g["dve"]

    def rope(self, buf, b0, t0, W, evac_act):
        sch, sb = self.sch, self.sb
        x2 = sb["ln_x2"]
        ps = self.ps_tp[0]
        gate = getattr(self, "_rope_gate", 0)
        pe = sch.op("tensor", lambda e: e.matmul(
            ps[:, 0:W], sb["c_R_rot_T"][:], buf[:, b0:b0 + W], start=True, stop=True),
            waits=[("act", evac_act), ("dve", gate)], incs=[("pe", 1)])
        sch.op("vector", lambda e: e.tensor_mul(
            x2[:, 0:W], buf[:, b0:b0 + W], sb["c_rope_cos"][:, t0:t0 + W]),
            waits=[("act", evac_act)], incs=[("dve", 1)])
        sch.op("vector", lambda e: e.tensor_mul(
            x2[:, 512:512 + W], ps[:, 0:W], sb["c_rope_sin"][:, t0:t0 + W]),
            waits=[("pe", pe["pe"])], incs=[("dve", 1)])
        v3 = sch.op("vector", lambda e: e.tensor_add(
            buf[:, b0:b0 + W], x2[:, 0:W], x2[:, 512:512 + W]),
            waits=[("pe", pe["pe"])], incs=[("dve", 1)])
        self._rope_gate = v3["dve"]

    def seq(self, sq, x_in, y_out,
            x_res, xn, K_fm, V_tm, Qb, ctxf, PT, rows, rb, x2,
            w_och, bias_col, bvb, stage):
        nc, sch, sb = self.nc, self.sch, self.sb
        ps_mm, ps_S, ps_ctx, ps_t = self.ps_mm, self.ps_S, self.ps_ctx, self.ps_t
        base = sq * S
        ident = sb["c_ident_f32"]
        rrow = rows
        lay = self.layer

        sch.fence()
        pe_after_tt = 0
        copy_done = {0: 0, 1: 0}
        for tt in range(16):
            ld = sch.op("sync", lambda e, tt=tt: e.dma_start(
                out=stage[:], in_=x_in[base + tt * 128: base + (tt + 1) * 128, :]),
                waits=[("pe", pe_after_tt)], incs=[("ld_sync", 16)])
            for c in range(8):
                par = c % 2
                pe = sch.op("tensor", lambda e, c=c, par=par: e.transpose(
                    self.ps_tp[par][:, 0:128], stage[:, c * 128:(c + 1) * 128],
                    ident[:]),
                    waits=[("ld_sync", ld["ld_sync"]), ("dve", copy_done[par])],
                    incs=[("pe", 1)])
                cp = sch.op("vector", lambda e, c=c, tt=tt, par=par: e.tensor_copy(
                    x_res[c][:, tt * 128:(tt + 1) * 128], self.ps_tp[par][:, 0:128]),
                    waits=[("pe", pe["pe"])], incs=[("dve", 1)])
                copy_done[par] = cp["dve"]
            pe_after_tt = sch.val("pe")
        sch.fence()

        for tch in range(4):
            self.ln(x_res, xn, tch * 512, 512, self._stats, ps_S)
        sch.fence()

        def load_w_cols(wname, och, buf, extra_waits=()):
            srcw = self.w_src[wname].rearrange("(c p) o -> p c o", p=128)[
                :, :, och * 128:(och + 1) * 128]
            return sch.op("gpsimd", lambda e, srcw=srcw, buf=buf: e.dma_start(
                out=buf[:, :, 0:128], in_=srcw),
                waits=list(extra_waits), incs=[("ld_gp", 16)])

        prev_w = {0: 0, 1: 0}
        evac_done = {0: 0, 1: 0}
        for och in range(8):
            par = och % 2
            ld = load_w_cols("Wk", och, w_och[par],
                             extra_waits=[("pe", prev_w[par])])
            for tch in range(4):
                mmpar = tch % 2
                for c in range(8):
                    pe = sch.op("tensor", lambda e, tch=tch, c=c, par=par,
                                mmpar=mmpar: e.matmul(
                        ps_mm[mmpar][:], w_och[par][:, c, 0:128],
                        xn[c][:, tch * 512:(tch + 1) * 512],
                        start=(c == 0), stop=(c == 7)),
                        waits=([("ld_gp", ld["ld_gp"]),
                                ("act", evac_done[mmpar])] if c == 0 else ()),
                        incs=[("pe", 1)] if c == 7 else ())
                if lay == 0:
                    sch.op("scalar", lambda e, och=och, tch=tch, mmpar=mmpar:
                           e.activation(
                        K_fm[och][:, tch * 512:(tch + 1) * 512], ps_mm[mmpar][:],
                        AF.Identity, bias=bias_col["bk"][:, och:och + 1], scale=1.0),
                        waits=[("pe", pe["pe"])], incs=[("act", 1)])
                else:
                    ac = sch.op("scalar", lambda e, och=och, tch=tch, mmpar=mmpar:
                                e.copy(K_fm[och][:, tch * 512:(tch + 1) * 512],
                                       ps_mm[mmpar][:]),
                                waits=[("pe", pe["pe"])], incs=[("act", 1)])
                    self.rope(K_fm[och], tch * 512, tch * 512, 512, ac["act"])
                evac_done[mmpar] = sch.val("act")
            prev_w[par] = sch.val("pe")
        sch.fence()

        vevac = {0: 0, 1: 0}
        for och in range(8):
            par = och % 2
            ld = load_w_cols("Wv", och, w_och[par],
                             extra_waits=[("pe", prev_w[par])])
            for tt in range(16):
                mmpar = tt % 2
                for c in range(8):
                    pe = sch.op("tensor", lambda e, tt=tt, c=c, par=par,
                                mmpar=mmpar: e.matmul(
                        ps_mm[mmpar][:, 0:128], xn[c][:, tt * 128:(tt + 1) * 128],
                        w_och[par][:, c, 0:128], start=(c == 0), stop=(c == 7)),
                        waits=([("ld_gp", ld["ld_gp"]),
                                ("dve", vevac[mmpar])] if c == 0 else ()),
                        incs=[("pe", 1)] if c == 7 else ())
                vap = V_tm[:].rearrange("p (t h x) -> p t h x", t=16, h=NH)[
                    :, tt, 2 * och:2 * och + 2, 0:64]
                psap = ps_mm[mmpar][:, 0:128].rearrange("p (h x) -> p h x", h=2)
                if lay == 0:
                    bvap = bvb[:, och * 128:(och + 1) * 128].rearrange(
                        "p (h x) -> p h x", h=2)
                    sch.op("vector", lambda e, vap=vap, psap=psap, bvap=bvap:
                           e.tensor_tensor(vap, psap, bvap, ALU.add),
                           waits=[("pe", pe["pe"])], incs=[("dve", 1)])
                else:
                    sch.op("vector", lambda e, vap=vap, psap=psap: e.tensor_copy(
                        vap, psap),
                        waits=[("pe", pe["pe"])], incs=[("dve", 1)])
                vevac[mmpar] = sch.val("dve")
            prev_w[par] = sch.val("pe")
        onesap = V_tm[:].rearrange("p (t h x) -> p t h x", t=16, h=NH)[:, :, :, 64:65]
        sch.op("vector", lambda e: e.memset(onesap, 1.0), incs=[("dve", 1)])
        sch.fence()

        ps_S_reader = {0: 0, 1: 0}
        PT_reader = {0: 0, 1: 0}
        ctx_reader_pe = 0
        S_reader_pe = 0
        resid_done = {0: 0, 1: 0}
        q_evac = {0: 0, 1: 0}
        for qb in range(4):
            q0 = qb * 512
            for och in range(8):
                par = och % 2
                ld = load_w_cols("Wq", och, w_och[par],
                                 extra_waits=[("pe", prev_w[par])])
                for c in range(8):
                    pe = sch.op("tensor", lambda e, c=c, par=par, q0=q0: e.matmul(
                        ps_mm[par][:], w_och[par][:, c, 0:128],
                        xn[c][:, q0:q0 + 512], start=(c == 0), stop=(c == 7)),
                        waits=([("ld_gp", ld["ld_gp"]),
                                ("act", q_evac[par]),
                                ("dve", resid_done[par])] if c == 0 else ()),
                        incs=[("pe", 1)] if c == 7 else ())
                if lay == 0:
                    qa = sch.op("scalar", lambda e, och=och, par=par: e.activation(
                        Qb[:, och * 512:(och + 1) * 512], ps_mm[par][:],
                        AF.Identity, bias=bias_col["bq"][:, och:och + 1], scale=0.125),
                        waits=[("pe", pe["pe"]), ("pe", S_reader_pe)],
                        incs=[("act", 1)])
                else:
                    qa = sch.op("scalar", lambda e, och=och, par=par: e.activation(
                        Qb[:, och * 512:(och + 1) * 512], ps_mm[par][:],
                        AF.Copy, scale=0.125),
                        waits=[("pe", pe["pe"]), ("pe", S_reader_pe)],
                        incs=[("act", 1)])
                    self.rope(Qb, och * 512, q0, 512, qa["act"])
                q_evac[par] = sch.val("act")
                prev_w[par] = sch.val("pe")
            q_ready = sch.val("act") if lay == 0 else 0
            q_ready_dve = sch.val("dve")

            nkc = (qb + 1) * 4
            for h in range(NH):
                chunk, hb = h // 2, 64 * (h % 2)
                for kc in range(nkc):
                    par = kc % 2
                    wq = ([("act", q_ready)] if lay == 0 else
                          [("dve", q_ready_dve)])
                    pe = sch.op("tensor", lambda e, kc=kc, chunk=chunk, hb=hb,
                                par=par: e.matmul(
                        ps_S[par][:], K_fm[chunk][hb:hb + 64, kc * 128:(kc + 1) * 128],
                        Qb[hb:hb + 64, chunk * 512:(chunk + 1) * 512],
                        start=True, stop=True),
                        waits=[("act", ps_S_reader[par])] + (wq if kc == 0 else []),
                        incs=[("pe", 1)])
                    waits = [("pe", pe["pe"])]
                    if kc >= nkc - 4:
                        r = kc - (nkc - 4)
                        dv = sch.op("vector", lambda e, par=par, r=r: e.tensor_add(
                            ps_S[par][:], ps_S[par][:],
                            sb["c_maskT"][:, r * 512:(r + 1) * 512]),
                            waits=waits, incs=[("dve", 1)])
                        waits = [("dve", dv["dve"])]
                    ac = sch.op("scalar", lambda e, par=par: e.activation(
                        PT[par][:], ps_S[par][:], AF.Exp),
                        waits=waits + [("pe", PT_reader[par])], incs=[("act", 1)])
                    ps_S_reader[par] = ac["act"]
                    pv = sch.op("tensor", lambda e, kc=kc, h=h, par=par,
                                nkc=nkc: e.matmul(
                        ps_ctx[0:65, :],
                        V_tm[:, kc * NH * 65 + h * 65: kc * NH * 65 + (h + 1) * 65],
                        PT[par][:], start=(kc == 0), stop=(kc == nkc - 1)),
                        waits=[("act", ac["act"])] + (
                            [("dve", sch.val("dve")), ("pe", ctx_reader_pe)]
                            if kc == 0 else []),
                        incs=[("pe", 1)])
                    PT_reader[par] = pv["pe"]
                pv_done = sch.val("pe")
                dv = sch.op("vector", lambda e: e.reciprocal(
                    rrow[0:1, 0:512], ps_ctx[64:65, :]),
                    waits=[("pe", pv_done)], incs=[("dve", 1)])
                pe = sch.op("tensor", lambda e: e.matmul(
                    ps_t[0:64, 0:512], sb["onesr"][0:1, 0:64], rrow[0:1, 0:512],
                    start=True, stop=True),
                    waits=[("dve", dv["dve"])], incs=[("pe", 1)])
                ac = sch.op("scalar", lambda e: e.copy(rb, ps_t[0:64, 0:512]),
                            waits=[("pe", pe["pe"])], incs=[("act", 1)])
                sch.op("vector", lambda e, chunk=chunk, hb=hb: e.tensor_mul(
                    ctxf[chunk][hb:hb + 64, :], ps_ctx[0:64, :], rb),
                    waits=[("act", ac["act"]), ("pe", ctx_reader_pe)],
                    incs=[("dve", 1)])
            S_reader_pe = sch.val("pe")
            ctx_ready = sch.val("dve")

            for och in range(8):
                par = och % 2
                ld = load_w_cols("Wo", och, w_och[par],
                                 extra_waits=[("pe", prev_w[par])])
                for c in range(8):
                    pe = sch.op("tensor", lambda e, c=c, par=par: e.matmul(
                        ps_mm[par][:], w_och[par][:, c, 0:128],
                        ctxf[c][:, 0:512], start=(c == 0), stop=(c == 7)),
                        waits=([("ld_gp", ld["ld_gp"]), ("dve", ctx_ready),
                                ("dve", resid_done[par]),
                                ("act", q_evac[par])] if c == 0 else ()),
                        incs=[("pe", 1)] if c == 7 else ())
                if lay == 0:
                    sch.op("vector", lambda e, och=och, par=par, q0=q0:
                           e.scalar_tensor_tensor(
                        x_res[och][:, q0:q0 + 512], ps_mm[par][:],
                        bias_col["bo"][:, och:och + 1], x_res[och][:, q0:q0 + 512],
                        ALU.add, ALU.add),
                        waits=[("pe", pe["pe"])], incs=[("dve", 1)])
                else:
                    sch.op("vector", lambda e, och=och, par=par, q0=q0: e.tensor_add(
                        x_res[och][:, q0:q0 + 512], ps_mm[par][:],
                        x_res[och][:, q0:q0 + 512]),
                        waits=[("pe", pe["pe"])], incs=[("dve", 1)])
                resid_done[par] = sch.val("dve")
                prev_w[par] = sch.val("pe")
            ctx_reader_pe = sch.val("pe")

        copy_done = {0: 0, 1: 0}
        for tt in range(16):
            st_before = sch.val("st")
            for c in range(8):
                par = c % 2
                pe = sch.op("tensor", lambda e, c=c, tt=tt, par=par: e.transpose(
                    self.ps_tp[par][:, 0:128], x_res[c][:, tt * 128:(tt + 1) * 128],
                    ident[:]),
                    waits=[("st", st_before), ("dve", copy_done[par])],
                    incs=[("pe", 1)])
                cp = sch.op("vector", lambda e, c=c, par=par: e.tensor_copy(
                    stage[:, c * 128:(c + 1) * 128], self.ps_tp[par][:, 0:128]),
                    waits=[("pe", pe["pe"])], incs=[("dve", 1)])
                copy_done[par] = cp["dve"]
            dv_all = sch.val("dve")
            sch.op("gpsimd", lambda e, tt=tt: e.dma_start(
                out=y_out[base + tt * 128: base + (tt + 1) * 128, :], in_=stage[:]),
                waits=[("dve", dv_all)], incs=[("st", 16)])
            sch.fence()




class L0FfnProgram:

    CONSTS = ("c_ident_f32", "c_ident_bf", "c_iota8", "c_ones_row")

    def __init__(self, nc, sch=None, pre=""):
        self.nc = nc
        self.pre = pre
        self.sch = sch if sch is not None else Sched(nc)
        self.sb = {}

    def sbuf(self, ctx, name, shape, dt, **kw):
        t = ctx.enter_context(self.nc.sbuf_tensor(self.pre + name, shape, dt, **kw))
        self.sb[name] = t
        return t

    def build(self):
        nc, sch = self.nc, self.sch
        ctx = contextlib.ExitStack()
        cdram = declare_const_dram(nc, self.CONSTS)
        x_in = nc.dram_tensor("x", [SEQ_PER_CORE * S, D], F32, kind="ExternalInput").ap()
        y_out = nc.dram_tensor("y", [SEQ_PER_CORE * S, D], F32, kind="ExternalOutput").ap()
        with ctx:
            block = ctx.enter_context(nc.Block())
            for s_ in ("ld_sync", "ld_gp", "ld_prep", "ldt_prep", "st_prep",
                        "pe", "act", "dve", "pool", "st"):
                sch.add_sem(s_, ctx.enter_context(nc.semaphore(s_)))
            self.prep_stg = [ctx.enter_context(nc.sbuf_tensor(
                self.pre + f"prep_stg{i}", [128, 512], BF16,
                is_transpose_dest=True)) for i in range(2)]
            with contextlib.ExitStack() as pctx:
                self.emit_body(pctx, x_in, y_out, cdram)
                st_total = sch.val("st")
                sch.op("gpsimd", lambda e: e.nop(), waits=[("st", st_total)])
                sch.emit(block)
        return nc

    def declare_w(self):
        if getattr(self, "_declared", False):
            return
        self._declared = True
        nc, pre = self.nc, self.pre
        d = {}
        d["gate_d"] = nc.dram_tensor(pre + "gate", [8, D], F32, kind="ExternalInput").ap()
        d["cb_d"] = nc.dram_tensor(pre + "corr_bias", [8], F32, kind="ExternalInput").ap()
        d["w1_d"] = nc.dram_tensor(pre + "w1", [8, 512, 512], F32, kind="ExternalInput").ap()
        d["w2_d"] = nc.dram_tensor(pre + "w2", [8, 512, 512], F32, kind="ExternalInput").ap()
        d["fc1_d"] = nc.dram_tensor(pre + "fc1_w", [512, D], F32, kind="ExternalInput").ap()
        d["fc2_d"] = nc.dram_tensor(pre + "fc2_w", [D, 512], F32, kind="ExternalInput").ap()
        d["up_d"] = nc.dram_tensor(pre + "up_w", [4096, D], F32, kind="ExternalInput").ap()
        d["down_d"] = nc.dram_tensor(pre + "down_w", [D, 4096], F32, kind="ExternalInput").ap()
        d["w1_bf"] = nc.dram_tensor(pre + "w1_bf", [8, 512, 512], BF16, kind="Internal").ap()
        d["w2_bf"] = nc.dram_tensor(pre + "w2_bf", [8, 512, 512], BF16, kind="Internal").ap()
        d["fc1_bf"] = nc.dram_tensor(pre + "fc1_bf", [512, D], BF16, kind="Internal").ap()
        d["fc2_bf"] = nc.dram_tensor(pre + "fc2_bf", [D, 512], BF16, kind="Internal").ap()
        d["up_bf"] = nc.dram_tensor(pre + "up_bf", [4096, D], BF16, kind="Internal").ap()
        d["down_bf"] = nc.dram_tensor(pre + "down_bf", [D, 4096], BF16, kind="Internal").ap()
        d["w1T_d"] = nc.dram_tensor(pre + "w1T", [8, 512, 512], BF16, kind="Internal").ap()
        d["w2T_d"] = nc.dram_tensor(pre + "w2T", [8, 512, 512], BF16, kind="Internal").ap()
        d["fc1T_d"] = nc.dram_tensor(pre + "fc1T", [D, 512], BF16, kind="Internal").ap()
        d["fc2T_d"] = nc.dram_tensor(pre + "fc2T", [512, D], BF16, kind="Internal").ap()
        d["upT_d"] = nc.dram_tensor(pre + "upT", [D, 4096], BF16, kind="Internal").ap()
        d["downT_d"] = nc.dram_tensor(pre + "downT", [4096, D], BF16, kind="Internal").ap()
        self.dw = d

    def emit_prep(self):
        self.declare_w()
        sch, d = self.sch, self.dw
        self.prep_emitted = True
        cast_ld = 0
        for (s_, t_) in (("w1_d", "w1_bf"), ("w2_d", "w2_bf"), ("fc1_d", "fc1_bf"),
                         ("fc2_d", "fc2_bf"), ("up_d", "up_bf"), ("down_d", "down_bf")):
            cl = sch.op("gpsimd", lambda e, s_=s_, t_=t_: e.dma_start(
                out=d[t_], in_=d[s_]), incs=[("ld_prep", 16)], nofence=True)
            cast_ld = cl["ld_prep"]
        state = {"i": 0, 0: sch.val("st_prep"), 1: sch.val("st_prep")}
        emit_prep_T(sch, self.prep_stg, d["fc1_bf"], d["fc1T_d"], 512, D, cast_ld, state)
        emit_prep_T(sch, self.prep_stg, d["fc2_bf"], d["fc2T_d"], D, 512, cast_ld, state)
        emit_prep_T(sch, self.prep_stg, d["up_bf"], d["upT_d"], 4096, D, cast_ld, state)
        emit_prep_T(sch, self.prep_stg, d["down_bf"], d["downT_d"], D, 4096, cast_ld, state)
        for e_ in range(8):
            emit_prep_T(sch, self.prep_stg, d["w1_bf"][e_], d["w1T_d"][e_], 512, 512,
                        cast_ld, state)
            emit_prep_T(sch, self.prep_stg, d["w2_bf"][e_], d["w2T_d"][e_], 512, 512,
                        cast_ld, state)
        flush_prep_T(sch, self.prep_stg, state)
        self.prep_st_total = sch.val("st_prep")

    def emit_body(self, ctx, x_in, y_out, cdram):
        nc, sch, sb = self.nc, self.sch, self.sb
        pre = self.pre
        self.declare_w()
        d = self.dw
        gate_d, cb_d = d["gate_d"], d["cb_d"]
        w1T_d, w2T_d = d["w1T_d"], d["w2T_d"]
        fc1T_d, fc2T_d, upT_d, downT_d = (d["fc1T_d"], d["fc2T_d"],
                                          d["upT_d"], d["downT_d"])
        self.dbg = nc.dram_tensor(pre + "dbg", [128, 4096], F32,
                                  kind="Internal").ap()

        if True:
            for name in self.CONSTS:
                shape, dt = CONST_SPECS[name]
                dram = cdram[name]
                t = self.sbuf(ctx, name + "_sb", list(shape), dt)
                self.sb[name] = t
                sch.op("sync", lambda e, t=t, dram=dram: e.dma_start(
                    out=t[:], in_=dram), incs=[("ld_sync", 16)])

            ones_col = self.sbuf(ctx, "ones_col", [128, 1], F32)
            onesr = self.sbuf(ctx, "onesr", [1, 128], F32)
            sch.op("vector", lambda e: e.memset(ones_col[:], 1.0), incs=[("dve", 1)])
            sch.op("vector", lambda e: e.memset(onesr[:], 1.0), incs=[("dve", 1)])

            x_res = [self.sbuf(ctx, f"x_res{c}", [128, S], F32) for c in range(8)]
            xn = [self.sbuf(ctx, f"xn{c}", [128, S], BF16) for c in range(8)]
            fc1T = self.sbuf(ctx, "fc1T_sb", [128, 8, 512], BF16)
            fc2T = self.sbuf(ctx, "fc2T_sb", [128, 4, 1024], BF16)
            w1e = [self.sbuf(ctx, f"w1e{p}", [128, 4, 512], BF16) for p in range(2)]
            w2e = [self.sbuf(ctx, f"w2e{p}", [128, 4, 512], BF16) for p in range(2)]
            dnT = [self.sbuf(ctx, f"dnT{p}", [128, 512], BF16) for p in range(2)]
            upw = [self.sbuf(ctx, f"upw{p}", [128, 8, 128], BF16) for p in range(2)]
            h_up = self.sbuf(ctx, "h_up", [128, 32, 512], BF16)
            lat = self.sbuf(ctx, "lat", [128, 4, 512], BF16)
            h1 = self.sbuf(ctx, "h1", [128, 4, 512], BF16)
            comb = self.sbuf(ctx, "comb", [128, 4, 512], F32)
            comb_fm = self.sbuf(ctx, "comb_fm", [128, 4, 512], BF16)
            cw = self.sbuf(ctx, "cw", [128, 4, 8], F32)
            gateT = self.sbuf(ctx, "gateT", [128, 8, 8], BF16)
            cbb = self.sbuf(ctx, "cbb", [128, 8], F32)
            rt = self.sbuf(ctx, "rt", [128, 64], F32)
            rti = self.sbuf(ctx, "rti", [128, 8], mybir.dt.uint32)
            gate_sb = self.sbuf(ctx, "gate_sb", [8, D], F32)
            x2 = self.sbuf(ctx, "ln_x2", [128, 1024], F32)
            rows = self.sbuf(ctx, "ln_rows", [1, 1536], F32)
            stage = x2

            ps = [ctx.enter_context(nc.psum_tensor(self.pre + f"psb{i}", [128, 512], F32))
                  for i in range(6)]
            ps_stats = ctx.enter_context(nc.psum_tensor(self.pre + "ps_stats", [33, 512], F32))
            ps_small = ctx.enter_context(nc.psum_tensor(self.pre + "ps_small", [128, 512], F32))
            self._stats = ps_stats
            self.ps_tp = [ps[4], ps[5]]

            if not getattr(self, "prep_emitted", False):
                self.emit_prep()
            sch.fence_include_val("st_prep", self.prep_st_total)
            sch.op("sync", lambda e: e.dma_start(out=gate_sb[:], in_=gate_d),
                   incs=[("ld_sync", 16)])
            sch.fence()
            sch.fence_include_val("st_prep", self.prep_st_total)

            for c in range(8):
                sch.op("sync", lambda e, c=c: e.dma_start(
                    out=fc1T[:, c, :],
                    in_=fc1T_d[c * 128:(c + 1) * 128, :]), incs=[("ld_sync", 16)])
            for lc in range(4):
                sch.op("sync", lambda e, lc=lc: e.dma_start(
                    out=fc2T[:, lc, :],
                    in_=fc2T_d[lc * 128:(lc + 1) * 128, :]), incs=[("ld_sync", 16)])
            ld = sch.val("ld_sync")
            for c in range(8):
                pe = sch.op("tensor", lambda e, c=c: e.transpose(
                    ps_small[0:128, 0:8], gate_sb[0:8, c * 128:(c + 1) * 128],
                    sb["c_ident_f32"][0:8, 0:8]),
                    waits=[("ld_sync", ld), ("dve", sch.val("dve"))],
                    incs=[("pe", 1)])
                sch.op("vector", lambda e, c=c: e.tensor_copy(
                    gateT[:, c, :], ps_small[0:128, 0:8]),
                    waits=[("pe", pe["pe"])], incs=[("dve", 1)])
                sch.fence()
            ldc = sch.op("sync", lambda e: e.dma_start(
                out=rows[0:1, 0:8], in_=cb_d.rearrange("(o d) -> o d", o=1)),
                incs=[("ld_sync", 16)])
            pe = sch.op("tensor", lambda e: e.matmul(
                ps_small[:, 0:8], onesr[0:1, :], rows[0:1, 0:8],
                start=True, stop=True),
                waits=[("ld_sync", ldc["ld_sync"])], incs=[("pe", 1)])
            sch.op("scalar", lambda e: e.copy(cbb[:], ps_small[:, 0:8]),
                   waits=[("pe", pe["pe"])], incs=[("act", 1)])
            sch.fence()

            for sq in range(SEQ_PER_CORE):
                self.seq(sq, x_in, y_out, w1T_d, w2T_d, upT_d, downT_d,
                         x_res, xn, fc1T, fc2T, w1e, w2e, dnT, upw,
                         h_up, lat, h1, comb, comb_fm, gateT, cbb, rt, rti,
                         x2, rows, stage, ps, ps_small)


    ln = AttnProgram.ln
    sbuf2 = None

    def seq(self, sq, x_in, y_out, w1T_d, w2T_d, upT_d, downT_d,
            x_res, xn, fc1T, fc2T, w1e, w2e, dnT, upw,
            h_up, lat, h1, comb, comb_fm, gateT, cbb, rt, rti,
            x2, rows, stage, ps, ps_small):
        nc, sch, sb = self.nc, self.sch, self.sb
        base = sq * S
        ident = sb["c_ident_f32"]
        iota8 = sb["c_iota8"]

        sch.fence()
        pe_after_tt = 0
        copy_done = {0: 0, 1: 0}
        for tt in range(16):
            ld = sch.op("sync", lambda e, tt=tt: e.dma_start(
                out=stage[:], in_=x_in[base + tt * 128: base + (tt + 1) * 128, :]),
                waits=[("pe", pe_after_tt)], incs=[("ld_sync", 16)])
            for c in range(8):
                par = c % 2
                pe = sch.op("tensor", lambda e, c=c, par=par: e.transpose(
                    self.ps_tp[par][:, 0:128], stage[:, c * 128:(c + 1) * 128],
                    ident[:]),
                    waits=[("ld_sync", ld["ld_sync"]), ("dve", copy_done[par])],
                    incs=[("pe", 1)])
                cp = sch.op("vector", lambda e, c=c, tt=tt, par=par: e.tensor_copy(
                    x_res[c][:, tt * 128:(tt + 1) * 128], self.ps_tp[par][:, 0:128]),
                    waits=[("pe", pe["pe"])], incs=[("dve", 1)])
                copy_done[par] = cp["dve"]
            pe_after_tt = sch.val("pe")
        sch.fence()

        for tch in range(4):
            self.ln(x_res, xn, tch * 512, 512, self._stats, ps[0:2])
        sch.fence()

        for tb in range(4):
            self.tile(sq, tb, y_out, w1T_d, w2T_d, upT_d, downT_d,
                      x_res, xn, fc1T, fc2T, w1e, w2e, dnT, upw,
                      h_up, lat, h1, comb, comb_fm, gateT, cbb, rt, rti,
                      ps, ps_small)

        copy_done = {0: 0, 1: 0}
        for tt in range(16):
            st_before = sch.val("st")
            for c in range(8):
                par = c % 2
                pe = sch.op("tensor", lambda e, c=c, tt=tt, par=par: e.transpose(
                    self.ps_tp[par][:, 0:128], x_res[c][:, tt * 128:(tt + 1) * 128],
                    ident[:]),
                    waits=[("st", st_before), ("dve", copy_done[par])],
                    incs=[("pe", 1)])
                cp = sch.op("vector", lambda e, c=c, par=par: e.tensor_copy(
                    stage[:, c * 128:(c + 1) * 128], self.ps_tp[par][:, 0:128]),
                    waits=[("pe", pe["pe"])], incs=[("dve", 1)])
                copy_done[par] = cp["dve"]
            dv_all = sch.val("dve")
            sch.op("gpsimd", lambda e, tt=tt: e.dma_start(
                out=y_out[base + tt * 128: base + (tt + 1) * 128, :], in_=stage[:]),
                waits=[("dve", dv_all)], incs=[("st", 16)])
            sch.fence()

    def tile(self, sq, tb, y_out, w1T_d, w2T_d, upT_d, downT_d,
             x_res, xn, fc1T, fc2T, w1e, w2e, dnT, upw,
             h_up, lat, h1, comb, comb_fm, gateT, cbb, rt, rti,
             ps, ps_small):
        sch, sb = self.sch, self.sb
        iota8 = sb["c_iota8"]
        x2 = sb["ln_x2"]
        cw = sb["cw"]
        t0 = tb * 512
        self._h1_evac = {}
        self._comb_rd = {}
        sch.fence()

        for st in range(4):
            for c in range(8):
                sch.op("tensor", lambda e, st=st, c=c: e.matmul(
                    ps_small[:, st * 8:(st + 1) * 8],
                    xn[c][:, t0 + st * 128: t0 + (st + 1) * 128],
                    gateT[:, c, :], start=(c == 0), stop=(c == 7)),
                    incs=[("pe", 1)] if c == 7 else ())
        pe_logits = sch.val("pe")
        combcol_done = []

        def vop(emit, extra=()):
            return sch.op("vector", emit,
                          waits=list(extra),
                          hard_waits=[("dve", sch.val("dve"))],
                          incs=[("dve", 1)])

        for st in range(4):
            sc_ = rt[:, 0:8]
            sfc = rt[:, 8:16]
            msk = rt[:, 16:24]
            t8v = rt[:, 24:32]
            oh1 = rt[:, 32:40]
            oh2 = rt[:, 40:48]
            tmp8 = rt[:, 48:56]
            s1 = rt[:, 56:57]
            s2 = rt[:, 57:58]
            m0 = rt[:, 58:59]
            m1 = rt[:, 59:60]
            w1n = rt[:, 60:61]
            w2n = rt[:, 61:62]
            g0 = rt[:, 62:63]
            g1 = rt[:, 63:64]
            ac = sch.op("scalar", lambda e, st=st: e.activation(
                sc_, ps_small[:, st * 8:(st + 1) * 8], AF.Sigmoid),
                waits=[("pe", pe_logits), ("dve", sch.val("dve"))],
                incs=[("act", 1)])
            vop(lambda e: e.tensor_add(sfc, sc_, cbb[:]),
                extra=[("act", ac["act"])])
            for g_, base_ in ((g0, 8), (g1, 12)):
                a_ = rt[:, base_:base_ + 1]
                b_ = rt[:, base_ + 1:base_ + 2]
                c_ = rt[:, base_ + 2:base_ + 3]
                d_ = rt[:, base_ + 3:base_ + 4]
                mx1 = rt[:, 48:49]
                mn1 = rt[:, 49:50]
                mx2 = rt[:, 50:51]
                mn2 = rt[:, 51:52]
                t1_ = rt[:, 52:53]
                t2_ = rt[:, 53:54]
                vop(lambda e, a_=a_, b_=b_, mx1=mx1: e.tensor_max(mx1, a_, b_))
                vop(lambda e, a_=a_, b_=b_, mn1=mn1: e.tensor_tensor(mn1, a_, b_, ALU.min))
                vop(lambda e, c_=c_, d_=d_, mx2=mx2: e.tensor_max(mx2, c_, d_))
                vop(lambda e, c_=c_, d_=d_, mn2=mn2: e.tensor_tensor(mn2, c_, d_, ALU.min))
                vop(lambda e, mn1=mn1, mx2=mx2, t1_=t1_: e.tensor_max(t1_, mn1, mx2))
                vop(lambda e, mn2=mn2, mx1=mx1, t2_=t2_: e.tensor_max(t2_, mn2, mx1))
                vop(lambda e, mx1=mx1, t1_=t1_: e.tensor_add(t1_, mx1, t1_))
                vop(lambda e, mx2=mx2, t2_=t2_: e.tensor_add(t2_, mx2, t2_))
                vop(lambda e, g_=g_, t1_=t1_, t2_=t2_: e.tensor_max(g_, t1_, t2_))
            vop(lambda e: e.tensor_tensor(m0, g0, g1, ALU.is_ge))
            vop(lambda e: e.tensor_scalar(m1, m0, -1.0, 1.0, ALU.mult, ALU.add))
            vop(lambda e: e.tensor_scalar_mul(rt[:, 16:20], rt[:, 8:12], m0))
            vop(lambda e: e.tensor_scalar_mul(rt[:, 20:24], rt[:, 12:16], m1))
            vop(lambda e: e.max(t8v, msk))
            vop(lambda e: e.max_index(rti[:], t8v, msk))
            vop(lambda e: e.tensor_copy(rt[:, 54:56], rti[:, 0:2]))
            i1 = rt[:, 54:55]
            i2 = rt[:, 55:56]
            vop(lambda e: e.tensor_single_scalar(oh1, iota8[:], i1, ALU.is_equal))
            vop(lambda e: e.tensor_single_scalar(oh2, iota8[:], i2, ALU.is_equal))
            vop(lambda e: e.tensor_mul(tmp8, oh1, sc_))
            vop(lambda e: e.reduce_sum(s1, tmp8, axis=mybir.AxisListType.X))
            vop(lambda e: e.tensor_mul(tmp8, oh2, sc_))
            vop(lambda e: e.reduce_sum(s2, tmp8, axis=mybir.AxisListType.X))
            vop(lambda e: e.tensor_add(w1n, s1, s2))
            vop(lambda e: e.tensor_scalar_add(w1n, w1n, 1e-20))
            vop(lambda e: e.reciprocal(w2n, w1n))
            vop(lambda e: e.tensor_mul(w1n, s1, w2n))
            vop(lambda e: e.tensor_mul(w2n, s2, w2n))
            vop(lambda e: e.tensor_scalar_mul(oh1, oh1, w1n))
            vop(lambda e: e.scalar_tensor_tensor(t8v, oh2, w2n, oh1, ALU.mult, ALU.add))
            vop(lambda e, st=st: e.tensor_copy(cw[:, st, :], t8v))
            combcol_done.append(sch.val("dve"))
            if sq == 0 and tb == 0 and st == 0:
                sch.op("gpsimd", lambda e: e.dma_start(
                    out=self.dbg[:, 2560:2624], in_=rt[:]),
                    waits=[("dve", sch.val("dve"))], incs=[("st", 16)])
                vop(lambda e: e.tensor_copy(rt[:, 48:56], rti[:]),
                    extra=[("st", sch.val("st"))])
                sch.op("gpsimd", lambda e: e.dma_start(
                    out=self.dbg[:, 2624:2632], in_=rt[:, 48:56]),
                    waits=[("dve", sch.val("dve"))], incs=[("st", 16)])
                sch.fence()
        sch.fence()

        lat_evac = {0: 0, 1: 0}
        dve_pre = sch.val("dve")
        for lc in range(4):
            for c in range(8):
                pe = sch.op("tensor", lambda e, lc=lc, c=c: e.matmul(
                    ps[lc % 2][:], fc1T[:, c, lc * 128:(lc + 1) * 128],
                    xn[c][:, t0:t0 + 512], start=(c == 0), stop=(c == 7)),
                    waits=[("dve", dve_pre), ("act", lat_evac[lc % 2])]
                    if c == 0 else (),
                    incs=[("pe", 1)] if c == 7 else ())
            a = sch.op("scalar", lambda e, lc=lc: e.copy(lat[:, lc, :], ps[lc % 2][:]),
                       waits=[("pe", pe["pe"])], incs=[("act", 1)])
            lat_evac[lc % 2] = a["act"]

        lat_done = sch.val("act")
        we_ld = {}
        for e_ in range(8):
            par = e_ % 2
            wld = sch.op("gpsimd", lambda e, e_=e_, par=par: e.dma_start(
                out=w1e[par][:], in_=w1T_d[e_].rearrange("(lc p) i -> p lc i", p=128)),
                waits=[("pe", we_ld.get(par, 0))], incs=[("ld_gp", 16)])
            wld2 = sch.op("gpsimd", lambda e, e_=e_, par=par: e.dma_start(
                out=w2e[par][:], in_=w2T_d[e_].rearrange("(ic p) l -> p ic l", p=128)),
                incs=[("ld_gp", 16)])
            for ic in range(4):
                par23 = 2 + ic % 2
                w0 = ([("ld_gp", wld2["ld_gp"]),
                       ("act", max(lat_done, self._h1_evac.get(par23, 0))),
                       ("dve", self._comb_rd.get(par23, 0))] if ic < 2 else
                      [("act", max(lat_done, self._h1_evac.get(par23, 0))),
                       ("dve", self._comb_rd.get(par23, 0))])
                for lc in range(4):
                    pe = sch.op("tensor", lambda e, ic=ic, lc=lc, par=par,
                                par23=par23: e.matmul(
                        ps[par23][:],
                        w1e[par][:, lc, ic * 128:(ic + 1) * 128],
                        lat[:, lc, :], start=(lc == 0), stop=(lc == 3)),
                        waits=(w0 if lc == 0 else ()),
                        incs=[("pe", 1)] if lc == 3 else ())
                a1 = sch.op("scalar", lambda e, ic=ic, par23=par23: e.activation(
                    x2[:, 0:512], ps[par23][:], AF.Relu),
                    waits=[("pe", pe["pe"])], incs=[("act", 1)])
                a2 = sch.op("scalar", lambda e, ic=ic: e.square(h1[:, ic, :], x2[:, 0:512]),
                            incs=[("act", 1)])
                self._h1_evac[par23] = a2["act"]
            act_h1 = sch.val("act")
            for st in range(4):
                par23 = 2 + st % 2
                for ic in range(4):
                    pe = sch.op("tensor", lambda e, st=st, ic=ic, par=par,
                                par23=par23: e.matmul(
                        ps[par23][:],
                        h1[:, ic, st * 128:(st + 1) * 128],
                        w2e[par][:, ic, :], start=(ic == 0), stop=(ic == 3)),
                        waits=([("act", act_h1),
                                ("dve", self._comb_rd.get(par23, 0)),
                                ("act", self._h1_evac.get(par23, 0))]
                               if ic == 0 else ()),
                        incs=[("pe", 1)] if ic == 3 else ())
                if e_ == 0:
                    cc = sch.op("vector", lambda e, st=st, e_=e_, par23=par23:
                                e.tensor_scalar_mul(
                        comb[:, st, :], ps[par23][:], cw[:, st, e_:e_ + 1]),
                        waits=[("pe", pe["pe"])],
                        hard_waits=[("dve", combcol_done[st])], incs=[("dve", 1)])
                else:
                    cc = sch.op("vector", lambda e, st=st, e_=e_, par23=par23:
                                e.scalar_tensor_tensor(
                        comb[:, st, :], ps[par23][:], cw[:, st, e_:e_ + 1],
                        comb[:, st, :], ALU.mult, ALU.add),
                        waits=[("pe", pe["pe"])], incs=[("dve", 1)])
                self._comb_rd[par23] = cc["dve"]
            we_ld[par] = sch.val("pe")
        sch.fence()

        cp_done = {0: 0, 1: 0}
        for st in range(4):
            for lc in range(4):
                par = lc % 2
                pe = sch.op("tensor", lambda e, st=st, lc=lc, par=par: e.transpose(
                    self.ps_tp[par][:, 0:128],
                    comb[:, st, lc * 128:(lc + 1) * 128], sb["c_ident_f32"][:]),
                    waits=[("dve", max(sch.val("dve") if st == 0 and lc == 0 else 0,
                                       cp_done[par]))],
                    incs=[("pe", 1)])
                cpd = sch.op("vector", lambda e, st=st, lc=lc, par=par: e.tensor_copy(
                    comb_fm[:, lc, st * 128:(st + 1) * 128],
                    self.ps_tp[par][:, 0:128]),
                    waits=[("pe", pe["pe"])], incs=[("dve", 1)])
                cp_done[par] = cpd["dve"]
        sch.fence()

        prev_up = {}
        up_evac = {0: 0, 1: 0}
        for fc in range(32):
            par = fc % 2
            srcw = upT_d.rearrange("(c p) f -> p c f", p=128)[
                :, :, fc * 128:(fc + 1) * 128]
            ldu = sch.op("gpsimd", lambda e, srcw=srcw, par=par: e.dma_start(
                out=upw[par][:], in_=srcw),
                waits=[("pe", prev_up[par])] if par in prev_up else (),
                incs=[("ld_gp", 16)])
            for c in range(8):
                pe = sch.op("tensor", lambda e, fc=fc, c=c, par=par: e.matmul(
                    ps[par][:], upw[par][:, c, :],
                    xn[c][:, t0:t0 + 512], start=(c == 0), stop=(c == 7)),
                    waits=([("ld_gp", ldu["ld_gp"]),
                            ("act", up_evac[par])] if c == 0 else ()),
                    incs=[("pe", 1)] if c == 7 else ())
            a1 = sch.op("scalar", lambda e, fc=fc, par=par: e.activation(
                x2[:, 0:512], ps[par][:], AF.Relu),
                waits=[("pe", pe["pe"])], incs=[("act", 1)])
            a2 = sch.op("scalar", lambda e, fc=fc: e.square(h_up[:, fc, :], x2[:, 0:512]),
                        incs=[("act", 1)])
            up_evac[par] = a2["act"]
            prev_up[par] = sch.val("pe")
        sch.fence()

        prev_dn = {}
        for dhalf in range(2):
            for fc in range(32):
                par = fc % 2
                ldd = sch.op("gpsimd", lambda e, fc=fc, par=par, dhalf=dhalf:
                             e.dma_start(
                    out=dnT[par][:],
                    in_=downT_d[fc * 128:(fc + 1) * 128,
                                dhalf * 512:(dhalf + 1) * 512]),
                    waits=[("pe", prev_dn[par])] if par in prev_dn else (),
                    incs=[("ld_gp", 16)])
                for dq in range(4):
                    sch.op("tensor", lambda e, fc=fc, dq=dq, par=par: e.matmul(
                        ps[dq][:], dnT[par][:, dq * 128:(dq + 1) * 128],
                        h_up[:, fc, :], start=(fc == 0), stop=False),
                        waits=([("ld_gp", ldd["ld_gp"]),
                                ("dve", sch.val("dve"))] if dq == 0 and fc == 0
                               else [("ld_gp", ldd["ld_gp"])] if dq == 0 else ()),
                        incs=[("pe", 1)] if dq == 3 else ())
                prev_dn[par] = sch.val("pe")
            for lc in range(4):
                for dq in range(4):
                    pe = sch.op("tensor", lambda e, lc=lc, dq=dq, dhalf=dhalf:
                                e.matmul(
                        ps[dq][:],
                        fc2T[:, lc, :][:, dhalf * 512 + dq * 128:
                                       dhalf * 512 + (dq + 1) * 128],
                        comb_fm[:, lc, :],
                        start=False, stop=(lc == 3)),
                        incs=[("pe", 1)] if lc == 3 and dq == 3 else ())
            pe_all = sch.val("pe")
            for dq in range(4):
                dch = dhalf * 4 + dq
                sch.op("vector", lambda e, dq=dq, dch=dch: e.tensor_add(
                    x_res[dch][:, t0:t0 + 512], ps[dq][:],
                    x_res[dch][:, t0:t0 + 512]),
                    waits=[("pe", pe_all)], incs=[("dve", 1)])
            sch.fence()
        if sq == 0 and tb == 0:
            sch.op("vector", lambda e: e.tensor_copy(x2[:, 0:512], h_up[:, 0, :]),
                   incs=[("dve", 1)])
            sch.op("gpsimd", lambda e: e.dma_start(
                out=self.dbg[:, 0:512], in_=x2[:, 0:512]),
                waits=[("dve", sch.val("dve"))], incs=[("st", 16)])
            sch.op("vector", lambda e: e.tensor_copy(x2[:, 0:512], lat[:, 0, :]),
                   waits=[("st", sch.val("st"))], incs=[("dve", 1)])
            sch.op("gpsimd", lambda e: e.dma_start(
                out=self.dbg[:, 512:1024], in_=x2[:, 0:512]),
                waits=[("dve", sch.val("dve"))], incs=[("st", 16)])
            sch.op("gpsimd", lambda e: e.dma_start(
                out=self.dbg[:, 1024:1536], in_=comb[:, 0, :]),
                incs=[("st", 16)])
            sch.op("gpsimd", lambda e: e.dma_start(
                out=self.dbg[:, 1536:1544], in_=cw[:, 0, :]),
                incs=[("st", 16)])
            sch.op("vector", lambda e: e.tensor_copy(x2[:, 0:512], xn[0][:, 0:512]),
                   waits=[("st", sch.val("st"))], incs=[("dve", 1)])
            sch.op("gpsimd", lambda e: e.dma_start(
                out=self.dbg[:, 2048:2560], in_=x2[:, 0:512]),
                waits=[("dve", sch.val("dve"))], incs=[("st", 16)])
            sch.fence()




class L1FfnProgram:

    CONSTS = ("c_ident_f32", "c_ident_bf", "c_iota8", "c_ones_row")

    def __init__(self, nc, sch=None, pre=""):
        self.nc = nc
        self.pre = pre
        self.sch = sch if sch is not None else Sched(nc)
        self.sb = {}

    def sbuf(self, ctx, name, shape, dt, **kw):
        t = ctx.enter_context(self.nc.sbuf_tensor(self.pre + name, shape, dt, **kw))
        self.sb[name] = t
        return t

    def build(self):
        nc, sch = self.nc, self.sch
        ctx = contextlib.ExitStack()
        cdram = declare_const_dram(nc, self.CONSTS)
        x_in = nc.dram_tensor("x", [SEQ_PER_CORE * S, D], F32, kind="ExternalInput").ap()
        y_out = nc.dram_tensor("y", [SEQ_PER_CORE * S, D], F32, kind="ExternalOutput").ap()
        with ctx:
            block = ctx.enter_context(nc.Block())
            for s_ in ("ld_sync", "ld_gp", "ld_prep", "ldt_prep", "st_prep",
                        "pe", "act", "dve", "pool", "st"):
                sch.add_sem(s_, ctx.enter_context(nc.semaphore(s_)))
            self.prep_stg = [ctx.enter_context(nc.sbuf_tensor(
                self.pre + f"prep_stg{i}", [128, 512], BF16,
                is_transpose_dest=True)) for i in range(2)]
            with contextlib.ExitStack() as pctx:
                self.emit_body(pctx, x_in, y_out, cdram)
                st_total = sch.val("st")
                sch.op("gpsimd", lambda e: e.nop(), waits=[("st", st_total)])
                sch.emit(block)
        return nc

    def declare_w(self):
        if getattr(self, "_declared", False):
            return
        self._declared = True
        nc, pre = self.nc, self.pre
        self.nw_d = nc.dram_tensor(pre + "norm_w", [D], F32, kind="ExternalInput").ap()
        self.wma_d = nc.dram_tensor(pre + "w_macro", [4, D], F32, kind="ExternalInput").ap()
        self.wmi_d = nc.dram_tensor(pre + "w_micro", [4, 2, D], F32, kind="ExternalInput").ap()
        self.w1_d = nc.dram_tensor(pre + "w1", [8, 2048, D], F32, kind="ExternalInput").ap()
        self.w2_d = nc.dram_tensor(pre + "w2", [8, D, 2048], F32, kind="ExternalInput").ap()
        self.w1_bf = nc.dram_tensor(pre + "w1_bf", [8, 2048, D], BF16, kind="Internal").ap()
        self.w2_bf = nc.dram_tensor(pre + "w2_bf", [8, D, 2048], BF16, kind="Internal").ap()
        self.w1T_d = nc.dram_tensor(pre + "w1T", [8, D, 2048], BF16, kind="Internal").ap()
        self.w2T_d = nc.dram_tensor(pre + "w2T", [8, 2048, D], BF16, kind="Internal").ap()

    def emit_prep_casts(self):
        self.declare_w()
        sch = self.sch
        self.casts_emitted = True
        cast_ld = 0
        for (s_, t_) in ((self.w1_d, self.w1_bf), (self.w2_d, self.w2_bf)):
            for e_ in range(8):
                cl = sch.op("gpsimd", lambda e, s_=s_, t_=t_, e_=e_: e.dma_start(
                    out=t_[e_], in_=s_[e_]), incs=[("ld_prep", 16)], nofence=True)
                cast_ld = cl["ld_prep"]
        self.cast_ld = cast_ld

    def emit_prep_T(self):
        if not getattr(self, "casts_emitted", False):
            self.emit_prep_casts()
        sch = self.sch
        self.prep_emitted = True
        state = {"i": 0, 0: sch.val("st_prep"), 1: sch.val("st_prep")}
        for e_ in range(8):
            emit_prep_T(sch, self.prep_stg, self.w1_bf[e_], self.w1T_d[e_], 2048, D,
                        self.cast_ld, state)
            emit_prep_T(sch, self.prep_stg, self.w2_bf[e_], self.w2T_d[e_], D, 2048,
                        self.cast_ld, state)
        flush_prep_T(sch, self.prep_stg, state)
        self.prep_st_total = sch.val("st_prep")

    def emit_body(self, ctx, x_in, y_out, cdram):
        nc, sch, sb = self.nc, self.sch, self.sb
        pre = self.pre
        self.declare_w()
        nw_d, wma_d, wmi_d = self.nw_d, self.wma_d, self.wmi_d
        w1T_d, w2T_d = self.w1T_d, self.w2T_d

        if True:
            for name in self.CONSTS:
                shape, dt = CONST_SPECS[name]
                dram = cdram[name]
                t = self.sbuf(ctx, name + "_sb", list(shape), dt)
                self.sb[name] = t
                sch.op("sync", lambda e, t=t, dram=dram: e.dma_start(
                    out=t[:], in_=dram), incs=[("ld_sync", 16)])

            ones_col = self.sbuf(ctx, "ones_col", [128, 1], F32)
            onesr = self.sbuf(ctx, "onesr", [1, 128], F32)
            sch.op("vector", lambda e: e.memset(ones_col[:], 1.0), incs=[("dve", 1)])
            sch.op("vector", lambda e: e.memset(onesr[:], 1.0), incs=[("dve", 1)])

            z = self.sbuf(ctx, "z", [128, 16, 1024], F32)
            zn_fm = self.sbuf(ctx, "zn_fm", [128, 8, S], BF16)
            w1e = self.sbuf(ctx, "w1e", [128, 8, 2048], BF16)
            w2e = self.sbuf(ctx, "w2e", [128, 16, 1024], BF16)
            h_fm = self.sbuf(ctx, "h_fm", [128, 16, 512], BF16)
            cw8 = self.sbuf(ctx, "cw8", [128, 16, 8], F32)
            nwb = self.sbuf(ctx, "nwb", [128, 1024], F32)
            wmT = self.sbuf(ctx, "wmT", [128, 8, 12], BF16)
            wm_sb = self.sbuf(ctx, "wm_sb", [12, D], F32)
            rt = self.sbuf(ctx, "rt", [128, 64], F32)
            rti = self.sbuf(ctx, "rti", [128, 8], mybir.dt.uint32)
            xs = self.sbuf(ctx, "xs", [128, 1024], F32)
            x2s = self.sbuf(ctx, "x2s", [128, 1024], F32)
            self.sbuf(ctx, "zn_bf", [128, 1024], BF16)
            rows = self.sbuf(ctx, "ln_rows", [1, 512], F32)

            ps = [ctx.enter_context(nc.psum_tensor(self.pre + f"psb{i}", [128, 512], F32))
                  for i in range(4)]
            ps_h = [ctx.enter_context(nc.psum_tensor(self.pre + f"psh{i}", [128, 512], F32))
                    for i in range(2)]
            ps_bt = [ctx.enter_context(nc.psum_tensor(self.pre + f"psbt{i}", [128, 1024], BF16))
                     for i in range(2)]
            self.ps, self.ps_h, self.ps_bt = ps, ps_h, ps_bt

            if not getattr(self, "prep_emitted", False):
                self.emit_prep_T()
            sch.fence_include_val("st_prep", self.prep_st_total)
            sch.op("sync", lambda e: e.dma_start(
                out=wm_sb[0:4, :], in_=wma_d), incs=[("ld_sync", 16)])
            sch.op("sync", lambda e: e.dma_start(
                out=wm_sb[4:12, :], in_=wmi_d.rearrange("g e d -> (g e) d")),
                incs=[("ld_sync", 16)])
            sch.fence()

            for half in range(2):
                ldh = sch.op("sync", lambda e, half=half: e.dma_start(
                    out=rows[0:1, 0:512],
                    in_=nw_d[half * 512:(half + 1) * 512]
                    .rearrange("(o d) -> o d", o=1)),
                    waits=[("pe", sch.val("pe")), ("act", sch.val("act"))],
                    incs=[("ld_sync", 16)])
                pe = sch.op("tensor", lambda e: e.matmul(
                    ps[0][:], onesr[0:1, :], rows[0:1, 0:512], start=True, stop=True),
                    waits=[("ld_sync", ldh["ld_sync"])], incs=[("pe", 1)])
                sch.op("scalar", lambda e, half=half: e.copy(
                    nwb[:, half * 512:(half + 1) * 512], ps[0][:]),
                    waits=[("pe", pe["pe"])], incs=[("act", 1)])
                sch.fence()
            for c in range(8):
                pe = sch.op("tensor", lambda e, c=c: e.transpose(
                    ps[1][0:128, 0:12], wm_sb[0:12, c * 128:(c + 1) * 128],
                    sb["c_ident_f32"][0:12, 0:12]),
                    waits=[("dve", sch.val("dve"))], incs=[("pe", 1)])
                sch.op("vector", lambda e, c=c: e.tensor_copy(
                    wmT[:, c, :], ps[1][0:128, 0:12]),
                    waits=[("pe", pe["pe"])], incs=[("dve", 1)])
                sch.fence()

            for sq in range(SEQ_PER_CORE):
                self.seq(sq, x_in, y_out, w1T_d, w2T_d)


    def vop(self, emit, extra=()):
        sch = self.sch
        return sch.op("vector", emit, waits=list(extra),
                      hard_waits=[("dve", sch.val("dve"))], incs=[("dve", 1)])

    def seq(self, sq, x_in, y_out, w1T_d, w2T_d):
        nc, sch, sb = self.nc, self.sch, self.sb
        z, zn_fm = sb["z"], sb["zn_fm"]
        xs, x2s, rows = sb["xs"], sb["x2s"], sb["ln_rows"]
        nwb, cw8 = sb["nwb"], sb["cw8"]
        ps, ps_h, ps_bt = self.ps, self.ps_h, self.ps_bt
        base = sq * S

        sch.fence()
        for tt in range(16):
            ld = sch.op("sync", lambda e, tt=tt: e.dma_start(
                out=xs[:], in_=x_in[base + tt * 128: base + (tt + 1) * 128, :]),
                waits=[("act", sch.val("act")), ("dve", sch.val("dve"))],
                incs=[("ld_sync", 16)])
            ac = sch.op("scalar", lambda e: e.square(x2s[:], xs[:]),
                        waits=[("ld_sync", ld["ld_sync"])], incs=[("act", 1)])
            v = self.vop(lambda e: e.reduce_sum(rt_col(sb, 0), xs[:],
                                                axis=mybir.AxisListType.X),
                         extra=[("ld_sync", ld["ld_sync"])])
            v = self.vop(lambda e: e.reduce_sum(rt_col(sb, 1), x2s[:],
                                                axis=mybir.AxisListType.X),
                         extra=[("act", ac["act"])])
            self.vop(lambda e: e.tensor_scalar_mul(rt_col(sb, 0), rt_col(sb, 0), 1.0 / D))
            self.vop(lambda e: e.tensor_scalar_mul(rt_col(sb, 1), rt_col(sb, 1), 1.0 / D))
            self.vop(lambda e: e.scalar_tensor_tensor(
                rt_col(sb, 2), rt_col(sb, 0), -1.0, rt_col(sb, 0), ALU.mult, ALU.mult))
            self.vop(lambda e: e.tensor_add(rt_col(sb, 1), rt_col(sb, 1), rt_col(sb, 2)))
            v = self.vop(lambda e: e.tensor_scalar_add(rt_col(sb, 1), rt_col(sb, 1), 1e-5))
            ac = sch.op("scalar", lambda e: e.activation(
                rt_col(sb, 1), rt_col(sb, 1), AF.Sqrt),
                waits=[("dve", v["dve"])], incs=[("act", 1)])
            self.vop(lambda e: e.reciprocal(rt_col(sb, 1), rt_col(sb, 1)),
                     extra=[("act", ac["act"])])
            v = self.vop(lambda e, tt=tt: e.tensor_scalar(
                z[:, tt, :], xs[:], rt_col(sb, 0), rt_col(sb, 1),
                ALU.subtract, ALU.mult))
        sch.fence()

        for lp in range(2):
            self.loop_iter(sq, lp, w1T_d, w2T_d)

        sch.fence()
        for tt in range(16):
            ld = sch.op("sync", lambda e, tt=tt: e.dma_start(
                out=xs[:], in_=x_in[base + tt * 128: base + (tt + 1) * 128, :]),
                waits=[("st", sch.val("st")), ("dve", sch.val("dve"))],
                incs=[("ld_sync", 16)])
            v = sch.op("vector", lambda e, tt=tt: e.tensor_add(
                xs[:], xs[:], z[:, tt, :]),
                waits=[("ld_sync", ld["ld_sync"])], incs=[("dve", 1)])
            sch.op("gpsimd", lambda e, tt=tt: e.dma_start(
                out=y_out[base + tt * 128: base + (tt + 1) * 128, :], in_=xs[:]),
                waits=[("dve", v["dve"])], incs=[("st", 16)])
            sch.fence()

    def loop_iter(self, sq, lp, w1T_d, w2T_d):
        nc, sch, sb = self.nc, self.sch, self.sb
        z, zn_fm = sb["z"], sb["zn_fm"]
        xs, x2s = sb["xs"], sb["x2s"]
        nwb, cw8, wmT = sb["nwb"], sb["cw8"], sb["wmT"]
        rt, rti, iota8 = sb["rt"], sb["rti"], sb["c_iota8"]
        ps, ps_h, ps_bt = self.ps, self.ps_h, self.ps_bt
        vop = self.vop
        sch.fence()

        tp_done = {0: 0, 1: 0}
        for tt in range(16):
            ac = sch.op("scalar", lambda e, tt=tt: e.square(x2s[:], z[:, tt, :]),
                        waits=[("dve", sch.val("dve"))], incs=[("act", 1)])
            v = self.vop(lambda e: e.reduce_sum(rt_col(sb, 0), x2s[:],
                                                axis=mybir.AxisListType.X),
                         extra=[("act", ac["act"])])
            self.vop(lambda e: e.tensor_scalar_mul(rt_col(sb, 0), rt_col(sb, 0), 1.0 / D))
            v = self.vop(lambda e: e.tensor_scalar_add(rt_col(sb, 0), rt_col(sb, 0), 1e-6))
            ac = sch.op("scalar", lambda e: e.activation(
                rt_col(sb, 0), rt_col(sb, 0), AF.Sqrt),
                waits=[("dve", v["dve"])], incs=[("act", 1)])
            self.vop(lambda e: e.reciprocal(rt_col(sb, 0), rt_col(sb, 0)),
                     extra=[("act", ac["act"])])
            self.vop(lambda e, tt=tt: e.tensor_scalar_mul(
                xs[:], z[:, tt, :], rt_col(sb, 0)))
            v = sch.op("vector", lambda e: e.tensor_mul(x2s[:], xs[:], nwb[:]),
                       incs=[("dve", 1)])
            ac = sch.op("scalar", lambda e: e.copy(sb["zn_bf"][:], x2s[:]),
                        waits=[("dve", v["dve"])], incs=[("act", 1)])
            for c in range(8):
                par = c % 2
                pe = sch.op("tensor", lambda e, c=c, par=par: e.transpose(
                    ps_bt[par][:, 0:128], sb["zn_bf"][:, c * 128:(c + 1) * 128],
                    sb["c_ident_bf"][:]),
                    waits=[("act", ac["act"]), ("dve", tp_done[par])],
                    incs=[("pe", 1)])
                cp = sch.op("vector", lambda e, c=c, tt=tt, par=par: e.tensor_copy(
                    zn_fm[:, c, tt * 128:(tt + 1) * 128], ps_bt[par][:, 0:128]),
                    waits=[("pe", pe["pe"])], incs=[("dve", 1)])
                tp_done[par] = cp["dve"]
        sch.fence()

        self._route_copy = 0
        zn_done = sch.val("dve")
        for tt in range(16):
            for c in range(8):
                pe = sch.op("tensor", lambda e, tt=tt, c=c: e.matmul(
                    ps[3][:, 0:12], zn_fm[:, c, tt * 128:(tt + 1) * 128],
                    wmT[:, c, :], start=(c == 0), stop=(c == 7)),
                    waits=[("dve", zn_done), ("act", self._route_copy)]
                    if c == 0 else (),
                    incs=[("pe", 1)] if c == 7 else ())
            self.route_dance(tt, pe["pe"])
        sch.fence()

        silu_evac = {0: 0, 1: 0}
        z_upd = {0: 0, 1: 0}
        h_last_pe = 0
        y_last_pe = 0
        for e_ in range(8):
            ld1 = sch.op("gpsimd", lambda e, e_=e_: e.dma_start(
                out=sb["w1e"][:], in_=w1T_d[e_].rearrange("(c p) f -> p c f", p=128)),
                waits=[("pe", h_last_pe)], incs=[("ld_gp", 16)])
            ld2 = sch.op("gpsimd", lambda e, e_=e_: e.dma_start(
                out=sb["w2e"][:], in_=w2T_d[e_].rearrange("(c p) d -> p c d", p=128)),
                waits=[("pe", y_last_pe)], incs=[("ld_gp", 16)])
            for tb in range(4):
                t0 = tb * 512
                for fc in range(16):
                    par = fc % 2
                    for c in range(8):
                        pe = sch.op("tensor", lambda e, fc=fc, c=c, par=par, t0=t0,
                                    e_=e_: e.matmul(
                            ps_h[par][:], sb["w1e"][:, c, fc * 128:(fc + 1) * 128],
                            zn_fm[:, c, t0:t0 + 512], start=(c == 0), stop=(c == 7)),
                            waits=([("ld_gp", ld1["ld_gp"]),
                                    ("act", silu_evac[par])] if c == 0 else ()),
                            incs=[("pe", 1)] if c == 7 else ())
                    a = sch.op("scalar", lambda e, fc=fc, par=par: e.activation(
                        sb["h_fm"][:, fc, :], ps_h[par][:], AF.Silu),
                        waits=[("pe", pe["pe"])], incs=[("act", 1)])
                    silu_evac[par] = a["act"]
                act_h = sch.val("act")
                if tb == 3:
                    h_last_pe = sch.val("pe")
                for st in range(4):
                    tt = tb * 4 + st
                    b0 = (st % 2) * 2
                    for fc in range(16):
                        pe = sch.op("tensor", lambda e, fc=fc, st=st, e_=e_, b0=b0:
                                    e.matmul(
                            ps[b0][:], sb["h_fm"][:, fc, st * 128:(st + 1) * 128],
                            sb["w2e"][:, fc, 0:512], start=(fc == 0), stop=(fc == 15)),
                            waits=([("act", act_h), ("ld_gp", ld2["ld_gp"]),
                                    ("dve", z_upd[st % 2])]
                                   if fc == 0 else ()),
                            incs=())
                        pe = sch.op("tensor", lambda e, fc=fc, st=st, e_=e_, b0=b0:
                                    e.matmul(
                            ps[b0 + 1][:], sb["h_fm"][:, fc, st * 128:(st + 1) * 128],
                            sb["w2e"][:, fc, 512:1024], start=(fc == 0), stop=(fc == 15)),
                            incs=[("pe", 1)] if fc == 15 else ())
                    for dh in range(2):
                        zu = sch.op("vector", lambda e, tt=tt, dh=dh, e_=e_, b0=b0:
                                    e.scalar_tensor_tensor(
                            z[:, tt, dh * 512:(dh + 1) * 512], ps[b0 + dh][:],
                            cw8[:, tt, e_:e_ + 1], z[:, tt, dh * 512:(dh + 1) * 512],
                            ALU.mult, ALU.add),
                            waits=[("pe", pe["pe"])], incs=[("dve", 1)])
                    z_upd[st % 2] = zu["dve"]
            y_last_pe = sch.val("pe")
        sch.fence()

    def route_dance(self, tt, pe_logits):
        sch, sb = self.sch, self.sb
        rt, rti, iota8 = sb["rt"], sb["rti"], sb["c_iota8"]
        vop = self.vop
        lg = rt[:, 0:12]
        mac = rt[:, 12:16]
        pad8 = rt[:, 16:24]
        t8v = rt[:, 24:32]
        li = rt[:, 32:36]
        oh = rt[:, 40:48]
        mx = rt[:, 48:49]
        ssum = rt[:, 49:50]
        v1 = rt[:, 50:51]
        v2 = rt[:, 51:52]
        i1 = rt[:, 52:53]
        i2 = rt[:, 53:54]
        e1 = rt[:, 54:55]
        e2 = rt[:, 55:56]
        tmp = rt[:, 56:57]
        ac = sch.op("scalar", lambda e: e.copy(lg, self.ps[3][:, 0:12]),
                    waits=[("pe", pe_logits), ("dve", sch.val("dve"))],
                    incs=[("act", 1)])
        self._route_copy = ac["act"]
        vop(lambda e: e.tensor_max(mx, rt[:, 0:1], rt[:, 1:2]),
            extra=[("act", ac["act"])])
        vop(lambda e: e.tensor_max(tmp, rt[:, 2:3], rt[:, 3:4]))
        vop(lambda e: e.tensor_max(mx, mx, tmp))
        v = vop(lambda e: e.tensor_scalar_mul(mx, mx, -1.0))
        ac = sch.op("scalar", lambda e: e.activation(mac, lg[:, 0:4], AF.Exp,
                                                     bias=mx),
                    waits=[("dve", v["dve"])], incs=[("act", 1)])
        vop(lambda e: e.reduce_sum(ssum, mac, axis=mybir.AxisListType.X),
            extra=[("act", ac["act"])])
        vop(lambda e: e.reciprocal(ssum, ssum))
        vop(lambda e: e.tensor_scalar_mul(mac, mac, ssum))
        vop(lambda e: e.memset(rt[:, 20:24], -1.0))
        vop(lambda e: e.tensor_copy(rt[:, 16:20], mac))
        v = vop(lambda e: e.max(t8v, pad8))
        v = vop(lambda e: e.max_index(rti[:], t8v, pad8))
        vop(lambda e: e.tensor_copy(rt[:, 52:54], rti[:, 0:2]))
        vop(lambda e: e.tensor_copy(v1, t8v[:, 0:1]))
        vop(lambda e: e.tensor_copy(v2, t8v[:, 1:2]))
        vop(lambda e: e.tensor_add(tmp, v1, v2))
        vop(lambda e: e.tensor_scalar_add(tmp, tmp, 1e-9))
        vop(lambda e: e.reciprocal(tmp, tmp))
        vop(lambda e: e.tensor_mul(v1, v1, tmp))
        vop(lambda e: e.tensor_mul(v2, v2, tmp))
        for g in range(4):
            vop(lambda e, g=g: e.tensor_tensor(
                rt[:, 36 + g:37 + g], rt[:, 4 + 2 * g:5 + 2 * g],
                rt[:, 5 + 2 * g:6 + 2 * g], ALU.is_lt))
        vop(lambda e: e.tensor_copy(li, rt[:, 36:40]))
        for (i_, e_out) in ((i1, e1), (i2, e2)):
            vop(lambda e, i_=i_: e.tensor_single_scalar(
                rt[:, 60:64], iota8[:, 0:4], i_, ALU.is_equal))
            vop(lambda e: e.tensor_mul(rt[:, 60:64], rt[:, 60:64], li))
            vop(lambda e, e_out=e_out: e.reduce_sum(
                e_out, rt[:, 60:64], axis=mybir.AxisListType.X))
            vop(lambda e, i_=i_, e_out=e_out: e.scalar_tensor_tensor(
                e_out, i_, 2.0, e_out, ALU.mult, ALU.add))
        vop(lambda e: e.tensor_single_scalar(oh, iota8[:], e1, ALU.is_equal))
        vop(lambda e: e.tensor_scalar_mul(oh, oh, v1))
        vop(lambda e: e.tensor_single_scalar(rt[:, 16:24], iota8[:], e2, ALU.is_equal))
        v = vop(lambda e: e.scalar_tensor_tensor(
            rt[:, 16:24], rt[:, 16:24], v2, oh, ALU.mult, ALU.add))
        vop(lambda e, tt=tt: e.tensor_copy(sb["cw8"][:, tt, :], rt[:, 16:24]))


def rt_col(sb, i):
    return sb["rt"][:, i:i + 1]


_CONSTS = None


def _consts():
    global _CONSTS
    if _CONSTS is None:
        _CONSTS = host_constants()
    return _CONSTS


def run_attn(x, inputs, layer):
    shards = np_split_tokens(x)
    consts = _consts()
    nc = bass.Bass("TRN2", target_bir_lowering=False)
    prog = AttnProgram(nc, layer)
    prog.build()
    pre = "L0_attn_" if layer == 0 else "L1_attn_"
    wnames = ("Wq", "Wk", "Wv", "Wo") + (("bq", "bk", "bv", "bo") if layer == 0 else ())
    in_maps = []
    for i in range(NCORES):
        m = {"x": shards[i]}
        for wn in wnames:
            m[wn] = np.asarray(inputs[pre + wn], dtype=np.float32)
        for cn in prog.CONSTS:
            m[cn] = consts[cn]
        in_maps.append(m)
    res = run_bass_kernel_spmd(nc, in_maps, core_ids=list(range(NCORES)))
    return np.concatenate([r["y"].reshape(SEQ_PER_CORE, S, D) for r in res.results],
                          axis=0)


def run_l0_ffn(x, inputs):
    shards = np_split_tokens(x)
    consts = _consts()
    nc = bass.Bass("TRN2", target_bir_lowering=False)
    prog = L0FfnProgram(nc)
    prog.build()
    in_maps = []
    for i in range(NCORES):
        m = {"x": shards[i]}
        for wn in ("gate", "corr_bias", "w1", "w2", "fc1_w", "fc2_w", "up_w", "down_w"):
            m[wn] = np.asarray(inputs["L0_ffn_" + wn], dtype=np.float32)
        for cn in prog.CONSTS:
            m[cn] = consts[cn]
        in_maps.append(m)
    res = run_bass_kernel_spmd(nc, in_maps, core_ids=list(range(NCORES)))
    run_l0_ffn.dbg = res.results[0].get("dbg")
    return np.concatenate([r["y"].reshape(SEQ_PER_CORE, S, D) for r in res.results],
                          axis=0)


def run_l1_ffn(x, inputs):
    shards = np_split_tokens(x)
    consts = _consts()
    nc = bass.Bass("TRN2", target_bir_lowering=False)
    prog = L1FfnProgram(nc)
    prog.build()
    in_maps = []
    for i in range(NCORES):
        m = {"x": shards[i]}
        for wn in ("norm_w", "w_macro", "w_micro", "w1", "w2"):
            m[wn] = np.asarray(inputs["L1_ffn_" + wn], dtype=np.float32)
        for cn in prog.CONSTS:
            m[cn] = consts[cn]
        in_maps.append(m)
    res = run_bass_kernel_spmd(nc, in_maps, core_ids=list(range(NCORES)))
    return np.concatenate([r["y"].reshape(SEQ_PER_CORE, S, D) for r in res.results],
                          axis=0)


def build_mega(nc):
    sch = Sched(nc)
    all_consts = tuple(CONST_SPECS.keys())
    cdram = declare_const_dram(nc, all_consts)
    x_in = nc.dram_tensor("x", [SEQ_PER_CORE * S, D], F32, kind="ExternalInput").ap()
    y_out = nc.dram_tensor("y", [SEQ_PER_CORE * S, D], F32, kind="ExternalOutput").ap()
    xh = [nc.dram_tensor(f"xh{i}", [SEQ_PER_CORE * S, D], F32, kind="Internal").ap()
          for i in range(2)]
    progs = [AttnProgram(nc, 0, sch=sch, pre="L0_attn_"),
             L0FfnProgram(nc, sch=sch, pre="L0_ffn_"),
             AttnProgram(nc, 1, sch=sch, pre="L1_attn_"),
             L1FfnProgram(nc, sch=sch, pre="L1_ffn_")]
    ios = [(x_in, xh[0]), (xh[0], xh[1]), (xh[1], xh[0]), (xh[0], y_out)]
    ctx = contextlib.ExitStack()
    with ctx:
        block = ctx.enter_context(nc.Block())
        for s_ in ("ld_sync", "ld_gp", "ld_prep", "ldt_prep", "st_prep",
                        "pe", "act", "dve", "pool", "st"):
            sch.add_sem(s_, ctx.enter_context(nc.semaphore(s_)))
        prep_stg = [ctx.enter_context(nc.sbuf_tensor(
            f"prep_stg{i}", [128, 512], BF16, is_transpose_dest=True))
            for i in range(2)]
        for p in progs:
            p.prep_stg = prep_stg
        progs[1].emit_prep()
        progs[2].emit_prep()
        for k, (prog, (i_, o_)) in enumerate(zip(progs, ios)):
            with contextlib.ExitStack() as pctx:
                if k == 1:
                    progs[3].emit_prep_casts()
                if k == 2:
                    progs[3].emit_prep_T()
                prog.emit_body(pctx, i_, o_, cdram)
                sch.fence()
                sch.emit(block)
        st_total = sch.val("st")
        sch.op("gpsimd", lambda e: e.nop(), waits=[("st", st_total)])
        sch.emit(block)
    return nc


MEGA_WNAMES = (
    "L0_attn_Wq", "L0_attn_Wk", "L0_attn_Wv", "L0_attn_Wo",
    "L0_attn_bq", "L0_attn_bk", "L0_attn_bv", "L0_attn_bo",
    "L0_ffn_gate", "L0_ffn_corr_bias", "L0_ffn_w1", "L0_ffn_w2",
    "L0_ffn_fc1_w", "L0_ffn_fc2_w", "L0_ffn_up_w", "L0_ffn_down_w",
    "L1_attn_Wq", "L1_attn_Wk", "L1_attn_Wv", "L1_attn_Wo",
    "L1_ffn_norm_w", "L1_ffn_w_macro", "L1_ffn_w_micro",
    "L1_ffn_w1", "L1_ffn_w2",
)


def kernel(**inputs):
    x = np.asarray(inputs["L0_attn_x"], dtype=np.float32)
    shards = np_split_tokens(x)
    consts = _consts()
    nc = bass.Bass("TRN2", target_bir_lowering=False)
    build_mega(nc)
    in_maps = []
    for i in range(NCORES):
        m = {"x": shards[i]}
        for wn in MEGA_WNAMES:
            m[wn] = np.asarray(inputs[wn], dtype=np.float32)
        m.update(consts)
        in_maps.append(m)
    res = run_bass_kernel_spmd(nc, in_maps, core_ids=list(range(NCORES)))
    return np.concatenate(
        [r["y"].reshape(SEQ_PER_CORE, S, D) for r in res.results],
        axis=0).astype(np.float32)


def kernel_unfused(**inputs):
    x = np.asarray(inputs["L0_attn_x"], dtype=np.float32)
    x = run_attn(x, inputs, 0)
    x = run_l0_ffn(x, inputs)
    x = run_attn(x, inputs, 1)
    x = run_l1_ffn(x, inputs)
    return x.astype(np.float32)


# version 10
# speedup vs baseline: 1.0057x; 1.0057x over previous
import contextlib
import numpy as np
import ml_dtypes
import concourse.bass as bass
import concourse.mybir as mybir
from concourse.bass_utils import run_bass_kernel_spmd

F32 = mybir.dt.float32
BF16 = mybir.dt.bfloat16
AF = mybir.ActivationFunctionType
ALU = mybir.AluOpType

D = 1024
S = 2048
SEQ_PER_CORE = 4
NCORES = 8
NH = 16
DH = 64
NEG = -30000.0


class Sched:

    ENGINES = ("sync", "tensor", "scalar", "vector", "gpsimd")

    def __init__(self, nc):
        self.nc = nc
        self.items = []
        self.counts = {}
        self.sems = {}
        self.fence_waits = {}
        self.emitted_upto = 0
        self.e_waited = {e: {} for e in self.ENGINES}
        self.e_selfcnt = {e: {} for e in self.ENGINES}

    def add_sem(self, name, handle):
        self.sems[name] = handle
        self.counts[name] = 0

    def op(self, engine, emit, waits=(), incs=(), hard_waits=(), nofence=False):
        after = {}
        w = {} if nofence else dict(self.fence_waits)
        for semname, val in waits:
            w[semname] = max(w.get(semname, 0), val)
        hw = {}
        for semname, val in hard_waits:
            hw[semname] = max(hw.get(semname, 0), val)
            w.pop(semname, None) if False else None
        inc_list = []
        for semname, amt in incs:
            self.counts[semname] += amt
            after[semname] = self.counts[semname]
            inc_list.append((semname, amt))
        self.items.append((engine, emit, sorted(w.items()), inc_list,
                           sorted(hw.items())))
        return after

    def val(self, semname):
        return self.counts[semname]

    FENCE_EXEMPT = ("ld_prep", "ldt_prep", "st_prep")

    def fence(self):
        self.fence_waits = {k: v for k, v in self.counts.items()
                            if k not in self.FENCE_EXEMPT}
        for s_, v in getattr(self, "sticky_waits", {}).items():
            self.fence_waits[s_] = max(self.fence_waits.get(s_, 0), v)

    def fence_include_val(self, sem, val):
        if not hasattr(self, "sticky_waits"):
            self.sticky_waits = {}
        self.sticky_waits[sem] = max(self.sticky_waits.get(sem, 0), val)
        self.fence_waits[sem] = max(self.fence_waits.get(sem, 0), val)

    def emit(self, block):
        new_items = self.items[self.emitted_upto:]
        self.emitted_upto = len(self.items)
        per_engine = {e: [] for e in self.ENGINES}
        for it in new_items:
            per_engine[it[0]].append(it)
        sems = self.sems

        def mk_body(items, waited, selfcnt):
            def body(eng):
                for (_e, emit, waits, incs, hard) in items:
                    for semname, val in waits:
                        if val <= waited.get(semname, 0) or val <= selfcnt.get(semname, 0):
                            continue
                        eng.wait_ge(sems[semname], val)
                        waited[semname] = val
                    for semname, val in hard:
                        if val <= waited.get(semname, 0):
                            continue
                        eng.wait_ge(sems[semname], val)
                        waited[semname] = val
                    instr = emit(eng)
                    for semname, amt in incs:
                        if instr is None:
                            eng.sem_inc(sems[semname], amt)
                        else:
                            instr = instr.then_inc(sems[semname], amt)
                        selfcnt[semname] = selfcnt.get(semname, 0) + amt
            return body

        for ename in self.ENGINES:
            if per_engine[ename]:
                getattr(block, ename)(mk_body(
                    per_engine[ename], self.e_waited[ename], self.e_selfcnt[ename]))


def host_constants():
    c = {}
    c["c_ident_f32"] = np.eye(128, dtype=np.float32)
    c["c_ident_bf"] = np.eye(128).astype(ml_dtypes.bfloat16)
    c["c_ones_row"] = np.ones((1, 512), dtype=np.float32)
    mask = np.zeros((4, 128, 512), dtype=np.float32)
    for r in range(4):
        for i in range(128):
            mask[r, i, : min(r * 128 + i, 512)] = NEG
    c["c_maskT"] = mask.astype(ml_dtypes.bfloat16)
    c["c_iota8"] = np.tile(np.arange(8, dtype=np.float32), (128, 1))
    inv_freq = 1.0 / (10000.0 ** (np.arange(0, DH, 2, dtype=np.float32) / DH))
    t = np.arange(S, dtype=np.float32)
    freqs = np.outer(t, inv_freq)
    emb = np.concatenate([freqs, freqs], axis=1)
    cos = np.cos(emb).T.astype(np.float32)
    sin = np.sin(emb).T.astype(np.float32)
    c["c_rope_cos"] = np.concatenate([cos, cos], axis=0).astype(ml_dtypes.bfloat16)
    c["c_rope_sin"] = np.concatenate([sin, sin], axis=0).astype(ml_dtypes.bfloat16)
    R = np.zeros((128, 128), dtype=np.float32)
    for h0 in (0, 64):
        for d_ in range(32):
            R[h0 + d_, h0 + 32 + d_] = -1.0
            R[h0 + 32 + d_, h0 + d_] = 1.0
    c["c_R_rot_T"] = R.T.copy().astype(ml_dtypes.bfloat16)
    return c


CONST_SPECS = {
    "c_ident_f32": ([128, 128], F32),
    "c_ident_bf": ([128, 128], BF16),
    "c_ones_row": ([1, 512], F32),
    "c_maskT": ([4, 128, 512], BF16),
    "c_iota8": ([128, 8], F32),
    "c_rope_cos": ([128, S], BF16),
    "c_rope_sin": ([128, S], BF16),
    "c_R_rot_T": ([128, 128], BF16),
}


def emit_prep_T(sch, stg2, src_bf, dst_T, nrow, ncol, cast_ld, state):
    pend = state.get("pend")
    for cc in range(ncol // 128):
        for rr in range(0, nrow, 512):
            rw = min(512, nrow - rr)
            par = state["i"] % 2
            ld = sch.op("sync", lambda e, cc=cc, rr=rr, rw=rw, par=par: e.dma_start(
                out=stg2[par][:, 0:rw],
                in_=src_bf[rr:rr + rw, cc * 128:(cc + 1) * 128],
                transpose=True),
                waits=[("st_prep", state.get(par, 0)), ("ld_prep", cast_ld)],
                incs=[("ldt_prep", 16)], nofence=True)
            if pend is not None:
                (pcc, prr, prw, ppar, pld, pdst) = pend
                st = sch.op("gpsimd", lambda e, pcc=pcc, prr=prr, prw=prw, ppar=ppar,
                            pdst=pdst: e.dma_start(
                    out=pdst[pcc * 128:(pcc + 1) * 128, prr:prr + prw],
                    in_=stg2[ppar][:, 0:prw]),
                    waits=[("ldt_prep", pld)], incs=[("st_prep", 16)], nofence=True)
                state[ppar] = st["st_prep"]
            pend = (cc, rr, rw, par, ld["ldt_prep"], dst_T)
            state["i"] += 1
    state["pend"] = pend


def flush_prep_T(sch, stg2, state):
    pend = state.pop("pend", None)
    if pend is not None:
        (pcc, prr, prw, ppar, pld, pdst) = pend
        st = sch.op("gpsimd", lambda e: e.dma_start(
            out=pdst[pcc * 128:(pcc + 1) * 128, prr:prr + prw],
            in_=stg2[ppar][:, 0:prw]),
            waits=[("ldt_prep", pld)], incs=[("st_prep", 16)], nofence=True)
        state[ppar] = st["st_prep"]


def declare_const_dram(nc, names):
    return {name: nc.dram_tensor(name, CONST_SPECS[name][0], CONST_SPECS[name][1],
                                 kind="ExternalInput").ap()
            for name in names}


def np_split_tokens(x):
    return [np.ascontiguousarray(
        x[i * SEQ_PER_CORE:(i + 1) * SEQ_PER_CORE].reshape(-1, D))
        for i in range(NCORES)]


class AttnProgram:

    def __init__(self, nc, layer, sch=None, pre=""):
        self.nc = nc
        self.layer = layer
        self.pre = pre
        self.CONSTS = ("c_ident_f32", "c_maskT") if layer == 0 else (
            "c_ident_f32", "c_maskT", "c_rope_cos", "c_rope_sin", "c_R_rot_T")
        self.sch = sch if sch is not None else Sched(nc)
        self.sb = {}

    def sbuf(self, ctx, name, shape, dt, **kw):
        t = ctx.enter_context(self.nc.sbuf_tensor(self.pre + name, shape, dt, **kw))
        self.sb[name] = t
        return t

    def build(self):
        nc, sch = self.nc, self.sch
        ctx = contextlib.ExitStack()
        cdram = declare_const_dram(nc, self.CONSTS)
        x_in = nc.dram_tensor("x", [SEQ_PER_CORE * S, D], F32, kind="ExternalInput").ap()
        y_out = nc.dram_tensor("y", [SEQ_PER_CORE * S, D], F32, kind="ExternalOutput").ap()
        with ctx:
            block = ctx.enter_context(nc.Block())
            for s_ in ("ld_sync", "ld_gp", "ld_prep", "ldt_prep", "st_prep",
                        "pe", "act", "dve", "pool", "st"):
                sch.add_sem(s_, ctx.enter_context(nc.semaphore(s_)))
            self.prep_stg = [ctx.enter_context(nc.sbuf_tensor(
                self.pre + f"prep_stg{i}", [128, 512], BF16,
                is_transpose_dest=True)) for i in range(2)]
            with contextlib.ExitStack() as pctx:
                self.emit_body(pctx, x_in, y_out, cdram)
                st_total = sch.val("st")
                sch.op("gpsimd", lambda e: e.nop(), waits=[("st", st_total)])
                sch.emit(block)
        return nc

    def declare_w(self):
        if getattr(self, "_declared", False):
            return
        self._declared = True
        nc, pre = self.nc, self.pre
        self.w_dram = {wn: nc.dram_tensor(pre + wn, [D, D], F32,
                                          kind="ExternalInput").ap()
                       for wn in ("Wq", "Wk", "Wv", "Wo")}
        if self.layer == 0:
            self.b_dram = {bn: nc.dram_tensor(pre + bn, [D], F32,
                                              kind="ExternalInput").ap()
                           for bn in ("bq", "bk", "bv", "bo")}
            self.w_src = self.w_dram
        else:
            self.wT_dram = {wn: nc.dram_tensor(pre + "T_" + wn, [D, D], BF16,
                                               kind="Internal").ap()
                            for wn in ("Wq", "Wk", "Wv", "Wo")}
            self.w_bf = {wn: nc.dram_tensor(pre + "bf_" + wn, [D, D], BF16,
                                            kind="Internal").ap()
                         for wn in ("Wq", "Wk", "Wv", "Wo")}
            self.w_src = self.wT_dram

    def emit_prep(self):
        self.declare_w()
        sch = self.sch
        self.prep_emitted = True
        cast_ld = 0
        for wn in ("Wq", "Wk", "Wv", "Wo"):
            cl = sch.op("gpsimd", lambda e, wn=wn: e.dma_start(
                out=self.w_bf[wn], in_=self.w_dram[wn]), incs=[("ld_prep", 16)],
                nofence=True)
            cast_ld = cl["ld_prep"]
        state = {"i": 0, 0: sch.val("st_prep"), 1: sch.val("st_prep")}
        for wn in ("Wq", "Wk", "Wv", "Wo"):
            emit_prep_T(sch, self.prep_stg, self.w_bf[wn], self.wT_dram[wn],
                        D, D, cast_ld, state)
        flush_prep_T(sch, self.prep_stg, state)
        self.prep_st_total = sch.val("st_prep")

    def emit_body(self, ctx, x_in, y_out, cdram):
        nc, sch, sb = self.nc, self.sch, self.sb
        pre = self.pre
        self.declare_w()
        w_dram = self.w_dram
        if self.layer == 0:
            b_dram = self.b_dram
        if True:
            for name in self.CONSTS:
                shape, dt = CONST_SPECS[name]
                dram = cdram[name]
                if name == "c_maskT":
                    t = self.sbuf(ctx, name + "_sb", [128, 4 * 512], dt)
                    self.sb[name] = t
                    for r in range(4):
                        sch.op("sync", lambda e, t=t, r=r, dram=dram: e.dma_start(
                            out=t[:, r * 512:(r + 1) * 512], in_=dram[r]),
                            incs=[("ld_sync", 16)])
                else:
                    t = self.sbuf(ctx, name + "_sb", list(shape), dt)
                    self.sb[name] = t
                    sch.op("sync", lambda e, t=t, dram=dram: e.dma_start(
                        out=t[:], in_=dram), incs=[("ld_sync", 16)])

            ones_col = self.sbuf(ctx, "ones_col", [128, 1], F32)
            onesr = self.sbuf(ctx, "onesr", [1, 128], F32)
            sch.op("vector", lambda e: e.memset(ones_col[:], 1.0), incs=[("dve", 1)])
            sch.op("vector", lambda e: e.memset(onesr[:], 1.0), incs=[("dve", 1)])

            x_res = [self.sbuf(ctx, f"x_res{c}", [128, S], F32) for c in range(8)]
            xn = [self.sbuf(ctx, f"xn{c}", [128, S], BF16) for c in range(8)]
            K_fm = [self.sbuf(ctx, f"K{c}", [128, S], BF16) for c in range(8)]
            V_tm = self.sbuf(ctx, "V", [128, 16 * NH * 65], BF16)
            Qb = self.sbuf(ctx, "Qb", [128, 8 * 512], BF16, is_transpose_dest=True)
            ctxf = [self.sbuf(ctx, f"ctx{c}", [128, 512], BF16) for c in range(8)]
            npt = 4 if self.layer == 0 else 2
            PT = [self.sbuf(ctx, f"PT{p}", [128, 512], BF16) for p in range(npt)]
            PT = [PT[p % npt] for p in range(4)]
            x2 = self.sbuf(ctx, "ln_x2", [128, 1024], F32)
            if self.layer == 0:
                rb_t = self.sbuf(ctx, "rb", [64, 512], F32)
                rb = rb_t[:, :]
            else:
                rb = x2[0:64, 512:1024]
            rows = self.sbuf(ctx, "ln_rows", [1, 1536], F32)
            w_och = [self.sbuf(ctx, f"w_och{p}", [128, 8, 128], BF16) for p in range(2)]
            if self.layer == 0:
                bias_col = {bn: self.sbuf(ctx, f"{bn}_col", [128, 8], F32)
                            for bn in ("bq", "bk", "bo")}
                bvb = self.sbuf(ctx, "bvb", [128, D], BF16)
            else:
                bias_col = None
                bvb = None
            brow = rows
            stage = x2

            ps_stats = ctx.enter_context(nc.psum_tensor(self.pre + "ps_stats", [33, 512], F32))
            ps_mm = [ctx.enter_context(nc.psum_tensor(self.pre + f"ps_mm{i}", [128, 512], F32))
                     for i in range(2)]
            ps_S = [ctx.enter_context(nc.psum_tensor(self.pre + f"ps_S{i}", [128, 512], F32))
                    for i in range(2)]
            ps_ctx = ctx.enter_context(nc.psum_tensor(self.pre + "ps_ctx", [128, 512], F32))
            ps_t = ctx.enter_context(nc.psum_tensor(self.pre + "ps_t", [128, 512], F32))
            ps_t2 = ctx.enter_context(nc.psum_tensor(self.pre + "ps_t2", [128, 512], F32))
            self.ps_tp = [ps_t, ps_t2]
            self.ps_mm, self.ps_S, self.ps_ctx, self.ps_t = ps_mm, ps_S, ps_ctx, ps_t
            self._stats = ps_stats

            if self.layer == 0:
                for bn in ("bq", "bk", "bo"):
                    for c in range(8):
                        sch.op("sync", lambda e, bn=bn, c=c: e.dma_start(
                            out=bias_col[bn][:, c:c + 1],
                            in_=b_dram[bn][c * 128:(c + 1) * 128]
                            .rearrange("(p o) -> p o", o=1)),
                            incs=[("ld_sync", 16)])
                sch.fence()
                sch.op("vector", lambda e: e.tensor_scalar_mul(
                    bias_col["bq"][:], bias_col["bq"][:], 0.125), incs=[("dve", 1)])
                for half in range(2):
                    ldh = sch.op("sync", lambda e, half=half: e.dma_start(
                        out=brow[0:1, 0:512],
                        in_=b_dram["bv"][half * 512:(half + 1) * 512]
                        .rearrange("(o d) -> o d", o=1)),
                        incs=[("ld_sync", 16)])
                    pe = sch.op("tensor", lambda e: e.matmul(
                        ps_t[:, 0:512], onesr[0:1, :], brow[0:1, 0:512],
                        start=True, stop=True),
                        waits=[("ld_sync", ldh["ld_sync"])], incs=[("pe", 1)])
                    sch.op("scalar", lambda e, half=half: e.copy(
                        bvb[:, half * 512:(half + 1) * 512], ps_t[:, 0:512]),
                        waits=[("pe", pe["pe"])], incs=[("act", 1)])
                    sch.fence()
            else:
                if not getattr(self, "prep_emitted", False):
                    self.emit_prep()
                sch.fence()
                sch.fence_include_val("st_prep", self.prep_st_total)

            for sq in range(SEQ_PER_CORE):
                self.seq(sq, x_in, y_out,
                         x_res, xn, K_fm, V_tm, Qb, ctxf, PT, rows, rb, x2,
                         w_och, bias_col, bvb, stage)


    def ln(self, x_res, xn, t0, W, ps_stats, ps_abc):
        sch, sb = self.sch, self.sb
        ones_col, onesr = sb["ones_col"], sb["onesr"]
        x2, rows = sb["ln_x2"], sb["ln_rows"]
        for c in range(8):
            sch.op("tensor", lambda e, c=c: e.matmul(
                ps_stats[0:1, 0:W], ones_col[:], x_res[c][:, t0:t0 + W],
                start=(c == 0), stop=(c == 7)),
                waits=[("dve", sch.val("dve"))] if c == 0 else (),
                incs=[("pe", 1)] if c == 7 else ())
        pe_entry = sch.val("pe")
        dve_entry = sch.val("dve")
        pe_gate = {}
        for c in range(8):
            half = c % 2
            w = ([("pe", pe_gate[half])] if half in pe_gate
                 else [("pe", pe_entry), ("dve", dve_entry)])
            a = sch.op("scalar", lambda e, c=c, half=half: e.square(
                x2[:, half * 512:half * 512 + W], x_res[c][:, t0:t0 + W]),
                waits=w, incs=[("act", 1)])
            p = sch.op("tensor", lambda e, c=c, half=half: e.matmul(
                ps_stats[32:33, 0:W], ones_col[:], x2[:, half * 512:half * 512 + W],
                start=(c == 0), stop=(c == 7)),
                waits=[("act", a["act"])], incs=[("pe", 1)])
            pe_gate[half] = p["pe"]
        pe_done = sch.val("pe")
        mean_r = rows[0:1, 0:W]
        a_r = rows[0:1, 512:512 + W]
        b_r = rows[0:1, 1024:1024 + W]
        sch.op("vector", lambda e: e.tensor_scalar_mul(
            mean_r, ps_stats[0:1, 0:W], 1.0 / D),
            waits=[("pe", pe_done)], incs=[("dve", 1)])
        sch.op("vector", lambda e: e.tensor_scalar_mul(
            b_r, ps_stats[32:33, 0:W], 1.0 / D), incs=[("dve", 1)])
        sch.op("vector", lambda e: e.scalar_tensor_tensor(
            a_r, mean_r, -1.0, mean_r, ALU.mult, ALU.mult), incs=[("dve", 1)])
        sch.op("vector", lambda e: e.tensor_add(b_r, b_r, a_r), incs=[("dve", 1)])
        v = sch.op("vector", lambda e: e.tensor_scalar_add(b_r, b_r, 1e-5),
                   incs=[("dve", 1)])
        sv = sch.op("scalar", lambda e: e.activation(a_r, b_r, AF.Sqrt),
                    waits=[("dve", v["dve"])], incs=[("act", 1)])
        sch.op("vector", lambda e: e.reciprocal(a_r, a_r),
               waits=[("act", sv["act"])], incs=[("dve", 1)])
        v = sch.op("vector", lambda e: e.scalar_tensor_tensor(
            b_r, mean_r, -1.0, a_r, ALU.mult, ALU.mult), incs=[("dve", 1)])
        dve_rows = v["dve"]
        pa = sch.op("tensor", lambda e: e.matmul(
            ps_abc[0][:, 0:W], onesr[0:1, :], a_r, start=True, stop=True),
            waits=[("dve", dve_rows)], incs=[("pe", 1)])
        pb = sch.op("tensor", lambda e: e.matmul(
            ps_abc[1][:, 0:W], onesr[0:1, :], b_r, start=True, stop=True),
            incs=[("pe", 1)])
        pe_bc = pb["pe"]
        gate = {}
        for c in range(8):
            half = c % 2
            w = [("pe", pe_bc)] + ([("dve", gate[half])] if half in gate else [])
            sch.op("vector", lambda e, c=c, half=half: e.tensor_mul(
                x2[:, half * 512:half * 512 + W], x_res[c][:, t0:t0 + W],
                ps_abc[0][:, 0:W]), waits=w, incs=[("dve", 1)])
            g = sch.op("vector", lambda e, c=c, half=half: e.tensor_tensor(
                xn[c][:, t0:t0 + W], x2[:, half * 512:half * 512 + W],
                ps_abc[1][:, 0:W], ALU.add), incs=[("dve", 1)])
            gate[half] = g["dve"]

    def rope(self, buf, b0, t0, W, evac_act):
        sch, sb = self.sch, self.sb
        x2 = sb["ln_x2"]
        ps = self.ps_tp[0]
        gate = getattr(self, "_rope_gate", 0)
        pe = sch.op("tensor", lambda e: e.matmul(
            ps[:, 0:W], sb["c_R_rot_T"][:], buf[:, b0:b0 + W], start=True, stop=True),
            waits=[("act", evac_act), ("dve", gate)], incs=[("pe", 1)])
        sch.op("vector", lambda e: e.tensor_mul(
            x2[:, 0:W], buf[:, b0:b0 + W], sb["c_rope_cos"][:, t0:t0 + W]),
            waits=[("act", evac_act)], incs=[("dve", 1)])
        sch.op("vector", lambda e: e.tensor_mul(
            x2[:, 512:512 + W], ps[:, 0:W], sb["c_rope_sin"][:, t0:t0 + W]),
            waits=[("pe", pe["pe"])], incs=[("dve", 1)])
        v3 = sch.op("vector", lambda e: e.tensor_add(
            buf[:, b0:b0 + W], x2[:, 0:W], x2[:, 512:512 + W]),
            waits=[("pe", pe["pe"])], incs=[("dve", 1)])
        self._rope_gate = v3["dve"]

    def seq(self, sq, x_in, y_out,
            x_res, xn, K_fm, V_tm, Qb, ctxf, PT, rows, rb, x2,
            w_och, bias_col, bvb, stage):
        nc, sch, sb = self.nc, self.sch, self.sb
        ps_mm, ps_S, ps_ctx, ps_t = self.ps_mm, self.ps_S, self.ps_ctx, self.ps_t
        base = sq * S
        ident = sb["c_ident_f32"]
        rrow = rows
        lay = self.layer

        sch.fence()
        pe_after_tt = 0
        copy_done = {0: 0, 1: 0}
        for tt in range(16):
            ld = sch.op("sync", lambda e, tt=tt: e.dma_start(
                out=stage[:], in_=x_in[base + tt * 128: base + (tt + 1) * 128, :]),
                waits=[("pe", pe_after_tt)], incs=[("ld_sync", 16)])
            for c in range(8):
                par = c % 2
                pe = sch.op("tensor", lambda e, c=c, par=par: e.transpose(
                    self.ps_tp[par][:, 0:128], stage[:, c * 128:(c + 1) * 128],
                    ident[:]),
                    waits=[("ld_sync", ld["ld_sync"]), ("dve", copy_done[par])],
                    incs=[("pe", 1)])
                cp = sch.op("vector", lambda e, c=c, tt=tt, par=par: e.tensor_copy(
                    x_res[c][:, tt * 128:(tt + 1) * 128], self.ps_tp[par][:, 0:128]),
                    waits=[("pe", pe["pe"])], incs=[("dve", 1)])
                copy_done[par] = cp["dve"]
            pe_after_tt = sch.val("pe")
        sch.fence()

        for tch in range(4):
            self.ln(x_res, xn, tch * 512, 512, self._stats, ps_S)
        sch.fence()

        def load_w_cols(wname, och, buf, extra_waits=()):
            srcw = self.w_src[wname].rearrange("(c p) o -> p c o", p=128)[
                :, :, och * 128:(och + 1) * 128]
            return sch.op("gpsimd", lambda e, srcw=srcw, buf=buf: e.dma_start(
                out=buf[:, :, 0:128], in_=srcw),
                waits=list(extra_waits), incs=[("ld_gp", 16)])

        prev_w = {0: 0, 1: 0}
        evac_done = {0: 0, 1: 0}
        for och in range(8):
            par = och % 2
            ld = load_w_cols("Wk", och, w_och[par],
                             extra_waits=[("pe", prev_w[par])])
            for tch in range(4):
                mmpar = tch % 2
                for c in range(8):
                    pe = sch.op("tensor", lambda e, tch=tch, c=c, par=par,
                                mmpar=mmpar: e.matmul(
                        ps_mm[mmpar][:], w_och[par][:, c, 0:128],
                        xn[c][:, tch * 512:(tch + 1) * 512],
                        start=(c == 0), stop=(c == 7)),
                        waits=([("ld_gp", ld["ld_gp"]),
                                ("act", evac_done[mmpar])] if c == 0 else ()),
                        incs=[("pe", 1)] if c == 7 else ())
                if lay == 0:
                    sch.op("scalar", lambda e, och=och, tch=tch, mmpar=mmpar:
                           e.activation(
                        K_fm[och][:, tch * 512:(tch + 1) * 512], ps_mm[mmpar][:],
                        AF.Identity, bias=bias_col["bk"][:, och:och + 1], scale=1.0),
                        waits=[("pe", pe["pe"])], incs=[("act", 1)])
                else:
                    ac = sch.op("scalar", lambda e, och=och, tch=tch, mmpar=mmpar:
                                e.copy(K_fm[och][:, tch * 512:(tch + 1) * 512],
                                       ps_mm[mmpar][:]),
                                waits=[("pe", pe["pe"])], incs=[("act", 1)])
                    self.rope(K_fm[och], tch * 512, tch * 512, 512, ac["act"])
                evac_done[mmpar] = sch.val("act")
            prev_w[par] = sch.val("pe")
        sch.fence()

        vevac = {0: 0, 1: 0}
        for och in range(8):
            par = och % 2
            ld = load_w_cols("Wv", och, w_och[par],
                             extra_waits=[("pe", prev_w[par])])
            for tt in range(16):
                mmpar = tt % 2
                for c in range(8):
                    pe = sch.op("tensor", lambda e, tt=tt, c=c, par=par,
                                mmpar=mmpar: e.matmul(
                        ps_mm[mmpar][:, 0:128], xn[c][:, tt * 128:(tt + 1) * 128],
                        w_och[par][:, c, 0:128], start=(c == 0), stop=(c == 7)),
                        waits=([("ld_gp", ld["ld_gp"]),
                                ("dve", vevac[mmpar])] if c == 0 else ()),
                        incs=[("pe", 1)] if c == 7 else ())
                vap = V_tm[:].rearrange("p (t h x) -> p t h x", t=16, h=NH)[
                    :, tt, 2 * och:2 * och + 2, 0:64]
                psap = ps_mm[mmpar][:, 0:128].rearrange("p (h x) -> p h x", h=2)
                if lay == 0:
                    bvap = bvb[:, och * 128:(och + 1) * 128].rearrange(
                        "p (h x) -> p h x", h=2)
                    sch.op("vector", lambda e, vap=vap, psap=psap, bvap=bvap:
                           e.tensor_tensor(vap, psap, bvap, ALU.add),
                           waits=[("pe", pe["pe"])], incs=[("dve", 1)])
                else:
                    sch.op("vector", lambda e, vap=vap, psap=psap: e.tensor_copy(
                        vap, psap),
                        waits=[("pe", pe["pe"])], incs=[("dve", 1)])
                vevac[mmpar] = sch.val("dve")
            prev_w[par] = sch.val("pe")
        onesap = V_tm[:].rearrange("p (t h x) -> p t h x", t=16, h=NH)[:, :, :, 64:65]
        sch.op("vector", lambda e: e.memset(onesap, 1.0), incs=[("dve", 1)])
        sch.fence()

        psS4 = [ps_S[0], ps_S[1], ps_mm[0], ps_mm[1]]
        ps_S_reader = {0: 0, 1: 0, 2: 0, 3: 0}
        PT_reader = {0: 0, 1: 0, 2: 0, 3: 0}
        ctx_reader_pe = 0
        S_reader_pe = 0
        resid_done = {0: 0, 1: 0}
        q_evac = {0: 0, 1: 0}
        for qb in range(4):
            q0 = qb * 512
            for och in range(8):
                par = och % 2
                ld = load_w_cols("Wq", och, w_och[par],
                                 extra_waits=[("pe", prev_w[par])])
                for c in range(8):
                    pe = sch.op("tensor", lambda e, c=c, par=par, q0=q0: e.matmul(
                        ps_mm[par][:], w_och[par][:, c, 0:128],
                        xn[c][:, q0:q0 + 512], start=(c == 0), stop=(c == 7)),
                        waits=([("ld_gp", ld["ld_gp"]),
                                ("act", max(q_evac[par], ps_S_reader[2 + par])),
                                ("dve", resid_done[par])] if c == 0 else ()),
                        incs=[("pe", 1)] if c == 7 else ())
                if lay == 0:
                    qa = sch.op("scalar", lambda e, och=och, par=par: e.activation(
                        Qb[:, och * 512:(och + 1) * 512], ps_mm[par][:],
                        AF.Identity, bias=bias_col["bq"][:, och:och + 1], scale=0.125),
                        waits=[("pe", pe["pe"]), ("pe", S_reader_pe)],
                        incs=[("act", 1)])
                else:
                    qa = sch.op("scalar", lambda e, och=och, par=par: e.activation(
                        Qb[:, och * 512:(och + 1) * 512], ps_mm[par][:],
                        AF.Copy, scale=0.125),
                        waits=[("pe", pe["pe"]), ("pe", S_reader_pe)],
                        incs=[("act", 1)])
                    self.rope(Qb, och * 512, q0, 512, qa["act"])
                q_evac[par] = sch.val("act")
                prev_w[par] = sch.val("pe")
            q_ready = sch.val("act") if lay == 0 else 0
            q_ready_dve = sch.val("dve")
            ps_S_reader[2] = max(ps_S_reader[2], sch.val("act"))
            ps_S_reader[3] = max(ps_S_reader[3], sch.val("act"))

            nkc = (qb + 1) * 4
            for h in range(NH):
                chunk, hb = h // 2, 64 * (h % 2)
                for kc in range(nkc):
                    par = kc % 4
                    wq = ([("act", q_ready)] if lay == 0 else
                          [("dve", q_ready_dve)])
                    pe = sch.op("tensor", lambda e, kc=kc, chunk=chunk, hb=hb,
                                par=par: e.matmul(
                        psS4[par][:], K_fm[chunk][hb:hb + 64, kc * 128:(kc + 1) * 128],
                        Qb[hb:hb + 64, chunk * 512:(chunk + 1) * 512],
                        start=True, stop=True),
                        waits=[("act", ps_S_reader[par])] + (wq if kc == 0 else [])
                        + ([("dve", resid_done[par - 2])] if par >= 2 else []),
                        incs=[("pe", 1)])
                    waits = [("pe", pe["pe"])]
                    if kc >= nkc - 4:
                        r = kc - (nkc - 4)
                        dv = sch.op("vector", lambda e, par=par, r=r: e.tensor_add(
                            psS4[par][:], psS4[par][:],
                            sb["c_maskT"][:, r * 512:(r + 1) * 512]),
                            waits=waits, incs=[("dve", 1)])
                        waits = [("dve", dv["dve"])]
                    ptp = kc % (4 if self.layer == 0 else 2)
                    ac = sch.op("scalar", lambda e, par=par, ptp=ptp: e.activation(
                        PT[ptp][:], psS4[par][:], AF.Exp),
                        waits=waits + [("pe", PT_reader[ptp])], incs=[("act", 1)])
                    ps_S_reader[par] = ac["act"]
                    pv = sch.op("tensor", lambda e, kc=kc, h=h, ptp=ptp,
                                nkc=nkc: e.matmul(
                        ps_ctx[0:65, :],
                        V_tm[:, kc * NH * 65 + h * 65: kc * NH * 65 + (h + 1) * 65],
                        PT[ptp][:], start=(kc == 0), stop=(kc == nkc - 1)),
                        waits=[("act", ac["act"])] + (
                            [("dve", sch.val("dve")), ("pe", ctx_reader_pe)]
                            if kc == 0 else []),
                        incs=[("pe", 1)])
                    PT_reader[ptp] = pv["pe"]
                pv_done = sch.val("pe")
                dv = sch.op("vector", lambda e: e.reciprocal(
                    rrow[0:1, 0:512], ps_ctx[64:65, :]),
                    waits=[("pe", pv_done)], incs=[("dve", 1)])
                pe = sch.op("tensor", lambda e: e.matmul(
                    ps_t[0:64, 0:512], sb["onesr"][0:1, 0:64], rrow[0:1, 0:512],
                    start=True, stop=True),
                    waits=[("dve", dv["dve"])], incs=[("pe", 1)])
                ac = sch.op("scalar", lambda e: e.copy(rb, ps_t[0:64, 0:512]),
                            waits=[("pe", pe["pe"])], incs=[("act", 1)])
                sch.op("vector", lambda e, chunk=chunk, hb=hb: e.tensor_mul(
                    ctxf[chunk][hb:hb + 64, :], ps_ctx[0:64, :], rb),
                    waits=[("act", ac["act"]), ("pe", ctx_reader_pe)],
                    incs=[("dve", 1)])
            S_reader_pe = sch.val("pe")
            ctx_ready = sch.val("dve")

            for och in range(8):
                par = och % 2
                ld = load_w_cols("Wo", och, w_och[par],
                                 extra_waits=[("pe", prev_w[par])])
                for c in range(8):
                    pe = sch.op("tensor", lambda e, c=c, par=par: e.matmul(
                        ps_mm[par][:], w_och[par][:, c, 0:128],
                        ctxf[c][:, 0:512], start=(c == 0), stop=(c == 7)),
                        waits=([("ld_gp", ld["ld_gp"]), ("dve", ctx_ready),
                                ("dve", resid_done[par]),
                                ("act", max(q_evac[par], ps_S_reader[2 + par]))]
                               if c == 0 else ()),
                        incs=[("pe", 1)] if c == 7 else ())
                if lay == 0:
                    sch.op("vector", lambda e, och=och, par=par, q0=q0:
                           e.scalar_tensor_tensor(
                        x_res[och][:, q0:q0 + 512], ps_mm[par][:],
                        bias_col["bo"][:, och:och + 1], x_res[och][:, q0:q0 + 512],
                        ALU.add, ALU.add),
                        waits=[("pe", pe["pe"])], incs=[("dve", 1)])
                else:
                    sch.op("vector", lambda e, och=och, par=par, q0=q0: e.tensor_add(
                        x_res[och][:, q0:q0 + 512], ps_mm[par][:],
                        x_res[och][:, q0:q0 + 512]),
                        waits=[("pe", pe["pe"])], incs=[("dve", 1)])
                resid_done[par] = sch.val("dve")
                prev_w[par] = sch.val("pe")
            ctx_reader_pe = sch.val("pe")

        copy_done = {0: 0, 1: 0}
        for tt in range(16):
            st_before = sch.val("st")
            for c in range(8):
                par = c % 2
                pe = sch.op("tensor", lambda e, c=c, tt=tt, par=par: e.transpose(
                    self.ps_tp[par][:, 0:128], x_res[c][:, tt * 128:(tt + 1) * 128],
                    ident[:]),
                    waits=[("st", st_before), ("dve", copy_done[par])],
                    incs=[("pe", 1)])
                cp = sch.op("vector", lambda e, c=c, par=par: e.tensor_copy(
                    stage[:, c * 128:(c + 1) * 128], self.ps_tp[par][:, 0:128]),
                    waits=[("pe", pe["pe"])], incs=[("dve", 1)])
                copy_done[par] = cp["dve"]
            dv_all = sch.val("dve")
            sch.op("gpsimd", lambda e, tt=tt: e.dma_start(
                out=y_out[base + tt * 128: base + (tt + 1) * 128, :], in_=stage[:]),
                waits=[("dve", dv_all)], incs=[("st", 16)])
            sch.fence()




class L0FfnProgram:

    CONSTS = ("c_ident_f32", "c_ident_bf", "c_iota8", "c_ones_row")

    def __init__(self, nc, sch=None, pre=""):
        self.nc = nc
        self.pre = pre
        self.sch = sch if sch is not None else Sched(nc)
        self.sb = {}

    def sbuf(self, ctx, name, shape, dt, **kw):
        t = ctx.enter_context(self.nc.sbuf_tensor(self.pre + name, shape, dt, **kw))
        self.sb[name] = t
        return t

    def build(self):
        nc, sch = self.nc, self.sch
        ctx = contextlib.ExitStack()
        cdram = declare_const_dram(nc, self.CONSTS)
        x_in = nc.dram_tensor("x", [SEQ_PER_CORE * S, D], F32, kind="ExternalInput").ap()
        y_out = nc.dram_tensor("y", [SEQ_PER_CORE * S, D], F32, kind="ExternalOutput").ap()
        with ctx:
            block = ctx.enter_context(nc.Block())
            for s_ in ("ld_sync", "ld_gp", "ld_prep", "ldt_prep", "st_prep",
                        "pe", "act", "dve", "pool", "st"):
                sch.add_sem(s_, ctx.enter_context(nc.semaphore(s_)))
            self.prep_stg = [ctx.enter_context(nc.sbuf_tensor(
                self.pre + f"prep_stg{i}", [128, 512], BF16,
                is_transpose_dest=True)) for i in range(2)]
            with contextlib.ExitStack() as pctx:
                self.emit_body(pctx, x_in, y_out, cdram)
                st_total = sch.val("st")
                sch.op("gpsimd", lambda e: e.nop(), waits=[("st", st_total)])
                sch.emit(block)
        return nc

    def declare_w(self):
        if getattr(self, "_declared", False):
            return
        self._declared = True
        nc, pre = self.nc, self.pre
        d = {}
        d["gate_d"] = nc.dram_tensor(pre + "gate", [8, D], F32, kind="ExternalInput").ap()
        d["cb_d"] = nc.dram_tensor(pre + "corr_bias", [8], F32, kind="ExternalInput").ap()
        d["w1_d"] = nc.dram_tensor(pre + "w1", [8, 512, 512], F32, kind="ExternalInput").ap()
        d["w2_d"] = nc.dram_tensor(pre + "w2", [8, 512, 512], F32, kind="ExternalInput").ap()
        d["fc1_d"] = nc.dram_tensor(pre + "fc1_w", [512, D], F32, kind="ExternalInput").ap()
        d["fc2_d"] = nc.dram_tensor(pre + "fc2_w", [D, 512], F32, kind="ExternalInput").ap()
        d["up_d"] = nc.dram_tensor(pre + "up_w", [4096, D], F32, kind="ExternalInput").ap()
        d["down_d"] = nc.dram_tensor(pre + "down_w", [D, 4096], F32, kind="ExternalInput").ap()
        d["w1_bf"] = nc.dram_tensor(pre + "w1_bf", [8, 512, 512], BF16, kind="Internal").ap()
        d["w2_bf"] = nc.dram_tensor(pre + "w2_bf", [8, 512, 512], BF16, kind="Internal").ap()
        d["fc1_bf"] = nc.dram_tensor(pre + "fc1_bf", [512, D], BF16, kind="Internal").ap()
        d["fc2_bf"] = nc.dram_tensor(pre + "fc2_bf", [D, 512], BF16, kind="Internal").ap()
        d["up_bf"] = nc.dram_tensor(pre + "up_bf", [4096, D], BF16, kind="Internal").ap()
        d["down_bf"] = nc.dram_tensor(pre + "down_bf", [D, 4096], BF16, kind="Internal").ap()
        d["w1T_d"] = nc.dram_tensor(pre + "w1T", [8, 512, 512], BF16, kind="Internal").ap()
        d["w2T_d"] = nc.dram_tensor(pre + "w2T", [8, 512, 512], BF16, kind="Internal").ap()
        d["fc1T_d"] = nc.dram_tensor(pre + "fc1T", [D, 512], BF16, kind="Internal").ap()
        d["fc2T_d"] = nc.dram_tensor(pre + "fc2T", [512, D], BF16, kind="Internal").ap()
        d["upT_d"] = nc.dram_tensor(pre + "upT", [D, 4096], BF16, kind="Internal").ap()
        d["downT_d"] = nc.dram_tensor(pre + "downT", [4096, D], BF16, kind="Internal").ap()
        self.dw = d

    def emit_prep(self):
        self.declare_w()
        sch, d = self.sch, self.dw
        self.prep_emitted = True
        cast_ld = 0
        for (s_, t_) in (("w1_d", "w1_bf"), ("w2_d", "w2_bf"), ("fc1_d", "fc1_bf"),
                         ("fc2_d", "fc2_bf"), ("up_d", "up_bf"), ("down_d", "down_bf")):
            cl = sch.op("gpsimd", lambda e, s_=s_, t_=t_: e.dma_start(
                out=d[t_], in_=d[s_]), incs=[("ld_prep", 16)], nofence=True)
            cast_ld = cl["ld_prep"]
        state = {"i": 0, 0: sch.val("st_prep"), 1: sch.val("st_prep")}
        emit_prep_T(sch, self.prep_stg, d["fc1_bf"], d["fc1T_d"], 512, D, cast_ld, state)
        emit_prep_T(sch, self.prep_stg, d["fc2_bf"], d["fc2T_d"], D, 512, cast_ld, state)
        emit_prep_T(sch, self.prep_stg, d["up_bf"], d["upT_d"], 4096, D, cast_ld, state)
        emit_prep_T(sch, self.prep_stg, d["down_bf"], d["downT_d"], D, 4096, cast_ld, state)
        for e_ in range(8):
            emit_prep_T(sch, self.prep_stg, d["w1_bf"][e_], d["w1T_d"][e_], 512, 512,
                        cast_ld, state)
            emit_prep_T(sch, self.prep_stg, d["w2_bf"][e_], d["w2T_d"][e_], 512, 512,
                        cast_ld, state)
        flush_prep_T(sch, self.prep_stg, state)
        self.prep_st_total = sch.val("st_prep")

    def emit_body(self, ctx, x_in, y_out, cdram):
        nc, sch, sb = self.nc, self.sch, self.sb
        pre = self.pre
        self.declare_w()
        d = self.dw
        gate_d, cb_d = d["gate_d"], d["cb_d"]
        w1T_d, w2T_d = d["w1T_d"], d["w2T_d"]
        fc1T_d, fc2T_d, upT_d, downT_d = (d["fc1T_d"], d["fc2T_d"],
                                          d["upT_d"], d["downT_d"])
        self.dbg = nc.dram_tensor(pre + "dbg", [128, 4096], F32,
                                  kind="Internal").ap()

        if True:
            for name in self.CONSTS:
                shape, dt = CONST_SPECS[name]
                dram = cdram[name]
                t = self.sbuf(ctx, name + "_sb", list(shape), dt)
                self.sb[name] = t
                sch.op("sync", lambda e, t=t, dram=dram: e.dma_start(
                    out=t[:], in_=dram), incs=[("ld_sync", 16)])

            ones_col = self.sbuf(ctx, "ones_col", [128, 1], F32)
            onesr = self.sbuf(ctx, "onesr", [1, 128], F32)
            sch.op("vector", lambda e: e.memset(ones_col[:], 1.0), incs=[("dve", 1)])
            sch.op("vector", lambda e: e.memset(onesr[:], 1.0), incs=[("dve", 1)])

            x_res = [self.sbuf(ctx, f"x_res{c}", [128, S], F32) for c in range(8)]
            xn = [self.sbuf(ctx, f"xn{c}", [128, S], BF16) for c in range(8)]
            fc1T = self.sbuf(ctx, "fc1T_sb", [128, 8, 512], BF16)
            fc2T = self.sbuf(ctx, "fc2T_sb", [128, 4, 1024], BF16)
            w1e = [self.sbuf(ctx, f"w1e{p}", [128, 4, 512], BF16) for p in range(2)]
            w2e = [self.sbuf(ctx, f"w2e{p}", [128, 4, 512], BF16) for p in range(2)]
            dnT = [self.sbuf(ctx, f"dnT{p}", [128, 512], BF16) for p in range(2)]
            upw = [self.sbuf(ctx, f"upw{p}", [128, 8, 128], BF16) for p in range(2)]
            h_up = self.sbuf(ctx, "h_up", [128, 32, 512], BF16)
            lat = self.sbuf(ctx, "lat", [128, 4, 512], BF16)
            h1 = self.sbuf(ctx, "h1", [128, 4, 512], BF16)
            comb = self.sbuf(ctx, "comb", [128, 4, 512], F32)
            comb_fm = self.sbuf(ctx, "comb_fm", [128, 4, 512], BF16)
            cw = self.sbuf(ctx, "cw", [128, 4, 8], F32)
            gateT = self.sbuf(ctx, "gateT", [128, 8, 8], BF16)
            cbb = self.sbuf(ctx, "cbb", [128, 8], F32)
            rt = self.sbuf(ctx, "rt", [128, 64], F32)
            rti = self.sbuf(ctx, "rti", [128, 8], mybir.dt.uint32)
            gate_sb = self.sbuf(ctx, "gate_sb", [8, D], F32)
            x2 = self.sbuf(ctx, "ln_x2", [128, 1024], F32)
            rows = self.sbuf(ctx, "ln_rows", [1, 1536], F32)
            stage = x2

            ps = [ctx.enter_context(nc.psum_tensor(self.pre + f"psb{i}", [128, 512], F32))
                  for i in range(6)]
            ps_stats = ctx.enter_context(nc.psum_tensor(self.pre + "ps_stats", [33, 512], F32))
            ps_small = ctx.enter_context(nc.psum_tensor(self.pre + "ps_small", [128, 512], F32))
            self._stats = ps_stats
            self.ps_tp = [ps[4], ps[5]]

            if not getattr(self, "prep_emitted", False):
                self.emit_prep()
            sch.fence_include_val("st_prep", self.prep_st_total)
            sch.op("sync", lambda e: e.dma_start(out=gate_sb[:], in_=gate_d),
                   incs=[("ld_sync", 16)])
            sch.fence()
            sch.fence_include_val("st_prep", self.prep_st_total)

            for c in range(8):
                sch.op("sync", lambda e, c=c: e.dma_start(
                    out=fc1T[:, c, :],
                    in_=fc1T_d[c * 128:(c + 1) * 128, :]), incs=[("ld_sync", 16)])
            for lc in range(4):
                sch.op("sync", lambda e, lc=lc: e.dma_start(
                    out=fc2T[:, lc, :],
                    in_=fc2T_d[lc * 128:(lc + 1) * 128, :]), incs=[("ld_sync", 16)])
            ld = sch.val("ld_sync")
            for c in range(8):
                pe = sch.op("tensor", lambda e, c=c: e.transpose(
                    ps_small[0:128, 0:8], gate_sb[0:8, c * 128:(c + 1) * 128],
                    sb["c_ident_f32"][0:8, 0:8]),
                    waits=[("ld_sync", ld), ("dve", sch.val("dve"))],
                    incs=[("pe", 1)])
                sch.op("vector", lambda e, c=c: e.tensor_copy(
                    gateT[:, c, :], ps_small[0:128, 0:8]),
                    waits=[("pe", pe["pe"])], incs=[("dve", 1)])
                sch.fence()
            ldc = sch.op("sync", lambda e: e.dma_start(
                out=rows[0:1, 0:8], in_=cb_d.rearrange("(o d) -> o d", o=1)),
                incs=[("ld_sync", 16)])
            pe = sch.op("tensor", lambda e: e.matmul(
                ps_small[:, 0:8], onesr[0:1, :], rows[0:1, 0:8],
                start=True, stop=True),
                waits=[("ld_sync", ldc["ld_sync"])], incs=[("pe", 1)])
            sch.op("scalar", lambda e: e.copy(cbb[:], ps_small[:, 0:8]),
                   waits=[("pe", pe["pe"])], incs=[("act", 1)])
            sch.fence()

            for sq in range(SEQ_PER_CORE):
                self.seq(sq, x_in, y_out, w1T_d, w2T_d, upT_d, downT_d,
                         x_res, xn, fc1T, fc2T, w1e, w2e, dnT, upw,
                         h_up, lat, h1, comb, comb_fm, gateT, cbb, rt, rti,
                         x2, rows, stage, ps, ps_small)


    ln = AttnProgram.ln
    sbuf2 = None

    def seq(self, sq, x_in, y_out, w1T_d, w2T_d, upT_d, downT_d,
            x_res, xn, fc1T, fc2T, w1e, w2e, dnT, upw,
            h_up, lat, h1, comb, comb_fm, gateT, cbb, rt, rti,
            x2, rows, stage, ps, ps_small):
        nc, sch, sb = self.nc, self.sch, self.sb
        base = sq * S
        ident = sb["c_ident_f32"]
        iota8 = sb["c_iota8"]

        sch.fence()
        pe_after_tt = 0
        copy_done = {0: 0, 1: 0}
        for tt in range(16):
            ld = sch.op("sync", lambda e, tt=tt: e.dma_start(
                out=stage[:], in_=x_in[base + tt * 128: base + (tt + 1) * 128, :]),
                waits=[("pe", pe_after_tt)], incs=[("ld_sync", 16)])
            for c in range(8):
                par = c % 2
                pe = sch.op("tensor", lambda e, c=c, par=par: e.transpose(
                    self.ps_tp[par][:, 0:128], stage[:, c * 128:(c + 1) * 128],
                    ident[:]),
                    waits=[("ld_sync", ld["ld_sync"]), ("dve", copy_done[par])],
                    incs=[("pe", 1)])
                cp = sch.op("vector", lambda e, c=c, tt=tt, par=par: e.tensor_copy(
                    x_res[c][:, tt * 128:(tt + 1) * 128], self.ps_tp[par][:, 0:128]),
                    waits=[("pe", pe["pe"])], incs=[("dve", 1)])
                copy_done[par] = cp["dve"]
            pe_after_tt = sch.val("pe")
        sch.fence()

        for tch in range(4):
            self.ln(x_res, xn, tch * 512, 512, self._stats, ps[0:2])
        sch.fence()

        for tb in range(4):
            self.tile(sq, tb, y_out, w1T_d, w2T_d, upT_d, downT_d,
                      x_res, xn, fc1T, fc2T, w1e, w2e, dnT, upw,
                      h_up, lat, h1, comb, comb_fm, gateT, cbb, rt, rti,
                      ps, ps_small)

        copy_done = {0: 0, 1: 0}
        for tt in range(16):
            st_before = sch.val("st")
            for c in range(8):
                par = c % 2
                pe = sch.op("tensor", lambda e, c=c, tt=tt, par=par: e.transpose(
                    self.ps_tp[par][:, 0:128], x_res[c][:, tt * 128:(tt + 1) * 128],
                    ident[:]),
                    waits=[("st", st_before), ("dve", copy_done[par])],
                    incs=[("pe", 1)])
                cp = sch.op("vector", lambda e, c=c, par=par: e.tensor_copy(
                    stage[:, c * 128:(c + 1) * 128], self.ps_tp[par][:, 0:128]),
                    waits=[("pe", pe["pe"])], incs=[("dve", 1)])
                copy_done[par] = cp["dve"]
            dv_all = sch.val("dve")
            sch.op("gpsimd", lambda e, tt=tt: e.dma_start(
                out=y_out[base + tt * 128: base + (tt + 1) * 128, :], in_=stage[:]),
                waits=[("dve", dv_all)], incs=[("st", 16)])
            sch.fence()

    def tile(self, sq, tb, y_out, w1T_d, w2T_d, upT_d, downT_d,
             x_res, xn, fc1T, fc2T, w1e, w2e, dnT, upw,
             h_up, lat, h1, comb, comb_fm, gateT, cbb, rt, rti,
             ps, ps_small):
        sch, sb = self.sch, self.sb
        iota8 = sb["c_iota8"]
        x2 = sb["ln_x2"]
        cw = sb["cw"]
        t0 = tb * 512
        self._h1_evac = {}
        self._comb_rd = {}
        sch.fence()

        for st in range(4):
            for c in range(8):
                sch.op("tensor", lambda e, st=st, c=c: e.matmul(
                    ps_small[:, st * 8:(st + 1) * 8],
                    xn[c][:, t0 + st * 128: t0 + (st + 1) * 128],
                    gateT[:, c, :], start=(c == 0), stop=(c == 7)),
                    incs=[("pe", 1)] if c == 7 else ())
        pe_logits = sch.val("pe")
        combcol_done = []

        def vop(emit, extra=()):
            return sch.op("vector", emit,
                          waits=list(extra),
                          hard_waits=[("dve", sch.val("dve"))],
                          incs=[("dve", 1)])

        for st in range(4):
            sc_ = rt[:, 0:8]
            sfc = rt[:, 8:16]
            msk = rt[:, 16:24]
            t8v = rt[:, 24:32]
            oh1 = rt[:, 32:40]
            oh2 = rt[:, 40:48]
            tmp8 = rt[:, 48:56]
            s1 = rt[:, 56:57]
            s2 = rt[:, 57:58]
            m0 = rt[:, 58:59]
            m1 = rt[:, 59:60]
            w1n = rt[:, 60:61]
            w2n = rt[:, 61:62]
            g0 = rt[:, 62:63]
            g1 = rt[:, 63:64]
            ac = sch.op("scalar", lambda e, st=st: e.activation(
                sc_, ps_small[:, st * 8:(st + 1) * 8], AF.Sigmoid),
                waits=[("pe", pe_logits), ("dve", sch.val("dve"))],
                incs=[("act", 1)])
            vop(lambda e: e.tensor_add(sfc, sc_, cbb[:]),
                extra=[("act", ac["act"])])
            for g_, base_ in ((g0, 8), (g1, 12)):
                a_ = rt[:, base_:base_ + 1]
                b_ = rt[:, base_ + 1:base_ + 2]
                c_ = rt[:, base_ + 2:base_ + 3]
                d_ = rt[:, base_ + 3:base_ + 4]
                mx1 = rt[:, 48:49]
                mn1 = rt[:, 49:50]
                mx2 = rt[:, 50:51]
                mn2 = rt[:, 51:52]
                t1_ = rt[:, 52:53]
                t2_ = rt[:, 53:54]
                vop(lambda e, a_=a_, b_=b_, mx1=mx1: e.tensor_max(mx1, a_, b_))
                vop(lambda e, a_=a_, b_=b_, mn1=mn1: e.tensor_tensor(mn1, a_, b_, ALU.min))
                vop(lambda e, c_=c_, d_=d_, mx2=mx2: e.tensor_max(mx2, c_, d_))
                vop(lambda e, c_=c_, d_=d_, mn2=mn2: e.tensor_tensor(mn2, c_, d_, ALU.min))
                vop(lambda e, mn1=mn1, mx2=mx2, t1_=t1_: e.tensor_max(t1_, mn1, mx2))
                vop(lambda e, mn2=mn2, mx1=mx1, t2_=t2_: e.tensor_max(t2_, mn2, mx1))
                vop(lambda e, mx1=mx1, t1_=t1_: e.tensor_add(t1_, mx1, t1_))
                vop(lambda e, mx2=mx2, t2_=t2_: e.tensor_add(t2_, mx2, t2_))
                vop(lambda e, g_=g_, t1_=t1_, t2_=t2_: e.tensor_max(g_, t1_, t2_))
            vop(lambda e: e.tensor_tensor(m0, g0, g1, ALU.is_ge))
            vop(lambda e: e.tensor_scalar(m1, m0, -1.0, 1.0, ALU.mult, ALU.add))
            vop(lambda e: e.tensor_scalar_mul(rt[:, 16:20], rt[:, 8:12], m0))
            vop(lambda e: e.tensor_scalar_mul(rt[:, 20:24], rt[:, 12:16], m1))
            vop(lambda e: e.max(t8v, msk))
            vop(lambda e: e.max_index(rti[:], t8v, msk))
            vop(lambda e: e.tensor_copy(rt[:, 54:56], rti[:, 0:2]))
            i1 = rt[:, 54:55]
            i2 = rt[:, 55:56]
            vop(lambda e: e.tensor_single_scalar(oh1, iota8[:], i1, ALU.is_equal))
            vop(lambda e: e.tensor_single_scalar(oh2, iota8[:], i2, ALU.is_equal))
            vop(lambda e: e.tensor_mul(tmp8, oh1, sc_))
            vop(lambda e: e.reduce_sum(s1, tmp8, axis=mybir.AxisListType.X))
            vop(lambda e: e.tensor_mul(tmp8, oh2, sc_))
            vop(lambda e: e.reduce_sum(s2, tmp8, axis=mybir.AxisListType.X))
            vop(lambda e: e.tensor_add(w1n, s1, s2))
            vop(lambda e: e.tensor_scalar_add(w1n, w1n, 1e-20))
            vop(lambda e: e.reciprocal(w2n, w1n))
            vop(lambda e: e.tensor_mul(w1n, s1, w2n))
            vop(lambda e: e.tensor_mul(w2n, s2, w2n))
            vop(lambda e: e.tensor_scalar_mul(oh1, oh1, w1n))
            vop(lambda e: e.scalar_tensor_tensor(t8v, oh2, w2n, oh1, ALU.mult, ALU.add))
            vop(lambda e, st=st: e.tensor_copy(cw[:, st, :], t8v))
            combcol_done.append(sch.val("dve"))
            if sq == 0 and tb == 0 and st == 0:
                sch.op("gpsimd", lambda e: e.dma_start(
                    out=self.dbg[:, 2560:2624], in_=rt[:]),
                    waits=[("dve", sch.val("dve"))], incs=[("st", 16)])
                vop(lambda e: e.tensor_copy(rt[:, 48:56], rti[:]),
                    extra=[("st", sch.val("st"))])
                sch.op("gpsimd", lambda e: e.dma_start(
                    out=self.dbg[:, 2624:2632], in_=rt[:, 48:56]),
                    waits=[("dve", sch.val("dve"))], incs=[("st", 16)])
                sch.fence()
        sch.fence()

        lat_evac = {0: 0, 1: 0}
        dve_pre = sch.val("dve")
        for lc in range(4):
            for c in range(8):
                pe = sch.op("tensor", lambda e, lc=lc, c=c: e.matmul(
                    ps[lc % 2][:], fc1T[:, c, lc * 128:(lc + 1) * 128],
                    xn[c][:, t0:t0 + 512], start=(c == 0), stop=(c == 7)),
                    waits=[("dve", dve_pre), ("act", lat_evac[lc % 2])]
                    if c == 0 else (),
                    incs=[("pe", 1)] if c == 7 else ())
            a = sch.op("scalar", lambda e, lc=lc: e.copy(lat[:, lc, :], ps[lc % 2][:]),
                       waits=[("pe", pe["pe"])], incs=[("act", 1)])
            lat_evac[lc % 2] = a["act"]

        lat_done = sch.val("act")
        we_ld = {}
        for e_ in range(8):
            par = e_ % 2
            wld = sch.op("gpsimd", lambda e, e_=e_, par=par: e.dma_start(
                out=w1e[par][:], in_=w1T_d[e_].rearrange("(lc p) i -> p lc i", p=128)),
                waits=[("pe", we_ld.get(par, 0))], incs=[("ld_gp", 16)])
            wld2 = sch.op("gpsimd", lambda e, e_=e_, par=par: e.dma_start(
                out=w2e[par][:], in_=w2T_d[e_].rearrange("(ic p) l -> p ic l", p=128)),
                incs=[("ld_gp", 16)])
            for ic in range(4):
                par23 = 2 + ic % 2
                w0 = ([("ld_gp", wld2["ld_gp"]),
                       ("act", max(lat_done, self._h1_evac.get(par23, 0))),
                       ("dve", self._comb_rd.get(par23, 0))] if ic < 2 else
                      [("act", max(lat_done, self._h1_evac.get(par23, 0))),
                       ("dve", self._comb_rd.get(par23, 0))])
                for lc in range(4):
                    pe = sch.op("tensor", lambda e, ic=ic, lc=lc, par=par,
                                par23=par23: e.matmul(
                        ps[par23][:],
                        w1e[par][:, lc, ic * 128:(ic + 1) * 128],
                        lat[:, lc, :], start=(lc == 0), stop=(lc == 3)),
                        waits=(w0 if lc == 0 else ()),
                        incs=[("pe", 1)] if lc == 3 else ())
                a1 = sch.op("scalar", lambda e, ic=ic, par23=par23: e.activation(
                    x2[:, 0:512], ps[par23][:], AF.Relu),
                    waits=[("pe", pe["pe"])], incs=[("act", 1)])
                a2 = sch.op("scalar", lambda e, ic=ic: e.square(h1[:, ic, :], x2[:, 0:512]),
                            incs=[("act", 1)])
                self._h1_evac[par23] = a2["act"]
            act_h1 = sch.val("act")
            for st in range(4):
                par23 = 2 + st % 2
                for ic in range(4):
                    pe = sch.op("tensor", lambda e, st=st, ic=ic, par=par,
                                par23=par23: e.matmul(
                        ps[par23][:],
                        h1[:, ic, st * 128:(st + 1) * 128],
                        w2e[par][:, ic, :], start=(ic == 0), stop=(ic == 3)),
                        waits=([("act", act_h1),
                                ("dve", self._comb_rd.get(par23, 0)),
                                ("act", self._h1_evac.get(par23, 0))]
                               if ic == 0 else ()),
                        incs=[("pe", 1)] if ic == 3 else ())
                if e_ == 0:
                    cc = sch.op("vector", lambda e, st=st, e_=e_, par23=par23:
                                e.tensor_scalar_mul(
                        comb[:, st, :], ps[par23][:], cw[:, st, e_:e_ + 1]),
                        waits=[("pe", pe["pe"])],
                        hard_waits=[("dve", combcol_done[st])], incs=[("dve", 1)])
                else:
                    cc = sch.op("vector", lambda e, st=st, e_=e_, par23=par23:
                                e.scalar_tensor_tensor(
                        comb[:, st, :], ps[par23][:], cw[:, st, e_:e_ + 1],
                        comb[:, st, :], ALU.mult, ALU.add),
                        waits=[("pe", pe["pe"])], incs=[("dve", 1)])
                self._comb_rd[par23] = cc["dve"]
            we_ld[par] = sch.val("pe")
        sch.fence()

        cp_done = {0: 0, 1: 0}
        for st in range(4):
            for lc in range(4):
                par = lc % 2
                pe = sch.op("tensor", lambda e, st=st, lc=lc, par=par: e.transpose(
                    self.ps_tp[par][:, 0:128],
                    comb[:, st, lc * 128:(lc + 1) * 128], sb["c_ident_f32"][:]),
                    waits=[("dve", max(sch.val("dve") if st == 0 and lc == 0 else 0,
                                       cp_done[par]))],
                    incs=[("pe", 1)])
                cpd = sch.op("vector", lambda e, st=st, lc=lc, par=par: e.tensor_copy(
                    comb_fm[:, lc, st * 128:(st + 1) * 128],
                    self.ps_tp[par][:, 0:128]),
                    waits=[("pe", pe["pe"])], incs=[("dve", 1)])
                cp_done[par] = cpd["dve"]
        sch.fence()

        prev_up = {}
        up_evac = {0: 0, 1: 0}
        for fc in range(32):
            par = fc % 2
            srcw = upT_d.rearrange("(c p) f -> p c f", p=128)[
                :, :, fc * 128:(fc + 1) * 128]
            ldu = sch.op("gpsimd", lambda e, srcw=srcw, par=par: e.dma_start(
                out=upw[par][:], in_=srcw),
                waits=[("pe", prev_up[par])] if par in prev_up else (),
                incs=[("ld_gp", 16)])
            for c in range(8):
                pe = sch.op("tensor", lambda e, fc=fc, c=c, par=par: e.matmul(
                    ps[par][:], upw[par][:, c, :],
                    xn[c][:, t0:t0 + 512], start=(c == 0), stop=(c == 7)),
                    waits=([("ld_gp", ldu["ld_gp"]),
                            ("act", up_evac[par])] if c == 0 else ()),
                    incs=[("pe", 1)] if c == 7 else ())
            a1 = sch.op("scalar", lambda e, fc=fc, par=par: e.activation(
                x2[:, 0:512], ps[par][:], AF.Relu),
                waits=[("pe", pe["pe"])], incs=[("act", 1)])
            a2 = sch.op("scalar", lambda e, fc=fc: e.square(h_up[:, fc, :], x2[:, 0:512]),
                        incs=[("act", 1)])
            up_evac[par] = a2["act"]
            prev_up[par] = sch.val("pe")
        sch.fence()

        prev_dn = {}
        for dhalf in range(2):
            for fc in range(32):
                par = fc % 2
                ldd = sch.op("gpsimd", lambda e, fc=fc, par=par, dhalf=dhalf:
                             e.dma_start(
                    out=dnT[par][:],
                    in_=downT_d[fc * 128:(fc + 1) * 128,
                                dhalf * 512:(dhalf + 1) * 512]),
                    waits=[("pe", prev_dn[par])] if par in prev_dn else (),
                    incs=[("ld_gp", 16)])
                for dq in range(4):
                    sch.op("tensor", lambda e, fc=fc, dq=dq, par=par: e.matmul(
                        ps[dq][:], dnT[par][:, dq * 128:(dq + 1) * 128],
                        h_up[:, fc, :], start=(fc == 0), stop=False),
                        waits=([("ld_gp", ldd["ld_gp"]),
                                ("dve", sch.val("dve"))] if dq == 0 and fc == 0
                               else [("ld_gp", ldd["ld_gp"])] if dq == 0 else ()),
                        incs=[("pe", 1)] if dq == 3 else ())
                prev_dn[par] = sch.val("pe")
            for lc in range(4):
                for dq in range(4):
                    pe = sch.op("tensor", lambda e, lc=lc, dq=dq, dhalf=dhalf:
                                e.matmul(
                        ps[dq][:],
                        fc2T[:, lc, :][:, dhalf * 512 + dq * 128:
                                       dhalf * 512 + (dq + 1) * 128],
                        comb_fm[:, lc, :],
                        start=False, stop=(lc == 3)),
                        incs=[("pe", 1)] if lc == 3 and dq == 3 else ())
            pe_all = sch.val("pe")
            for dq in range(4):
                dch = dhalf * 4 + dq
                sch.op("vector", lambda e, dq=dq, dch=dch: e.tensor_add(
                    x_res[dch][:, t0:t0 + 512], ps[dq][:],
                    x_res[dch][:, t0:t0 + 512]),
                    waits=[("pe", pe_all)], incs=[("dve", 1)])
            sch.fence()
        if sq == 0 and tb == 0:
            sch.op("vector", lambda e: e.tensor_copy(x2[:, 0:512], h_up[:, 0, :]),
                   incs=[("dve", 1)])
            sch.op("gpsimd", lambda e: e.dma_start(
                out=self.dbg[:, 0:512], in_=x2[:, 0:512]),
                waits=[("dve", sch.val("dve"))], incs=[("st", 16)])
            sch.op("vector", lambda e: e.tensor_copy(x2[:, 0:512], lat[:, 0, :]),
                   waits=[("st", sch.val("st"))], incs=[("dve", 1)])
            sch.op("gpsimd", lambda e: e.dma_start(
                out=self.dbg[:, 512:1024], in_=x2[:, 0:512]),
                waits=[("dve", sch.val("dve"))], incs=[("st", 16)])
            sch.op("gpsimd", lambda e: e.dma_start(
                out=self.dbg[:, 1024:1536], in_=comb[:, 0, :]),
                incs=[("st", 16)])
            sch.op("gpsimd", lambda e: e.dma_start(
                out=self.dbg[:, 1536:1544], in_=cw[:, 0, :]),
                incs=[("st", 16)])
            sch.op("vector", lambda e: e.tensor_copy(x2[:, 0:512], xn[0][:, 0:512]),
                   waits=[("st", sch.val("st"))], incs=[("dve", 1)])
            sch.op("gpsimd", lambda e: e.dma_start(
                out=self.dbg[:, 2048:2560], in_=x2[:, 0:512]),
                waits=[("dve", sch.val("dve"))], incs=[("st", 16)])
            sch.fence()




class L1FfnProgram:

    CONSTS = ("c_ident_f32", "c_ident_bf", "c_iota8", "c_ones_row")

    def __init__(self, nc, sch=None, pre=""):
        self.nc = nc
        self.pre = pre
        self.sch = sch if sch is not None else Sched(nc)
        self.sb = {}

    def sbuf(self, ctx, name, shape, dt, **kw):
        t = ctx.enter_context(self.nc.sbuf_tensor(self.pre + name, shape, dt, **kw))
        self.sb[name] = t
        return t

    def build(self):
        nc, sch = self.nc, self.sch
        ctx = contextlib.ExitStack()
        cdram = declare_const_dram(nc, self.CONSTS)
        x_in = nc.dram_tensor("x", [SEQ_PER_CORE * S, D], F32, kind="ExternalInput").ap()
        y_out = nc.dram_tensor("y", [SEQ_PER_CORE * S, D], F32, kind="ExternalOutput").ap()
        with ctx:
            block = ctx.enter_context(nc.Block())
            for s_ in ("ld_sync", "ld_gp", "ld_prep", "ldt_prep", "st_prep",
                        "pe", "act", "dve", "pool", "st"):
                sch.add_sem(s_, ctx.enter_context(nc.semaphore(s_)))
            self.prep_stg = [ctx.enter_context(nc.sbuf_tensor(
                self.pre + f"prep_stg{i}", [128, 512], BF16,
                is_transpose_dest=True)) for i in range(2)]
            with contextlib.ExitStack() as pctx:
                self.emit_body(pctx, x_in, y_out, cdram)
                st_total = sch.val("st")
                sch.op("gpsimd", lambda e: e.nop(), waits=[("st", st_total)])
                sch.emit(block)
        return nc

    def declare_w(self):
        if getattr(self, "_declared", False):
            return
        self._declared = True
        nc, pre = self.nc, self.pre
        self.nw_d = nc.dram_tensor(pre + "norm_w", [D], F32, kind="ExternalInput").ap()
        self.wma_d = nc.dram_tensor(pre + "w_macro", [4, D], F32, kind="ExternalInput").ap()
        self.wmi_d = nc.dram_tensor(pre + "w_micro", [4, 2, D], F32, kind="ExternalInput").ap()
        self.w1_d = nc.dram_tensor(pre + "w1", [8, 2048, D], F32, kind="ExternalInput").ap()
        self.w2_d = nc.dram_tensor(pre + "w2", [8, D, 2048], F32, kind="ExternalInput").ap()
        self.w1_bf = nc.dram_tensor(pre + "w1_bf", [8, 2048, D], BF16, kind="Internal").ap()
        self.w2_bf = nc.dram_tensor(pre + "w2_bf", [8, D, 2048], BF16, kind="Internal").ap()
        self.w1T_d = nc.dram_tensor(pre + "w1T", [8, D, 2048], BF16, kind="Internal").ap()
        self.w2T_d = nc.dram_tensor(pre + "w2T", [8, 2048, D], BF16, kind="Internal").ap()

    def emit_prep_casts(self):
        self.declare_w()
        sch = self.sch
        self.casts_emitted = True
        cast_ld = 0
        for (s_, t_) in ((self.w1_d, self.w1_bf), (self.w2_d, self.w2_bf)):
            for e_ in range(8):
                cl = sch.op("gpsimd", lambda e, s_=s_, t_=t_, e_=e_: e.dma_start(
                    out=t_[e_], in_=s_[e_]), incs=[("ld_prep", 16)], nofence=True)
                cast_ld = cl["ld_prep"]
        self.cast_ld = cast_ld

    def emit_prep_T(self):
        if not getattr(self, "casts_emitted", False):
            self.emit_prep_casts()
        sch = self.sch
        self.prep_emitted = True
        state = {"i": 0, 0: sch.val("st_prep"), 1: sch.val("st_prep")}
        for e_ in range(8):
            emit_prep_T(sch, self.prep_stg, self.w1_bf[e_], self.w1T_d[e_], 2048, D,
                        self.cast_ld, state)
            emit_prep_T(sch, self.prep_stg, self.w2_bf[e_], self.w2T_d[e_], D, 2048,
                        self.cast_ld, state)
        flush_prep_T(sch, self.prep_stg, state)
        self.prep_st_total = sch.val("st_prep")

    def emit_body(self, ctx, x_in, y_out, cdram):
        nc, sch, sb = self.nc, self.sch, self.sb
        pre = self.pre
        self.declare_w()
        nw_d, wma_d, wmi_d = self.nw_d, self.wma_d, self.wmi_d
        w1T_d, w2T_d = self.w1T_d, self.w2T_d

        if True:
            for name in self.CONSTS:
                shape, dt = CONST_SPECS[name]
                dram = cdram[name]
                t = self.sbuf(ctx, name + "_sb", list(shape), dt)
                self.sb[name] = t
                sch.op("sync", lambda e, t=t, dram=dram: e.dma_start(
                    out=t[:], in_=dram), incs=[("ld_sync", 16)])

            ones_col = self.sbuf(ctx, "ones_col", [128, 1], F32)
            onesr = self.sbuf(ctx, "onesr", [1, 128], F32)
            sch.op("vector", lambda e: e.memset(ones_col[:], 1.0), incs=[("dve", 1)])
            sch.op("vector", lambda e: e.memset(onesr[:], 1.0), incs=[("dve", 1)])

            z = self.sbuf(ctx, "z", [128, 16, 1024], F32)
            zn_fm = self.sbuf(ctx, "zn_fm", [128, 8, S], BF16)
            w1e = self.sbuf(ctx, "w1e", [128, 8, 2048], BF16)
            w2e = self.sbuf(ctx, "w2e", [128, 16, 1024], BF16)
            h_fm = self.sbuf(ctx, "h_fm", [128, 16, 512], BF16)
            cw8 = self.sbuf(ctx, "cw8", [128, 16, 8], F32)
            nwb = self.sbuf(ctx, "nwb", [128, 1024], F32)
            wmT = self.sbuf(ctx, "wmT", [128, 8, 12], BF16)
            wm_sb = self.sbuf(ctx, "wm_sb", [12, D], F32)
            rt = self.sbuf(ctx, "rt", [128, 64], F32)
            rti = self.sbuf(ctx, "rti", [128, 8], mybir.dt.uint32)
            xs = self.sbuf(ctx, "xs", [128, 1024], F32)
            x2s = self.sbuf(ctx, "x2s", [128, 1024], F32)
            self.sbuf(ctx, "zn_bf", [128, 1024], BF16)
            rows = self.sbuf(ctx, "ln_rows", [1, 512], F32)

            ps = [ctx.enter_context(nc.psum_tensor(self.pre + f"psb{i}", [128, 512], F32))
                  for i in range(4)]
            ps_h = [ctx.enter_context(nc.psum_tensor(self.pre + f"psh{i}", [128, 512], F32))
                    for i in range(2)]
            ps_bt = [ctx.enter_context(nc.psum_tensor(self.pre + f"psbt{i}", [128, 1024], BF16))
                     for i in range(2)]
            self.ps, self.ps_h, self.ps_bt = ps, ps_h, ps_bt

            if not getattr(self, "prep_emitted", False):
                self.emit_prep_T()
            sch.fence_include_val("st_prep", self.prep_st_total)
            sch.op("sync", lambda e: e.dma_start(
                out=wm_sb[0:4, :], in_=wma_d), incs=[("ld_sync", 16)])
            sch.op("sync", lambda e: e.dma_start(
                out=wm_sb[4:12, :], in_=wmi_d.rearrange("g e d -> (g e) d")),
                incs=[("ld_sync", 16)])
            sch.fence()

            for half in range(2):
                ldh = sch.op("sync", lambda e, half=half: e.dma_start(
                    out=rows[0:1, 0:512],
                    in_=nw_d[half * 512:(half + 1) * 512]
                    .rearrange("(o d) -> o d", o=1)),
                    waits=[("pe", sch.val("pe")), ("act", sch.val("act"))],
                    incs=[("ld_sync", 16)])
                pe = sch.op("tensor", lambda e: e.matmul(
                    ps[0][:], onesr[0:1, :], rows[0:1, 0:512], start=True, stop=True),
                    waits=[("ld_sync", ldh["ld_sync"])], incs=[("pe", 1)])
                sch.op("scalar", lambda e, half=half: e.copy(
                    nwb[:, half * 512:(half + 1) * 512], ps[0][:]),
                    waits=[("pe", pe["pe"])], incs=[("act", 1)])
                sch.fence()
            for c in range(8):
                pe = sch.op("tensor", lambda e, c=c: e.transpose(
                    ps[1][0:128, 0:12], wm_sb[0:12, c * 128:(c + 1) * 128],
                    sb["c_ident_f32"][0:12, 0:12]),
                    waits=[("dve", sch.val("dve"))], incs=[("pe", 1)])
                sch.op("vector", lambda e, c=c: e.tensor_copy(
                    wmT[:, c, :], ps[1][0:128, 0:12]),
                    waits=[("pe", pe["pe"])], incs=[("dve", 1)])
                sch.fence()

            for sq in range(SEQ_PER_CORE):
                self.seq(sq, x_in, y_out, w1T_d, w2T_d)


    def vop(self, emit, extra=()):
        sch = self.sch
        return sch.op("vector", emit, waits=list(extra),
                      hard_waits=[("dve", sch.val("dve"))], incs=[("dve", 1)])

    def seq(self, sq, x_in, y_out, w1T_d, w2T_d):
        nc, sch, sb = self.nc, self.sch, self.sb
        z, zn_fm = sb["z"], sb["zn_fm"]
        xs, x2s, rows = sb["xs"], sb["x2s"], sb["ln_rows"]
        nwb, cw8 = sb["nwb"], sb["cw8"]
        ps, ps_h, ps_bt = self.ps, self.ps_h, self.ps_bt
        base = sq * S

        sch.fence()
        for tt in range(16):
            ld = sch.op("sync", lambda e, tt=tt: e.dma_start(
                out=xs[:], in_=x_in[base + tt * 128: base + (tt + 1) * 128, :]),
                waits=[("act", sch.val("act")), ("dve", sch.val("dve"))],
                incs=[("ld_sync", 16)])
            ac = sch.op("scalar", lambda e: e.square(x2s[:], xs[:]),
                        waits=[("ld_sync", ld["ld_sync"])], incs=[("act", 1)])
            v = self.vop(lambda e: e.reduce_sum(rt_col(sb, 0), xs[:],
                                                axis=mybir.AxisListType.X),
                         extra=[("ld_sync", ld["ld_sync"])])
            v = self.vop(lambda e: e.reduce_sum(rt_col(sb, 1), x2s[:],
                                                axis=mybir.AxisListType.X),
                         extra=[("act", ac["act"])])
            self.vop(lambda e: e.tensor_scalar_mul(rt_col(sb, 0), rt_col(sb, 0), 1.0 / D))
            self.vop(lambda e: e.tensor_scalar_mul(rt_col(sb, 1), rt_col(sb, 1), 1.0 / D))
            self.vop(lambda e: e.scalar_tensor_tensor(
                rt_col(sb, 2), rt_col(sb, 0), -1.0, rt_col(sb, 0), ALU.mult, ALU.mult))
            self.vop(lambda e: e.tensor_add(rt_col(sb, 1), rt_col(sb, 1), rt_col(sb, 2)))
            v = self.vop(lambda e: e.tensor_scalar_add(rt_col(sb, 1), rt_col(sb, 1), 1e-5))
            ac = sch.op("scalar", lambda e: e.activation(
                rt_col(sb, 1), rt_col(sb, 1), AF.Sqrt),
                waits=[("dve", v["dve"])], incs=[("act", 1)])
            self.vop(lambda e: e.reciprocal(rt_col(sb, 1), rt_col(sb, 1)),
                     extra=[("act", ac["act"])])
            v = self.vop(lambda e, tt=tt: e.tensor_scalar(
                z[:, tt, :], xs[:], rt_col(sb, 0), rt_col(sb, 1),
                ALU.subtract, ALU.mult))
        sch.fence()

        for lp in range(2):
            self.loop_iter(sq, lp, w1T_d, w2T_d)

        sch.fence()
        for tt in range(16):
            ld = sch.op("sync", lambda e, tt=tt: e.dma_start(
                out=xs[:], in_=x_in[base + tt * 128: base + (tt + 1) * 128, :]),
                waits=[("st", sch.val("st")), ("dve", sch.val("dve"))],
                incs=[("ld_sync", 16)])
            v = sch.op("vector", lambda e, tt=tt: e.tensor_add(
                xs[:], xs[:], z[:, tt, :]),
                waits=[("ld_sync", ld["ld_sync"])], incs=[("dve", 1)])
            sch.op("gpsimd", lambda e, tt=tt: e.dma_start(
                out=y_out[base + tt * 128: base + (tt + 1) * 128, :], in_=xs[:]),
                waits=[("dve", v["dve"])], incs=[("st", 16)])
            sch.fence()

    def loop_iter(self, sq, lp, w1T_d, w2T_d):
        nc, sch, sb = self.nc, self.sch, self.sb
        z, zn_fm = sb["z"], sb["zn_fm"]
        xs, x2s = sb["xs"], sb["x2s"]
        nwb, cw8, wmT = sb["nwb"], sb["cw8"], sb["wmT"]
        rt, rti, iota8 = sb["rt"], sb["rti"], sb["c_iota8"]
        ps, ps_h, ps_bt = self.ps, self.ps_h, self.ps_bt
        vop = self.vop
        sch.fence()

        tp_done = {0: 0, 1: 0}
        for tt in range(16):
            ac = sch.op("scalar", lambda e, tt=tt: e.square(x2s[:], z[:, tt, :]),
                        waits=[("dve", sch.val("dve"))], incs=[("act", 1)])
            v = self.vop(lambda e: e.reduce_sum(rt_col(sb, 0), x2s[:],
                                                axis=mybir.AxisListType.X),
                         extra=[("act", ac["act"])])
            self.vop(lambda e: e.tensor_scalar_mul(rt_col(sb, 0), rt_col(sb, 0), 1.0 / D))
            v = self.vop(lambda e: e.tensor_scalar_add(rt_col(sb, 0), rt_col(sb, 0), 1e-6))
            ac = sch.op("scalar", lambda e: e.activation(
                rt_col(sb, 0), rt_col(sb, 0), AF.Sqrt),
                waits=[("dve", v["dve"])], incs=[("act", 1)])
            self.vop(lambda e: e.reciprocal(rt_col(sb, 0), rt_col(sb, 0)),
                     extra=[("act", ac["act"])])
            self.vop(lambda e, tt=tt: e.tensor_scalar_mul(
                xs[:], z[:, tt, :], rt_col(sb, 0)))
            v = sch.op("vector", lambda e: e.tensor_mul(x2s[:], xs[:], nwb[:]),
                       incs=[("dve", 1)])
            ac = sch.op("scalar", lambda e: e.copy(sb["zn_bf"][:], x2s[:]),
                        waits=[("dve", v["dve"])], incs=[("act", 1)])
            for c in range(8):
                par = c % 2
                pe = sch.op("tensor", lambda e, c=c, par=par: e.transpose(
                    ps_bt[par][:, 0:128], sb["zn_bf"][:, c * 128:(c + 1) * 128],
                    sb["c_ident_bf"][:]),
                    waits=[("act", ac["act"]), ("dve", tp_done[par])],
                    incs=[("pe", 1)])
                cp = sch.op("vector", lambda e, c=c, tt=tt, par=par: e.tensor_copy(
                    zn_fm[:, c, tt * 128:(tt + 1) * 128], ps_bt[par][:, 0:128]),
                    waits=[("pe", pe["pe"])], incs=[("dve", 1)])
                tp_done[par] = cp["dve"]
        sch.fence()

        self._route_copy = 0
        zn_done = sch.val("dve")
        for tt in range(16):
            for c in range(8):
                pe = sch.op("tensor", lambda e, tt=tt, c=c: e.matmul(
                    ps[3][:, 0:12], zn_fm[:, c, tt * 128:(tt + 1) * 128],
                    wmT[:, c, :], start=(c == 0), stop=(c == 7)),
                    waits=[("dve", zn_done), ("act", self._route_copy)]
                    if c == 0 else (),
                    incs=[("pe", 1)] if c == 7 else ())
            self.route_dance(tt, pe["pe"])
        sch.fence()

        silu_evac = {0: 0, 1: 0}
        z_upd = {0: 0, 1: 0}
        h_last_pe = 0
        y_last_pe = 0
        for e_ in range(8):
            ld1 = sch.op("gpsimd", lambda e, e_=e_: e.dma_start(
                out=sb["w1e"][:], in_=w1T_d[e_].rearrange("(c p) f -> p c f", p=128)),
                waits=[("pe", h_last_pe)], incs=[("ld_gp", 16)])
            ld2 = sch.op("gpsimd", lambda e, e_=e_: e.dma_start(
                out=sb["w2e"][:], in_=w2T_d[e_].rearrange("(c p) d -> p c d", p=128)),
                waits=[("pe", y_last_pe)], incs=[("ld_gp", 16)])
            for tb in range(4):
                t0 = tb * 512
                for fc in range(16):
                    par = fc % 2
                    for c in range(8):
                        pe = sch.op("tensor", lambda e, fc=fc, c=c, par=par, t0=t0,
                                    e_=e_: e.matmul(
                            ps_h[par][:], sb["w1e"][:, c, fc * 128:(fc + 1) * 128],
                            zn_fm[:, c, t0:t0 + 512], start=(c == 0), stop=(c == 7)),
                            waits=([("ld_gp", ld1["ld_gp"]),
                                    ("act", silu_evac[par])] if c == 0 else ()),
                            incs=[("pe", 1)] if c == 7 else ())
                    a = sch.op("scalar", lambda e, fc=fc, par=par: e.activation(
                        sb["h_fm"][:, fc, :], ps_h[par][:], AF.Silu),
                        waits=[("pe", pe["pe"])], incs=[("act", 1)])
                    silu_evac[par] = a["act"]
                act_h = sch.val("act")
                if tb == 3:
                    h_last_pe = sch.val("pe")
                for st in range(4):
                    tt = tb * 4 + st
                    b0 = (st % 2) * 2
                    for fc in range(16):
                        pe = sch.op("tensor", lambda e, fc=fc, st=st, e_=e_, b0=b0:
                                    e.matmul(
                            ps[b0][:], sb["h_fm"][:, fc, st * 128:(st + 1) * 128],
                            sb["w2e"][:, fc, 0:512], start=(fc == 0), stop=(fc == 15)),
                            waits=([("act", act_h), ("ld_gp", ld2["ld_gp"]),
                                    ("dve", z_upd[st % 2])]
                                   if fc == 0 else ()),
                            incs=())
                        pe = sch.op("tensor", lambda e, fc=fc, st=st, e_=e_, b0=b0:
                                    e.matmul(
                            ps[b0 + 1][:], sb["h_fm"][:, fc, st * 128:(st + 1) * 128],
                            sb["w2e"][:, fc, 512:1024], start=(fc == 0), stop=(fc == 15)),
                            incs=[("pe", 1)] if fc == 15 else ())
                    for dh in range(2):
                        zu = sch.op("vector", lambda e, tt=tt, dh=dh, e_=e_, b0=b0:
                                    e.scalar_tensor_tensor(
                            z[:, tt, dh * 512:(dh + 1) * 512], ps[b0 + dh][:],
                            cw8[:, tt, e_:e_ + 1], z[:, tt, dh * 512:(dh + 1) * 512],
                            ALU.mult, ALU.add),
                            waits=[("pe", pe["pe"])], incs=[("dve", 1)])
                    z_upd[st % 2] = zu["dve"]
            y_last_pe = sch.val("pe")
        sch.fence()

    def route_dance(self, tt, pe_logits):
        sch, sb = self.sch, self.sb
        rt, rti, iota8 = sb["rt"], sb["rti"], sb["c_iota8"]
        vop = self.vop
        lg = rt[:, 0:12]
        mac = rt[:, 12:16]
        pad8 = rt[:, 16:24]
        t8v = rt[:, 24:32]
        li = rt[:, 32:36]
        oh = rt[:, 40:48]
        mx = rt[:, 48:49]
        ssum = rt[:, 49:50]
        v1 = rt[:, 50:51]
        v2 = rt[:, 51:52]
        i1 = rt[:, 52:53]
        i2 = rt[:, 53:54]
        e1 = rt[:, 54:55]
        e2 = rt[:, 55:56]
        tmp = rt[:, 56:57]
        ac = sch.op("scalar", lambda e: e.copy(lg, self.ps[3][:, 0:12]),
                    waits=[("pe", pe_logits), ("dve", sch.val("dve"))],
                    incs=[("act", 1)])
        self._route_copy = ac["act"]
        vop(lambda e: e.tensor_max(mx, rt[:, 0:1], rt[:, 1:2]),
            extra=[("act", ac["act"])])
        vop(lambda e: e.tensor_max(tmp, rt[:, 2:3], rt[:, 3:4]))
        vop(lambda e: e.tensor_max(mx, mx, tmp))
        v = vop(lambda e: e.tensor_scalar_mul(mx, mx, -1.0))
        ac = sch.op("scalar", lambda e: e.activation(mac, lg[:, 0:4], AF.Exp,
                                                     bias=mx),
                    waits=[("dve", v["dve"])], incs=[("act", 1)])
        vop(lambda e: e.reduce_sum(ssum, mac, axis=mybir.AxisListType.X),
            extra=[("act", ac["act"])])
        vop(lambda e: e.reciprocal(ssum, ssum))
        vop(lambda e: e.tensor_scalar_mul(mac, mac, ssum))
        vop(lambda e: e.memset(rt[:, 20:24], -1.0))
        vop(lambda e: e.tensor_copy(rt[:, 16:20], mac))
        v = vop(lambda e: e.max(t8v, pad8))
        v = vop(lambda e: e.max_index(rti[:], t8v, pad8))
        vop(lambda e: e.tensor_copy(rt[:, 52:54], rti[:, 0:2]))
        vop(lambda e: e.tensor_copy(v1, t8v[:, 0:1]))
        vop(lambda e: e.tensor_copy(v2, t8v[:, 1:2]))
        vop(lambda e: e.tensor_add(tmp, v1, v2))
        vop(lambda e: e.tensor_scalar_add(tmp, tmp, 1e-9))
        vop(lambda e: e.reciprocal(tmp, tmp))
        vop(lambda e: e.tensor_mul(v1, v1, tmp))
        vop(lambda e: e.tensor_mul(v2, v2, tmp))
        for g in range(4):
            vop(lambda e, g=g: e.tensor_tensor(
                rt[:, 36 + g:37 + g], rt[:, 4 + 2 * g:5 + 2 * g],
                rt[:, 5 + 2 * g:6 + 2 * g], ALU.is_lt))
        vop(lambda e: e.tensor_copy(li, rt[:, 36:40]))
        for (i_, e_out) in ((i1, e1), (i2, e2)):
            vop(lambda e, i_=i_: e.tensor_single_scalar(
                rt[:, 60:64], iota8[:, 0:4], i_, ALU.is_equal))
            vop(lambda e: e.tensor_mul(rt[:, 60:64], rt[:, 60:64], li))
            vop(lambda e, e_out=e_out: e.reduce_sum(
                e_out, rt[:, 60:64], axis=mybir.AxisListType.X))
            vop(lambda e, i_=i_, e_out=e_out: e.scalar_tensor_tensor(
                e_out, i_, 2.0, e_out, ALU.mult, ALU.add))
        vop(lambda e: e.tensor_single_scalar(oh, iota8[:], e1, ALU.is_equal))
        vop(lambda e: e.tensor_scalar_mul(oh, oh, v1))
        vop(lambda e: e.tensor_single_scalar(rt[:, 16:24], iota8[:], e2, ALU.is_equal))
        v = vop(lambda e: e.scalar_tensor_tensor(
            rt[:, 16:24], rt[:, 16:24], v2, oh, ALU.mult, ALU.add))
        vop(lambda e, tt=tt: e.tensor_copy(sb["cw8"][:, tt, :], rt[:, 16:24]))


def rt_col(sb, i):
    return sb["rt"][:, i:i + 1]


_CONSTS = None


def _consts():
    global _CONSTS
    if _CONSTS is None:
        _CONSTS = host_constants()
    return _CONSTS


def run_attn(x, inputs, layer):
    shards = np_split_tokens(x)
    consts = _consts()
    nc = bass.Bass("TRN2", target_bir_lowering=False)
    prog = AttnProgram(nc, layer)
    prog.build()
    pre = "L0_attn_" if layer == 0 else "L1_attn_"
    wnames = ("Wq", "Wk", "Wv", "Wo") + (("bq", "bk", "bv", "bo") if layer == 0 else ())
    in_maps = []
    for i in range(NCORES):
        m = {"x": shards[i]}
        for wn in wnames:
            m[wn] = np.asarray(inputs[pre + wn], dtype=np.float32)
        for cn in prog.CONSTS:
            m[cn] = consts[cn]
        in_maps.append(m)
    res = run_bass_kernel_spmd(nc, in_maps, core_ids=list(range(NCORES)))
    return np.concatenate([r["y"].reshape(SEQ_PER_CORE, S, D) for r in res.results],
                          axis=0)


def run_l0_ffn(x, inputs):
    shards = np_split_tokens(x)
    consts = _consts()
    nc = bass.Bass("TRN2", target_bir_lowering=False)
    prog = L0FfnProgram(nc)
    prog.build()
    in_maps = []
    for i in range(NCORES):
        m = {"x": shards[i]}
        for wn in ("gate", "corr_bias", "w1", "w2", "fc1_w", "fc2_w", "up_w", "down_w"):
            m[wn] = np.asarray(inputs["L0_ffn_" + wn], dtype=np.float32)
        for cn in prog.CONSTS:
            m[cn] = consts[cn]
        in_maps.append(m)
    res = run_bass_kernel_spmd(nc, in_maps, core_ids=list(range(NCORES)))
    run_l0_ffn.dbg = res.results[0].get("dbg")
    return np.concatenate([r["y"].reshape(SEQ_PER_CORE, S, D) for r in res.results],
                          axis=0)


def run_l1_ffn(x, inputs):
    shards = np_split_tokens(x)
    consts = _consts()
    nc = bass.Bass("TRN2", target_bir_lowering=False)
    prog = L1FfnProgram(nc)
    prog.build()
    in_maps = []
    for i in range(NCORES):
        m = {"x": shards[i]}
        for wn in ("norm_w", "w_macro", "w_micro", "w1", "w2"):
            m[wn] = np.asarray(inputs["L1_ffn_" + wn], dtype=np.float32)
        for cn in prog.CONSTS:
            m[cn] = consts[cn]
        in_maps.append(m)
    res = run_bass_kernel_spmd(nc, in_maps, core_ids=list(range(NCORES)))
    return np.concatenate([r["y"].reshape(SEQ_PER_CORE, S, D) for r in res.results],
                          axis=0)


def build_mega(nc):
    sch = Sched(nc)
    all_consts = tuple(CONST_SPECS.keys())
    cdram = declare_const_dram(nc, all_consts)
    x_in = nc.dram_tensor("x", [SEQ_PER_CORE * S, D], F32, kind="ExternalInput").ap()
    y_out = nc.dram_tensor("y", [SEQ_PER_CORE * S, D], F32, kind="ExternalOutput").ap()
    xh = [nc.dram_tensor(f"xh{i}", [SEQ_PER_CORE * S, D], F32, kind="Internal").ap()
          for i in range(2)]
    progs = [AttnProgram(nc, 0, sch=sch, pre="L0_attn_"),
             L0FfnProgram(nc, sch=sch, pre="L0_ffn_"),
             AttnProgram(nc, 1, sch=sch, pre="L1_attn_"),
             L1FfnProgram(nc, sch=sch, pre="L1_ffn_")]
    ios = [(x_in, xh[0]), (xh[0], xh[1]), (xh[1], xh[0]), (xh[0], y_out)]
    ctx = contextlib.ExitStack()
    with ctx:
        block = ctx.enter_context(nc.Block())
        for s_ in ("ld_sync", "ld_gp", "ld_prep", "ldt_prep", "st_prep",
                        "pe", "act", "dve", "pool", "st"):
            sch.add_sem(s_, ctx.enter_context(nc.semaphore(s_)))
        prep_stg = [ctx.enter_context(nc.sbuf_tensor(
            f"prep_stg{i}", [128, 512], BF16, is_transpose_dest=True))
            for i in range(2)]
        for p in progs:
            p.prep_stg = prep_stg
        progs[1].emit_prep()
        progs[2].emit_prep()
        for k, (prog, (i_, o_)) in enumerate(zip(progs, ios)):
            with contextlib.ExitStack() as pctx:
                if k == 1:
                    progs[3].emit_prep_casts()
                if k == 2:
                    progs[3].emit_prep_T()
                prog.emit_body(pctx, i_, o_, cdram)
                sch.fence()
                sch.emit(block)
        st_total = sch.val("st")
        sch.op("gpsimd", lambda e: e.nop(), waits=[("st", st_total)])
        sch.emit(block)
    return nc


MEGA_WNAMES = (
    "L0_attn_Wq", "L0_attn_Wk", "L0_attn_Wv", "L0_attn_Wo",
    "L0_attn_bq", "L0_attn_bk", "L0_attn_bv", "L0_attn_bo",
    "L0_ffn_gate", "L0_ffn_corr_bias", "L0_ffn_w1", "L0_ffn_w2",
    "L0_ffn_fc1_w", "L0_ffn_fc2_w", "L0_ffn_up_w", "L0_ffn_down_w",
    "L1_attn_Wq", "L1_attn_Wk", "L1_attn_Wv", "L1_attn_Wo",
    "L1_ffn_norm_w", "L1_ffn_w_macro", "L1_ffn_w_micro",
    "L1_ffn_w1", "L1_ffn_w2",
)


def kernel(**inputs):
    x = np.asarray(inputs["L0_attn_x"], dtype=np.float32)
    shards = np_split_tokens(x)
    consts = _consts()
    nc = bass.Bass("TRN2", target_bir_lowering=False)
    build_mega(nc)
    in_maps = []
    for i in range(NCORES):
        m = {"x": shards[i]}
        for wn in MEGA_WNAMES:
            m[wn] = np.asarray(inputs[wn], dtype=np.float32)
        m.update(consts)
        in_maps.append(m)
    res = run_bass_kernel_spmd(nc, in_maps, core_ids=list(range(NCORES)))
    return np.concatenate(
        [r["y"].reshape(SEQ_PER_CORE, S, D) for r in res.results],
        axis=0).astype(np.float32)


def kernel_unfused(**inputs):
    x = np.asarray(inputs["L0_attn_x"], dtype=np.float32)
    x = run_attn(x, inputs, 0)
    x = run_l0_ffn(x, inputs)
    x = run_attn(x, inputs, 1)
    x = run_l1_ffn(x, inputs)
    return x.astype(np.float32)


# version 11
# speedup vs baseline: 1.0151x; 1.0094x over previous
import contextlib
import numpy as np
import ml_dtypes
import concourse.bass as bass
import concourse.mybir as mybir
from concourse.bass_utils import run_bass_kernel_spmd

F32 = mybir.dt.float32
BF16 = mybir.dt.bfloat16
AF = mybir.ActivationFunctionType
ALU = mybir.AluOpType

D = 1024
S = 2048
SEQ_PER_CORE = 4
NCORES = 8
NH = 16
DH = 64
NEG = -30000.0


class Sched:

    ENGINES = ("sync", "tensor", "scalar", "vector", "gpsimd")

    def __init__(self, nc):
        self.nc = nc
        self.items = []
        self.counts = {}
        self.sems = {}
        self.fence_waits = {}
        self.emitted_upto = 0
        self.e_waited = {e: {} for e in self.ENGINES}
        self.e_selfcnt = {e: {} for e in self.ENGINES}

    def add_sem(self, name, handle):
        self.sems[name] = handle
        self.counts[name] = 0

    def op(self, engine, emit, waits=(), incs=(), hard_waits=(), nofence=False):
        after = {}
        w = {} if nofence else dict(self.fence_waits)
        for semname, val in waits:
            w[semname] = max(w.get(semname, 0), val)
        hw = {}
        for semname, val in hard_waits:
            hw[semname] = max(hw.get(semname, 0), val)
            w.pop(semname, None) if False else None
        inc_list = []
        for semname, amt in incs:
            self.counts[semname] += amt
            after[semname] = self.counts[semname]
            inc_list.append((semname, amt))
        self.items.append((engine, emit, sorted(w.items()), inc_list,
                           sorted(hw.items())))
        return after

    def val(self, semname):
        return self.counts[semname]

    FENCE_EXEMPT = ("ld_prep", "ldt_prep", "st_prep")

    def fence(self):
        self.fence_waits = {k: v for k, v in self.counts.items()
                            if k not in self.FENCE_EXEMPT}
        for s_, v in getattr(self, "sticky_waits", {}).items():
            self.fence_waits[s_] = max(self.fence_waits.get(s_, 0), v)

    def fence_include_val(self, sem, val):
        if not hasattr(self, "sticky_waits"):
            self.sticky_waits = {}
        self.sticky_waits[sem] = max(self.sticky_waits.get(sem, 0), val)
        self.fence_waits[sem] = max(self.fence_waits.get(sem, 0), val)

    def emit(self, block):
        new_items = self.items[self.emitted_upto:]
        self.emitted_upto = len(self.items)
        per_engine = {e: [] for e in self.ENGINES}
        for it in new_items:
            per_engine[it[0]].append(it)
        sems = self.sems

        def mk_body(items, waited, selfcnt):
            def body(eng):
                for (_e, emit, waits, incs, hard) in items:
                    for semname, val in waits:
                        if val <= waited.get(semname, 0) or val <= selfcnt.get(semname, 0):
                            continue
                        eng.wait_ge(sems[semname], val)
                        waited[semname] = val
                    for semname, val in hard:
                        if val <= waited.get(semname, 0):
                            continue
                        eng.wait_ge(sems[semname], val)
                        waited[semname] = val
                    instr = emit(eng)
                    for semname, amt in incs:
                        if instr is None:
                            eng.sem_inc(sems[semname], amt)
                        else:
                            instr = instr.then_inc(sems[semname], amt)
                        selfcnt[semname] = selfcnt.get(semname, 0) + amt
            return body

        for ename in self.ENGINES:
            if per_engine[ename]:
                getattr(block, ename)(mk_body(
                    per_engine[ename], self.e_waited[ename], self.e_selfcnt[ename]))


def host_constants():
    c = {}
    c["c_ident_f32"] = np.eye(128, dtype=np.float32)
    c["c_ident_bf"] = np.eye(128).astype(ml_dtypes.bfloat16)
    c["c_ones_row"] = np.ones((1, 512), dtype=np.float32)
    mask = np.zeros((4, 128, 512), dtype=np.float32)
    for r in range(4):
        for i in range(128):
            mask[r, i, : min(r * 128 + i, 512)] = NEG
    c["c_maskT"] = mask.astype(ml_dtypes.bfloat16)
    c["c_iota8"] = np.tile(np.arange(8, dtype=np.float32), (128, 1))
    inv_freq = 1.0 / (10000.0 ** (np.arange(0, DH, 2, dtype=np.float32) / DH))
    t = np.arange(S, dtype=np.float32)
    freqs = np.outer(t, inv_freq)
    emb = np.concatenate([freqs, freqs], axis=1)
    cos = np.cos(emb).T.astype(np.float32)
    sin = np.sin(emb).T.astype(np.float32)
    c["c_rope_cos"] = np.concatenate([cos, cos], axis=0).astype(ml_dtypes.bfloat16)
    c["c_rope_sin"] = np.concatenate([sin, sin], axis=0).astype(ml_dtypes.bfloat16)
    R = np.zeros((128, 128), dtype=np.float32)
    for h0 in (0, 64):
        for d_ in range(32):
            R[h0 + d_, h0 + 32 + d_] = -1.0
            R[h0 + 32 + d_, h0 + d_] = 1.0
    c["c_R_rot_T"] = R.T.copy().astype(ml_dtypes.bfloat16)
    return c


CONST_SPECS = {
    "c_ident_f32": ([128, 128], F32),
    "c_ident_bf": ([128, 128], BF16),
    "c_ones_row": ([1, 512], F32),
    "c_maskT": ([4, 128, 512], BF16),
    "c_iota8": ([128, 8], F32),
    "c_rope_cos": ([128, S], BF16),
    "c_rope_sin": ([128, S], BF16),
    "c_R_rot_T": ([128, 128], BF16),
}


def emit_prep_T(sch, stg2, src_bf, dst_T, nrow, ncol, cast_ld, state):
    pend = state.get("pend")
    for cc in range(ncol // 128):
        for rr in range(0, nrow, 512):
            rw = min(512, nrow - rr)
            par = state["i"] % 2
            ld = sch.op("sync", lambda e, cc=cc, rr=rr, rw=rw, par=par: e.dma_start(
                out=stg2[par][:, 0:rw],
                in_=src_bf[rr:rr + rw, cc * 128:(cc + 1) * 128],
                transpose=True),
                waits=[("st_prep", state.get(par, 0)), ("ld_prep", cast_ld)],
                incs=[("ldt_prep", 16)], nofence=True)
            if pend is not None:
                (pcc, prr, prw, ppar, pld, pdst) = pend
                st = sch.op("gpsimd", lambda e, pcc=pcc, prr=prr, prw=prw, ppar=ppar,
                            pdst=pdst: e.dma_start(
                    out=pdst[pcc * 128:(pcc + 1) * 128, prr:prr + prw],
                    in_=stg2[ppar][:, 0:prw]),
                    waits=[("ldt_prep", pld)], incs=[("st_prep", 16)], nofence=True)
                state[ppar] = st["st_prep"]
            pend = (cc, rr, rw, par, ld["ldt_prep"], dst_T)
            state["i"] += 1
    state["pend"] = pend


def flush_prep_T(sch, stg2, state):
    pend = state.pop("pend", None)
    if pend is not None:
        (pcc, prr, prw, ppar, pld, pdst) = pend
        st = sch.op("gpsimd", lambda e: e.dma_start(
            out=pdst[pcc * 128:(pcc + 1) * 128, prr:prr + prw],
            in_=stg2[ppar][:, 0:prw]),
            waits=[("ldt_prep", pld)], incs=[("st_prep", 16)], nofence=True)
        state[ppar] = st["st_prep"]


def declare_const_dram(nc, names):
    return {name: nc.dram_tensor(name, CONST_SPECS[name][0], CONST_SPECS[name][1],
                                 kind="ExternalInput").ap()
            for name in names}


def np_split_tokens(x):
    return [np.ascontiguousarray(
        x[i * SEQ_PER_CORE:(i + 1) * SEQ_PER_CORE].reshape(-1, D))
        for i in range(NCORES)]


class AttnProgram:

    def __init__(self, nc, layer, sch=None, pre=""):
        self.nc = nc
        self.layer = layer
        self.pre = pre
        self.CONSTS = ("c_ident_f32", "c_maskT") if layer == 0 else (
            "c_ident_f32", "c_maskT", "c_rope_cos", "c_rope_sin", "c_R_rot_T")
        self.sch = sch if sch is not None else Sched(nc)
        self.sb = {}

    def sbuf(self, ctx, name, shape, dt, **kw):
        t = ctx.enter_context(self.nc.sbuf_tensor(self.pre + name, shape, dt, **kw))
        self.sb[name] = t
        return t

    def build(self):
        nc, sch = self.nc, self.sch
        ctx = contextlib.ExitStack()
        cdram = declare_const_dram(nc, self.CONSTS)
        x_in = nc.dram_tensor("x", [SEQ_PER_CORE * S, D], F32, kind="ExternalInput").ap()
        y_out = nc.dram_tensor("y", [SEQ_PER_CORE * S, D], F32, kind="ExternalOutput").ap()
        with ctx:
            block = ctx.enter_context(nc.Block())
            for s_ in ("ld_sync", "ld_gp", "ld_prep", "ldt_prep", "st_prep",
                        "pe", "act", "dve", "pool", "st"):
                sch.add_sem(s_, ctx.enter_context(nc.semaphore(s_)))
            self.prep_stg = [ctx.enter_context(nc.sbuf_tensor(
                self.pre + f"prep_stg{i}", [128, 512], BF16,
                is_transpose_dest=True)) for i in range(2)]
            with contextlib.ExitStack() as pctx:
                self.emit_body(pctx, x_in, y_out, cdram)
                st_total = sch.val("st")
                sch.op("gpsimd", lambda e: e.nop(), waits=[("st", st_total)])
                sch.emit(block)
        return nc

    def declare_w(self):
        if getattr(self, "_declared", False):
            return
        self._declared = True
        nc, pre = self.nc, self.pre
        self.w_dram = {wn: nc.dram_tensor(pre + wn, [D, D], F32,
                                          kind="ExternalInput").ap()
                       for wn in ("Wq", "Wk", "Wv", "Wo")}
        if self.layer == 0:
            self.b_dram = {bn: nc.dram_tensor(pre + bn, [D], F32,
                                              kind="ExternalInput").ap()
                           for bn in ("bq", "bk", "bv", "bo")}
            self.w_src = self.w_dram
        else:
            self.wT_dram = {wn: nc.dram_tensor(pre + "T_" + wn, [D, D], BF16,
                                               kind="Internal").ap()
                            for wn in ("Wq", "Wk", "Wv", "Wo")}
            self.w_bf = {wn: nc.dram_tensor(pre + "bf_" + wn, [D, D], BF16,
                                            kind="Internal").ap()
                         for wn in ("Wq", "Wk", "Wv", "Wo")}
            self.w_src = self.wT_dram

    def emit_prep(self):
        self.declare_w()
        sch = self.sch
        self.prep_emitted = True
        cast_ld = 0
        for wn in ("Wq", "Wk", "Wv", "Wo"):
            cl = sch.op("gpsimd", lambda e, wn=wn: e.dma_start(
                out=self.w_bf[wn], in_=self.w_dram[wn]), incs=[("ld_prep", 16)],
                nofence=True)
            cast_ld = cl["ld_prep"]
        state = {"i": 0, 0: sch.val("st_prep"), 1: sch.val("st_prep")}
        for wn in ("Wq", "Wk", "Wv", "Wo"):
            emit_prep_T(sch, self.prep_stg, self.w_bf[wn], self.wT_dram[wn],
                        D, D, cast_ld, state)
        flush_prep_T(sch, self.prep_stg, state)
        self.prep_st_total = sch.val("st_prep")

    def emit_body(self, ctx, x_in, y_out, cdram):
        nc, sch, sb = self.nc, self.sch, self.sb
        pre = self.pre
        self.declare_w()
        w_dram = self.w_dram
        if self.layer == 0:
            b_dram = self.b_dram
        if True:
            for name in self.CONSTS:
                shape, dt = CONST_SPECS[name]
                dram = cdram[name]
                if name == "c_maskT":
                    t = self.sbuf(ctx, name + "_sb", [128, 4 * 512], dt)
                    self.sb[name] = t
                    for r in range(4):
                        sch.op("sync", lambda e, t=t, r=r, dram=dram: e.dma_start(
                            out=t[:, r * 512:(r + 1) * 512], in_=dram[r]),
                            incs=[("ld_sync", 16)])
                else:
                    t = self.sbuf(ctx, name + "_sb", list(shape), dt)
                    self.sb[name] = t
                    sch.op("sync", lambda e, t=t, dram=dram: e.dma_start(
                        out=t[:], in_=dram), incs=[("ld_sync", 16)])

            ones_col = self.sbuf(ctx, "ones_col", [128, 1], F32)
            onesr = self.sbuf(ctx, "onesr", [1, 128], F32)
            sch.op("vector", lambda e: e.memset(ones_col[:], 1.0), incs=[("dve", 1)])
            sch.op("vector", lambda e: e.memset(onesr[:], 1.0), incs=[("dve", 1)])

            x_res = [self.sbuf(ctx, f"x_res{c}", [128, S], F32) for c in range(8)]
            xn = [self.sbuf(ctx, f"xn{c}", [128, S], BF16) for c in range(8)]
            K_fm = [self.sbuf(ctx, f"K{c}", [128, S], BF16) for c in range(8)]
            V_tm = self.sbuf(ctx, "V", [128, 16 * NH * 65], BF16)
            Qb = self.sbuf(ctx, "Qb", [128, 8 * 512], BF16, is_transpose_dest=True)
            ctxf = [self.sbuf(ctx, f"ctx{c}", [128, 512], BF16) for c in range(8)]
            npt = 4 if self.layer == 0 else 2
            PT = [self.sbuf(ctx, f"PT{p}", [128, 512], BF16) for p in range(npt)]
            PT = [PT[p % npt] for p in range(4)]
            x2 = self.sbuf(ctx, "ln_x2", [128, 1024], F32)
            if self.layer == 0:
                rb_t = self.sbuf(ctx, "rb", [64, 512], F32)
                rb = rb_t[:, :]
            else:
                rb = x2[0:64, 512:1024]
            rows = self.sbuf(ctx, "ln_rows", [1, 1536], F32)
            w_och = [self.sbuf(ctx, f"w_och{p}", [128, 8, 128], BF16) for p in range(2)]
            if self.layer == 0:
                bias_col = {bn: self.sbuf(ctx, f"{bn}_col", [128, 8], F32)
                            for bn in ("bq", "bk", "bo")}
                bvb = self.sbuf(ctx, "bvb", [128, D], BF16)
            else:
                bias_col = None
                bvb = None
            brow = rows
            stage = x2

            ps_stats = ctx.enter_context(nc.psum_tensor(self.pre + "ps_stats", [33, 512], F32))
            ps_mm = [ctx.enter_context(nc.psum_tensor(self.pre + f"ps_mm{i}", [128, 512], F32))
                     for i in range(2)]
            ps_S = [ctx.enter_context(nc.psum_tensor(self.pre + f"ps_S{i}", [128, 512], F32))
                    for i in range(2)]
            ps_ctx = ctx.enter_context(nc.psum_tensor(self.pre + "ps_ctx", [128, 512], F32))
            ps_t = ctx.enter_context(nc.psum_tensor(self.pre + "ps_t", [128, 512], F32))
            ps_t2 = ctx.enter_context(nc.psum_tensor(self.pre + "ps_t2", [128, 512], F32))
            self.ps_tp = [ps_t, ps_t2]
            self.ps_mm, self.ps_S, self.ps_ctx, self.ps_t = ps_mm, ps_S, ps_ctx, ps_t
            self._stats = ps_stats

            if self.layer == 0:
                for bn in ("bq", "bk", "bo"):
                    for c in range(8):
                        sch.op("sync", lambda e, bn=bn, c=c: e.dma_start(
                            out=bias_col[bn][:, c:c + 1],
                            in_=b_dram[bn][c * 128:(c + 1) * 128]
                            .rearrange("(p o) -> p o", o=1)),
                            incs=[("ld_sync", 16)])
                sch.fence()
                sch.op("vector", lambda e: e.tensor_scalar_mul(
                    bias_col["bq"][:], bias_col["bq"][:], 0.125), incs=[("dve", 1)])
                for half in range(2):
                    ldh = sch.op("sync", lambda e, half=half: e.dma_start(
                        out=brow[0:1, 0:512],
                        in_=b_dram["bv"][half * 512:(half + 1) * 512]
                        .rearrange("(o d) -> o d", o=1)),
                        incs=[("ld_sync", 16)])
                    pe = sch.op("tensor", lambda e: e.matmul(
                        ps_t[:, 0:512], onesr[0:1, :], brow[0:1, 0:512],
                        start=True, stop=True),
                        waits=[("ld_sync", ldh["ld_sync"])], incs=[("pe", 1)])
                    sch.op("scalar", lambda e, half=half: e.copy(
                        bvb[:, half * 512:(half + 1) * 512], ps_t[:, 0:512]),
                        waits=[("pe", pe["pe"])], incs=[("act", 1)])
                    sch.fence()
            else:
                if not getattr(self, "prep_emitted", False):
                    self.emit_prep()
                sch.fence()
                sch.fence_include_val("st_prep", self.prep_st_total)

            for sq in range(SEQ_PER_CORE):
                self.seq(sq, x_in, y_out,
                         x_res, xn, K_fm, V_tm, Qb, ctxf, PT, rows, rb, x2,
                         w_och, bias_col, bvb, stage)


    def ln(self, x_res, xn, t0, W, ps_stats, ps_abc):
        sch, sb = self.sch, self.sb
        ones_col, onesr = sb["ones_col"], sb["onesr"]
        x2, rows = sb["ln_x2"], sb["ln_rows"]
        for c in range(8):
            sch.op("tensor", lambda e, c=c: e.matmul(
                ps_stats[0:1, 0:W], ones_col[:], x_res[c][:, t0:t0 + W],
                start=(c == 0), stop=(c == 7)),
                waits=[("dve", sch.val("dve"))] if c == 0 else (),
                incs=[("pe", 1)] if c == 7 else ())
        pe_entry = sch.val("pe")
        dve_entry = sch.val("dve")
        pe_gate = {}
        for c in range(8):
            half = c % 2
            w = ([("pe", pe_gate[half])] if half in pe_gate
                 else [("pe", pe_entry), ("dve", dve_entry)])
            a = sch.op("scalar", lambda e, c=c, half=half: e.square(
                x2[:, half * 512:half * 512 + W], x_res[c][:, t0:t0 + W]),
                waits=w, incs=[("act", 1)])
            p = sch.op("tensor", lambda e, c=c, half=half: e.matmul(
                ps_stats[32:33, 0:W], ones_col[:], x2[:, half * 512:half * 512 + W],
                start=(c == 0), stop=(c == 7)),
                waits=[("act", a["act"])], incs=[("pe", 1)])
            pe_gate[half] = p["pe"]
        pe_done = sch.val("pe")
        mean_r = rows[0:1, 0:W]
        a_r = rows[0:1, 512:512 + W]
        b_r = rows[0:1, 1024:1024 + W]
        sch.op("vector", lambda e: e.tensor_scalar_mul(
            mean_r, ps_stats[0:1, 0:W], 1.0 / D),
            waits=[("pe", pe_done)], incs=[("dve", 1)])
        sch.op("vector", lambda e: e.tensor_scalar_mul(
            b_r, ps_stats[32:33, 0:W], 1.0 / D), incs=[("dve", 1)])
        sch.op("vector", lambda e: e.scalar_tensor_tensor(
            a_r, mean_r, -1.0, mean_r, ALU.mult, ALU.mult), incs=[("dve", 1)])
        sch.op("vector", lambda e: e.tensor_add(b_r, b_r, a_r), incs=[("dve", 1)])
        v = sch.op("vector", lambda e: e.tensor_scalar_add(b_r, b_r, 1e-5),
                   incs=[("dve", 1)])
        sv = sch.op("scalar", lambda e: e.activation(a_r, b_r, AF.Sqrt),
                    waits=[("dve", v["dve"])], incs=[("act", 1)])
        sch.op("vector", lambda e: e.reciprocal(a_r, a_r),
               waits=[("act", sv["act"])], incs=[("dve", 1)])
        v = sch.op("vector", lambda e: e.scalar_tensor_tensor(
            b_r, mean_r, -1.0, a_r, ALU.mult, ALU.mult), incs=[("dve", 1)])
        dve_rows = v["dve"]
        pa = sch.op("tensor", lambda e: e.matmul(
            ps_abc[0][:, 0:W], onesr[0:1, :], a_r, start=True, stop=True),
            waits=[("dve", dve_rows)], incs=[("pe", 1)])
        pb = sch.op("tensor", lambda e: e.matmul(
            ps_abc[1][:, 0:W], onesr[0:1, :], b_r, start=True, stop=True),
            incs=[("pe", 1)])
        pe_bc = pb["pe"]
        gate = {}
        for c in range(8):
            half = c % 2
            w = [("pe", pe_bc)] + ([("dve", gate[half])] if half in gate else [])
            sch.op("vector", lambda e, c=c, half=half: e.tensor_mul(
                x2[:, half * 512:half * 512 + W], x_res[c][:, t0:t0 + W],
                ps_abc[0][:, 0:W]), waits=w, incs=[("dve", 1)])
            g = sch.op("vector", lambda e, c=c, half=half: e.tensor_tensor(
                xn[c][:, t0:t0 + W], x2[:, half * 512:half * 512 + W],
                ps_abc[1][:, 0:W], ALU.add), incs=[("dve", 1)])
            gate[half] = g["dve"]

    def rope(self, buf, b0, t0, W, evac_act):
        sch, sb = self.sch, self.sb
        x2 = sb["ln_x2"]
        ps = self.ps_tp[0]
        gate = getattr(self, "_rope_gate", 0)
        pe = sch.op("tensor", lambda e: e.matmul(
            ps[:, 0:W], sb["c_R_rot_T"][:], buf[:, b0:b0 + W], start=True, stop=True),
            waits=[("act", evac_act), ("dve", gate)], incs=[("pe", 1)])
        sch.op("vector", lambda e: e.tensor_mul(
            x2[:, 0:W], buf[:, b0:b0 + W], sb["c_rope_cos"][:, t0:t0 + W]),
            waits=[("act", evac_act)], incs=[("dve", 1)])
        sch.op("vector", lambda e: e.tensor_mul(
            x2[:, 512:512 + W], ps[:, 0:W], sb["c_rope_sin"][:, t0:t0 + W]),
            waits=[("pe", pe["pe"])], incs=[("dve", 1)])
        v3 = sch.op("vector", lambda e: e.tensor_add(
            buf[:, b0:b0 + W], x2[:, 0:W], x2[:, 512:512 + W]),
            waits=[("pe", pe["pe"])], incs=[("dve", 1)])
        self._rope_gate = v3["dve"]

    def seq(self, sq, x_in, y_out,
            x_res, xn, K_fm, V_tm, Qb, ctxf, PT, rows, rb, x2,
            w_och, bias_col, bvb, stage):
        nc, sch, sb = self.nc, self.sch, self.sb
        ps_mm, ps_S, ps_ctx, ps_t = self.ps_mm, self.ps_S, self.ps_ctx, self.ps_t
        base = sq * S
        ident = sb["c_ident_f32"]
        rrow = rows
        lay = self.layer

        sch.fence()
        pe_after_tt = 0
        copy_done = {0: 0, 1: 0}
        for tt in range(16):
            ld = sch.op("sync", lambda e, tt=tt: e.dma_start(
                out=stage[:], in_=x_in[base + tt * 128: base + (tt + 1) * 128, :]),
                waits=[("pe", pe_after_tt)], incs=[("ld_sync", 16)])
            for c in range(8):
                par = c % 2
                pe = sch.op("tensor", lambda e, c=c, par=par: e.transpose(
                    self.ps_tp[par][:, 0:128], stage[:, c * 128:(c + 1) * 128],
                    ident[:]),
                    waits=[("ld_sync", ld["ld_sync"]), ("dve", copy_done[par])],
                    incs=[("pe", 1)])
                cp = sch.op("vector", lambda e, c=c, tt=tt, par=par: e.tensor_copy(
                    x_res[c][:, tt * 128:(tt + 1) * 128], self.ps_tp[par][:, 0:128]),
                    waits=[("pe", pe["pe"])], incs=[("dve", 1)])
                copy_done[par] = cp["dve"]
            pe_after_tt = sch.val("pe")
        sch.fence()

        for tch in range(4):
            self.ln(x_res, xn, tch * 512, 512, self._stats, ps_S)
        sch.fence()

        def load_w_cols(wname, och, buf, extra_waits=()):
            srcw = self.w_src[wname].rearrange("(c p) o -> p c o", p=128)[
                :, :, och * 128:(och + 1) * 128]
            return sch.op("gpsimd", lambda e, srcw=srcw, buf=buf: e.dma_start(
                out=buf[:, :, 0:128], in_=srcw),
                waits=list(extra_waits), incs=[("ld_gp", 16)])

        prev_w = {0: 0, 1: 0}
        evac_done = {0: 0, 1: 0}
        for och in range(8):
            par = och % 2
            ld = load_w_cols("Wk", och, w_och[par],
                             extra_waits=[("pe", prev_w[par])])
            for tch in range(4):
                mmpar = tch % 2
                for c in range(8):
                    pe = sch.op("tensor", lambda e, tch=tch, c=c, par=par,
                                mmpar=mmpar: e.matmul(
                        ps_mm[mmpar][:], w_och[par][:, c, 0:128],
                        xn[c][:, tch * 512:(tch + 1) * 512],
                        start=(c == 0), stop=(c == 7)),
                        waits=([("ld_gp", ld["ld_gp"]),
                                ("act", evac_done[mmpar])] if c == 0 else ()),
                        incs=[("pe", 1)] if c == 7 else ())
                if lay == 0:
                    sch.op("scalar", lambda e, och=och, tch=tch, mmpar=mmpar:
                           e.activation(
                        K_fm[och][:, tch * 512:(tch + 1) * 512], ps_mm[mmpar][:],
                        AF.Identity, bias=bias_col["bk"][:, och:och + 1], scale=1.0),
                        waits=[("pe", pe["pe"])], incs=[("act", 1)])
                else:
                    ac = sch.op("scalar", lambda e, och=och, tch=tch, mmpar=mmpar:
                                e.copy(K_fm[och][:, tch * 512:(tch + 1) * 512],
                                       ps_mm[mmpar][:]),
                                waits=[("pe", pe["pe"])], incs=[("act", 1)])
                    self.rope(K_fm[och], tch * 512, tch * 512, 512, ac["act"])
                evac_done[mmpar] = sch.val("act")
            prev_w[par] = sch.val("pe")
        sch.fence()

        vevac = {0: 0, 1: 0}
        for och in range(8):
            par = och % 2
            ld = load_w_cols("Wv", och, w_och[par],
                             extra_waits=[("pe", prev_w[par])])
            for tt in range(16):
                mmpar = tt % 2
                for c in range(8):
                    pe = sch.op("tensor", lambda e, tt=tt, c=c, par=par,
                                mmpar=mmpar: e.matmul(
                        ps_mm[mmpar][:, 0:128], xn[c][:, tt * 128:(tt + 1) * 128],
                        w_och[par][:, c, 0:128], start=(c == 0), stop=(c == 7)),
                        waits=([("ld_gp", ld["ld_gp"]),
                                ("dve", vevac[mmpar])] if c == 0 else ()),
                        incs=[("pe", 1)] if c == 7 else ())
                vap = V_tm[:].rearrange("p (t h x) -> p t h x", t=16, h=NH)[
                    :, tt, 2 * och:2 * och + 2, 0:64]
                psap = ps_mm[mmpar][:, 0:128].rearrange("p (h x) -> p h x", h=2)
                if lay == 0:
                    bvap = bvb[:, och * 128:(och + 1) * 128].rearrange(
                        "p (h x) -> p h x", h=2)
                    sch.op("vector", lambda e, vap=vap, psap=psap, bvap=bvap:
                           e.tensor_tensor(vap, psap, bvap, ALU.add),
                           waits=[("pe", pe["pe"])], incs=[("dve", 1)])
                else:
                    sch.op("vector", lambda e, vap=vap, psap=psap: e.tensor_copy(
                        vap, psap),
                        waits=[("pe", pe["pe"])], incs=[("dve", 1)])
                vevac[mmpar] = sch.val("dve")
            prev_w[par] = sch.val("pe")
        onesap = V_tm[:].rearrange("p (t h x) -> p t h x", t=16, h=NH)[:, :, :, 64:65]
        sch.op("vector", lambda e: e.memset(onesap, 1.0), incs=[("dve", 1)])
        sch.fence()

        psS4 = [ps_S[0], ps_S[1], ps_mm[0], ps_mm[1]]
        ps_S_reader = {0: 0, 1: 0, 2: 0, 3: 0}
        PT_reader = {0: 0, 1: 0, 2: 0, 3: 0}
        ctx_reader_pe = 0
        S_reader_pe = 0
        resid_done = {0: 0, 1: 0}
        q_evac = {0: 0, 1: 0}
        for qb in range(4):
            q0 = qb * 512
            for och in range(8):
                par = och % 2
                ld = load_w_cols("Wq", och, w_och[par],
                                 extra_waits=[("pe", prev_w[par])])
                for c in range(8):
                    pe = sch.op("tensor", lambda e, c=c, par=par, q0=q0: e.matmul(
                        ps_mm[par][:], w_och[par][:, c, 0:128],
                        xn[c][:, q0:q0 + 512], start=(c == 0), stop=(c == 7)),
                        waits=([("ld_gp", ld["ld_gp"]),
                                ("act", max(q_evac[par], ps_S_reader[2 + par])),
                                ("dve", resid_done[par])] if c == 0 else ()),
                        incs=[("pe", 1)] if c == 7 else ())
                if lay == 0:
                    qa = sch.op("scalar", lambda e, och=och, par=par: e.activation(
                        Qb[:, och * 512:(och + 1) * 512], ps_mm[par][:],
                        AF.Identity, bias=bias_col["bq"][:, och:och + 1], scale=0.125),
                        waits=[("pe", pe["pe"]), ("pe", S_reader_pe)],
                        incs=[("act", 1)])
                else:
                    qa = sch.op("scalar", lambda e, och=och, par=par: e.activation(
                        Qb[:, och * 512:(och + 1) * 512], ps_mm[par][:],
                        AF.Copy, scale=0.125),
                        waits=[("pe", pe["pe"]), ("pe", S_reader_pe)],
                        incs=[("act", 1)])
                    self.rope(Qb, och * 512, q0, 512, qa["act"])
                q_evac[par] = sch.val("act")
                prev_w[par] = sch.val("pe")
            q_ready = sch.val("act") if lay == 0 else 0
            q_ready_dve = sch.val("dve")
            ps_S_reader[2] = max(ps_S_reader[2], sch.val("act"))
            ps_S_reader[3] = max(ps_S_reader[3], sch.val("act"))

            nkc = (qb + 1) * 4
            ctx_banks = [ps_ctx, ps_t]
            ctx_free = getattr(self, "_ctx_free", {0: 0, 1: 0})
            self._ctx_free = ctx_free
            for h in range(NH):
                chunk, hb = h // 2, 64 * (h % 2)
                cb = ctx_banks[h % 2]
                for kc in range(nkc):
                    par = kc % 4
                    wq = ([("act", q_ready)] if lay == 0 else
                          [("dve", q_ready_dve)])
                    pe = sch.op("tensor", lambda e, kc=kc, chunk=chunk, hb=hb,
                                par=par: e.matmul(
                        psS4[par][:], K_fm[chunk][hb:hb + 64, kc * 128:(kc + 1) * 128],
                        Qb[hb:hb + 64, chunk * 512:(chunk + 1) * 512],
                        start=True, stop=True),
                        waits=[("act", ps_S_reader[par])] + (wq if kc == 0 else [])
                        + ([("dve", resid_done[par - 2])] if par >= 2 else []),
                        incs=[("pe", 1)])
                    waits = [("pe", pe["pe"])]
                    if kc >= nkc - 4:
                        r = kc - (nkc - 4)
                        dv = sch.op("vector", lambda e, par=par, r=r: e.tensor_add(
                            psS4[par][:], psS4[par][:],
                            sb["c_maskT"][:, r * 512:(r + 1) * 512]),
                            waits=waits, incs=[("dve", 1)])
                        waits = [("dve", dv["dve"])]
                    ptp = kc % (4 if self.layer == 0 else 2)
                    ac = sch.op("scalar", lambda e, par=par, ptp=ptp: e.activation(
                        PT[ptp][:], psS4[par][:], AF.Exp),
                        waits=waits + [("pe", PT_reader[ptp])], incs=[("act", 1)])
                    ps_S_reader[par] = ac["act"]
                    pv = sch.op("tensor", lambda e, kc=kc, h=h, ptp=ptp, cb=cb,
                                nkc=nkc: e.matmul(
                        cb[0:65, :],
                        V_tm[:, kc * NH * 65 + h * 65: kc * NH * 65 + (h + 1) * 65],
                        PT[ptp][:], start=(kc == 0), stop=(kc == nkc - 1)),
                        waits=[("act", ac["act"])] + (
                            [("dve", ctx_free[h % 2]), ("pe", ctx_reader_pe)]
                            if kc == 0 else []),
                        incs=[("pe", 1)])
                    PT_reader[ptp] = pv["pe"]
                pv_done = sch.val("pe")
                hr = rrow[0:1, 0:512] if h % 2 == 0 else rrow[0:1, 512:1024]
                dv = sch.op("vector", lambda e, cb=cb, hr=hr: e.reciprocal(
                    hr, cb[64:65, :]),
                    waits=[("pe", pv_done)], incs=[("dve", 1)])
                pe = sch.op("tensor", lambda e, cb=cb, hr=hr: e.matmul(
                    cb[64:128, 0:512], sb["onesr"][0:1, 0:64], hr,
                    start=True, stop=True),
                    waits=[("dve", dv["dve"])], incs=[("pe", 1)])
                hrb = rb[:, 0:512] if h % 2 == 0 else sb["ln_x2"][0:64, 0:512]
                ac = sch.op("scalar", lambda e, cb=cb, hrb=hrb: e.copy(
                    hrb, cb[64:128, 0:512]),
                    waits=[("pe", pe["pe"])], incs=[("act", 1)])
                cf = sch.op("vector", lambda e, chunk=chunk, hb=hb, cb=cb, hrb=hrb:
                            e.tensor_mul(
                    ctxf[chunk][hb:hb + 64, :], cb[0:64, :], hrb),
                    waits=[("act", ac["act"]), ("pe", ctx_reader_pe)],
                    incs=[("dve", 1)])
                ctx_free[h % 2] = cf["dve"]
            S_reader_pe = sch.val("pe")
            ctx_ready = sch.val("dve")

            for och in range(8):
                par = och % 2
                ld = load_w_cols("Wo", och, w_och[par],
                                 extra_waits=[("pe", prev_w[par])])
                for c in range(8):
                    pe = sch.op("tensor", lambda e, c=c, par=par: e.matmul(
                        ps_mm[par][:], w_och[par][:, c, 0:128],
                        ctxf[c][:, 0:512], start=(c == 0), stop=(c == 7)),
                        waits=([("ld_gp", ld["ld_gp"]), ("dve", ctx_ready),
                                ("dve", resid_done[par]),
                                ("act", max(q_evac[par], ps_S_reader[2 + par]))]
                               if c == 0 else ()),
                        incs=[("pe", 1)] if c == 7 else ())
                if lay == 0:
                    sch.op("vector", lambda e, och=och, par=par, q0=q0:
                           e.scalar_tensor_tensor(
                        x_res[och][:, q0:q0 + 512], ps_mm[par][:],
                        bias_col["bo"][:, och:och + 1], x_res[och][:, q0:q0 + 512],
                        ALU.add, ALU.add),
                        waits=[("pe", pe["pe"])], incs=[("dve", 1)])
                else:
                    sch.op("vector", lambda e, och=och, par=par, q0=q0: e.tensor_add(
                        x_res[och][:, q0:q0 + 512], ps_mm[par][:],
                        x_res[och][:, q0:q0 + 512]),
                        waits=[("pe", pe["pe"])], incs=[("dve", 1)])
                resid_done[par] = sch.val("dve")
                prev_w[par] = sch.val("pe")
            ctx_reader_pe = sch.val("pe")

        copy_done = {0: 0, 1: 0}
        for tt in range(16):
            st_before = sch.val("st")
            for c in range(8):
                par = c % 2
                pe = sch.op("tensor", lambda e, c=c, tt=tt, par=par: e.transpose(
                    self.ps_tp[par][:, 0:128], x_res[c][:, tt * 128:(tt + 1) * 128],
                    ident[:]),
                    waits=[("st", st_before), ("dve", copy_done[par])],
                    incs=[("pe", 1)])
                cp = sch.op("vector", lambda e, c=c, par=par: e.tensor_copy(
                    stage[:, c * 128:(c + 1) * 128], self.ps_tp[par][:, 0:128]),
                    waits=[("pe", pe["pe"])], incs=[("dve", 1)])
                copy_done[par] = cp["dve"]
            dv_all = sch.val("dve")
            sch.op("gpsimd", lambda e, tt=tt: e.dma_start(
                out=y_out[base + tt * 128: base + (tt + 1) * 128, :], in_=stage[:]),
                waits=[("dve", dv_all)], incs=[("st", 16)])
            sch.fence()




class L0FfnProgram:

    CONSTS = ("c_ident_f32", "c_ident_bf", "c_iota8", "c_ones_row")

    def __init__(self, nc, sch=None, pre=""):
        self.nc = nc
        self.pre = pre
        self.sch = sch if sch is not None else Sched(nc)
        self.sb = {}

    def sbuf(self, ctx, name, shape, dt, **kw):
        t = ctx.enter_context(self.nc.sbuf_tensor(self.pre + name, shape, dt, **kw))
        self.sb[name] = t
        return t

    def build(self):
        nc, sch = self.nc, self.sch
        ctx = contextlib.ExitStack()
        cdram = declare_const_dram(nc, self.CONSTS)
        x_in = nc.dram_tensor("x", [SEQ_PER_CORE * S, D], F32, kind="ExternalInput").ap()
        y_out = nc.dram_tensor("y", [SEQ_PER_CORE * S, D], F32, kind="ExternalOutput").ap()
        with ctx:
            block = ctx.enter_context(nc.Block())
            for s_ in ("ld_sync", "ld_gp", "ld_prep", "ldt_prep", "st_prep",
                        "pe", "act", "dve", "pool", "st"):
                sch.add_sem(s_, ctx.enter_context(nc.semaphore(s_)))
            self.prep_stg = [ctx.enter_context(nc.sbuf_tensor(
                self.pre + f"prep_stg{i}", [128, 512], BF16,
                is_transpose_dest=True)) for i in range(2)]
            with contextlib.ExitStack() as pctx:
                self.emit_body(pctx, x_in, y_out, cdram)
                st_total = sch.val("st")
                sch.op("gpsimd", lambda e: e.nop(), waits=[("st", st_total)])
                sch.emit(block)
        return nc

    def declare_w(self):
        if getattr(self, "_declared", False):
            return
        self._declared = True
        nc, pre = self.nc, self.pre
        d = {}
        d["gate_d"] = nc.dram_tensor(pre + "gate", [8, D], F32, kind="ExternalInput").ap()
        d["cb_d"] = nc.dram_tensor(pre + "corr_bias", [8], F32, kind="ExternalInput").ap()
        d["w1_d"] = nc.dram_tensor(pre + "w1", [8, 512, 512], F32, kind="ExternalInput").ap()
        d["w2_d"] = nc.dram_tensor(pre + "w2", [8, 512, 512], F32, kind="ExternalInput").ap()
        d["fc1_d"] = nc.dram_tensor(pre + "fc1_w", [512, D], F32, kind="ExternalInput").ap()
        d["fc2_d"] = nc.dram_tensor(pre + "fc2_w", [D, 512], F32, kind="ExternalInput").ap()
        d["up_d"] = nc.dram_tensor(pre + "up_w", [4096, D], F32, kind="ExternalInput").ap()
        d["down_d"] = nc.dram_tensor(pre + "down_w", [D, 4096], F32, kind="ExternalInput").ap()
        d["w1_bf"] = nc.dram_tensor(pre + "w1_bf", [8, 512, 512], BF16, kind="Internal").ap()
        d["w2_bf"] = nc.dram_tensor(pre + "w2_bf", [8, 512, 512], BF16, kind="Internal").ap()
        d["fc1_bf"] = nc.dram_tensor(pre + "fc1_bf", [512, D], BF16, kind="Internal").ap()
        d["fc2_bf"] = nc.dram_tensor(pre + "fc2_bf", [D, 512], BF16, kind="Internal").ap()
        d["up_bf"] = nc.dram_tensor(pre + "up_bf", [4096, D], BF16, kind="Internal").ap()
        d["down_bf"] = nc.dram_tensor(pre + "down_bf", [D, 4096], BF16, kind="Internal").ap()
        d["w1T_d"] = nc.dram_tensor(pre + "w1T", [8, 512, 512], BF16, kind="Internal").ap()
        d["w2T_d"] = nc.dram_tensor(pre + "w2T", [8, 512, 512], BF16, kind="Internal").ap()
        d["fc1T_d"] = nc.dram_tensor(pre + "fc1T", [D, 512], BF16, kind="Internal").ap()
        d["fc2T_d"] = nc.dram_tensor(pre + "fc2T", [512, D], BF16, kind="Internal").ap()
        d["upT_d"] = nc.dram_tensor(pre + "upT", [D, 4096], BF16, kind="Internal").ap()
        d["downT_d"] = nc.dram_tensor(pre + "downT", [4096, D], BF16, kind="Internal").ap()
        self.dw = d

    def emit_prep(self):
        self.declare_w()
        sch, d = self.sch, self.dw
        self.prep_emitted = True
        cast_ld = 0
        for (s_, t_) in (("w1_d", "w1_bf"), ("w2_d", "w2_bf"), ("fc1_d", "fc1_bf"),
                         ("fc2_d", "fc2_bf"), ("up_d", "up_bf"), ("down_d", "down_bf")):
            cl = sch.op("gpsimd", lambda e, s_=s_, t_=t_: e.dma_start(
                out=d[t_], in_=d[s_]), incs=[("ld_prep", 16)], nofence=True)
            cast_ld = cl["ld_prep"]
        state = {"i": 0, 0: sch.val("st_prep"), 1: sch.val("st_prep")}
        emit_prep_T(sch, self.prep_stg, d["fc1_bf"], d["fc1T_d"], 512, D, cast_ld, state)
        emit_prep_T(sch, self.prep_stg, d["fc2_bf"], d["fc2T_d"], D, 512, cast_ld, state)
        emit_prep_T(sch, self.prep_stg, d["up_bf"], d["upT_d"], 4096, D, cast_ld, state)
        emit_prep_T(sch, self.prep_stg, d["down_bf"], d["downT_d"], D, 4096, cast_ld, state)
        for e_ in range(8):
            emit_prep_T(sch, self.prep_stg, d["w1_bf"][e_], d["w1T_d"][e_], 512, 512,
                        cast_ld, state)
            emit_prep_T(sch, self.prep_stg, d["w2_bf"][e_], d["w2T_d"][e_], 512, 512,
                        cast_ld, state)
        flush_prep_T(sch, self.prep_stg, state)
        self.prep_st_total = sch.val("st_prep")

    def emit_body(self, ctx, x_in, y_out, cdram):
        nc, sch, sb = self.nc, self.sch, self.sb
        pre = self.pre
        self.declare_w()
        d = self.dw
        gate_d, cb_d = d["gate_d"], d["cb_d"]
        w1T_d, w2T_d = d["w1T_d"], d["w2T_d"]
        fc1T_d, fc2T_d, upT_d, downT_d = (d["fc1T_d"], d["fc2T_d"],
                                          d["upT_d"], d["downT_d"])
        self.dbg = nc.dram_tensor(pre + "dbg", [128, 4096], F32,
                                  kind="Internal").ap()

        if True:
            for name in self.CONSTS:
                shape, dt = CONST_SPECS[name]
                dram = cdram[name]
                t = self.sbuf(ctx, name + "_sb", list(shape), dt)
                self.sb[name] = t
                sch.op("sync", lambda e, t=t, dram=dram: e.dma_start(
                    out=t[:], in_=dram), incs=[("ld_sync", 16)])

            ones_col = self.sbuf(ctx, "ones_col", [128, 1], F32)
            onesr = self.sbuf(ctx, "onesr", [1, 128], F32)
            sch.op("vector", lambda e: e.memset(ones_col[:], 1.0), incs=[("dve", 1)])
            sch.op("vector", lambda e: e.memset(onesr[:], 1.0), incs=[("dve", 1)])

            x_res = [self.sbuf(ctx, f"x_res{c}", [128, S], F32) for c in range(8)]
            xn = [self.sbuf(ctx, f"xn{c}", [128, S], BF16) for c in range(8)]
            fc1T = self.sbuf(ctx, "fc1T_sb", [128, 8, 512], BF16)
            fc2T = self.sbuf(ctx, "fc2T_sb", [128, 4, 1024], BF16)
            w1e = [self.sbuf(ctx, f"w1e{p}", [128, 4, 512], BF16) for p in range(2)]
            w2e = [self.sbuf(ctx, f"w2e{p}", [128, 4, 512], BF16) for p in range(2)]
            dnT = [self.sbuf(ctx, f"dnT{p}", [128, 512], BF16) for p in range(2)]
            upw = [self.sbuf(ctx, f"upw{p}", [128, 8, 128], BF16) for p in range(2)]
            h_up = self.sbuf(ctx, "h_up", [128, 32, 512], BF16)
            lat = self.sbuf(ctx, "lat", [128, 4, 512], BF16)
            h1 = self.sbuf(ctx, "h1", [128, 4, 512], BF16)
            comb = self.sbuf(ctx, "comb", [128, 4, 512], F32)
            comb_fm = self.sbuf(ctx, "comb_fm", [128, 4, 512], BF16)
            cw = self.sbuf(ctx, "cw", [128, 4, 8], F32)
            gateT = self.sbuf(ctx, "gateT", [128, 8, 8], BF16)
            cbb = self.sbuf(ctx, "cbb", [128, 8], F32)
            rt = self.sbuf(ctx, "rt", [128, 64], F32)
            rti = self.sbuf(ctx, "rti", [128, 8], mybir.dt.uint32)
            gate_sb = self.sbuf(ctx, "gate_sb", [8, D], F32)
            x2 = self.sbuf(ctx, "ln_x2", [128, 1024], F32)
            rows = self.sbuf(ctx, "ln_rows", [1, 1536], F32)
            stage = x2

            ps = [ctx.enter_context(nc.psum_tensor(self.pre + f"psb{i}", [128, 512], F32))
                  for i in range(6)]
            ps_stats = ctx.enter_context(nc.psum_tensor(self.pre + "ps_stats", [33, 512], F32))
            ps_small = ctx.enter_context(nc.psum_tensor(self.pre + "ps_small", [128, 512], F32))
            self._stats = ps_stats
            self.ps_tp = [ps[4], ps[5]]

            if not getattr(self, "prep_emitted", False):
                self.emit_prep()
            sch.fence_include_val("st_prep", self.prep_st_total)
            sch.op("sync", lambda e: e.dma_start(out=gate_sb[:], in_=gate_d),
                   incs=[("ld_sync", 16)])
            sch.fence()
            sch.fence_include_val("st_prep", self.prep_st_total)

            for c in range(8):
                sch.op("sync", lambda e, c=c: e.dma_start(
                    out=fc1T[:, c, :],
                    in_=fc1T_d[c * 128:(c + 1) * 128, :]), incs=[("ld_sync", 16)])
            for lc in range(4):
                sch.op("sync", lambda e, lc=lc: e.dma_start(
                    out=fc2T[:, lc, :],
                    in_=fc2T_d[lc * 128:(lc + 1) * 128, :]), incs=[("ld_sync", 16)])
            ld = sch.val("ld_sync")
            for c in range(8):
                pe = sch.op("tensor", lambda e, c=c: e.transpose(
                    ps_small[0:128, 0:8], gate_sb[0:8, c * 128:(c + 1) * 128],
                    sb["c_ident_f32"][0:8, 0:8]),
                    waits=[("ld_sync", ld), ("dve", sch.val("dve"))],
                    incs=[("pe", 1)])
                sch.op("vector", lambda e, c=c: e.tensor_copy(
                    gateT[:, c, :], ps_small[0:128, 0:8]),
                    waits=[("pe", pe["pe"])], incs=[("dve", 1)])
                sch.fence()
            ldc = sch.op("sync", lambda e: e.dma_start(
                out=rows[0:1, 0:8], in_=cb_d.rearrange("(o d) -> o d", o=1)),
                incs=[("ld_sync", 16)])
            pe = sch.op("tensor", lambda e: e.matmul(
                ps_small[:, 0:8], onesr[0:1, :], rows[0:1, 0:8],
                start=True, stop=True),
                waits=[("ld_sync", ldc["ld_sync"])], incs=[("pe", 1)])
            sch.op("scalar", lambda e: e.copy(cbb[:], ps_small[:, 0:8]),
                   waits=[("pe", pe["pe"])], incs=[("act", 1)])
            sch.fence()

            for sq in range(SEQ_PER_CORE):
                self.seq(sq, x_in, y_out, w1T_d, w2T_d, upT_d, downT_d,
                         x_res, xn, fc1T, fc2T, w1e, w2e, dnT, upw,
                         h_up, lat, h1, comb, comb_fm, gateT, cbb, rt, rti,
                         x2, rows, stage, ps, ps_small)


    ln = AttnProgram.ln
    sbuf2 = None

    def seq(self, sq, x_in, y_out, w1T_d, w2T_d, upT_d, downT_d,
            x_res, xn, fc1T, fc2T, w1e, w2e, dnT, upw,
            h_up, lat, h1, comb, comb_fm, gateT, cbb, rt, rti,
            x2, rows, stage, ps, ps_small):
        nc, sch, sb = self.nc, self.sch, self.sb
        base = sq * S
        ident = sb["c_ident_f32"]
        iota8 = sb["c_iota8"]

        sch.fence()
        pe_after_tt = 0
        copy_done = {0: 0, 1: 0}
        for tt in range(16):
            ld = sch.op("sync", lambda e, tt=tt: e.dma_start(
                out=stage[:], in_=x_in[base + tt * 128: base + (tt + 1) * 128, :]),
                waits=[("pe", pe_after_tt)], incs=[("ld_sync", 16)])
            for c in range(8):
                par = c % 2
                pe = sch.op("tensor", lambda e, c=c, par=par: e.transpose(
                    self.ps_tp[par][:, 0:128], stage[:, c * 128:(c + 1) * 128],
                    ident[:]),
                    waits=[("ld_sync", ld["ld_sync"]), ("dve", copy_done[par])],
                    incs=[("pe", 1)])
                cp = sch.op("vector", lambda e, c=c, tt=tt, par=par: e.tensor_copy(
                    x_res[c][:, tt * 128:(tt + 1) * 128], self.ps_tp[par][:, 0:128]),
                    waits=[("pe", pe["pe"])], incs=[("dve", 1)])
                copy_done[par] = cp["dve"]
            pe_after_tt = sch.val("pe")
        sch.fence()

        for tch in range(4):
            self.ln(x_res, xn, tch * 512, 512, self._stats, ps[0:2])
        sch.fence()

        for tb in range(4):
            self.tile(sq, tb, y_out, w1T_d, w2T_d, upT_d, downT_d,
                      x_res, xn, fc1T, fc2T, w1e, w2e, dnT, upw,
                      h_up, lat, h1, comb, comb_fm, gateT, cbb, rt, rti,
                      ps, ps_small)

        copy_done = {0: 0, 1: 0}
        for tt in range(16):
            st_before = sch.val("st")
            for c in range(8):
                par = c % 2
                pe = sch.op("tensor", lambda e, c=c, tt=tt, par=par: e.transpose(
                    self.ps_tp[par][:, 0:128], x_res[c][:, tt * 128:(tt + 1) * 128],
                    ident[:]),
                    waits=[("st", st_before), ("dve", copy_done[par])],
                    incs=[("pe", 1)])
                cp = sch.op("vector", lambda e, c=c, par=par: e.tensor_copy(
                    stage[:, c * 128:(c + 1) * 128], self.ps_tp[par][:, 0:128]),
                    waits=[("pe", pe["pe"])], incs=[("dve", 1)])
                copy_done[par] = cp["dve"]
            dv_all = sch.val("dve")
            sch.op("gpsimd", lambda e, tt=tt: e.dma_start(
                out=y_out[base + tt * 128: base + (tt + 1) * 128, :], in_=stage[:]),
                waits=[("dve", dv_all)], incs=[("st", 16)])
            sch.fence()

    def tile(self, sq, tb, y_out, w1T_d, w2T_d, upT_d, downT_d,
             x_res, xn, fc1T, fc2T, w1e, w2e, dnT, upw,
             h_up, lat, h1, comb, comb_fm, gateT, cbb, rt, rti,
             ps, ps_small):
        sch, sb = self.sch, self.sb
        iota8 = sb["c_iota8"]
        x2 = sb["ln_x2"]
        cw = sb["cw"]
        t0 = tb * 512
        self._h1_evac = {}
        self._comb_rd = {}
        sch.fence()

        for st in range(4):
            for c in range(8):
                sch.op("tensor", lambda e, st=st, c=c: e.matmul(
                    ps_small[:, st * 8:(st + 1) * 8],
                    xn[c][:, t0 + st * 128: t0 + (st + 1) * 128],
                    gateT[:, c, :], start=(c == 0), stop=(c == 7)),
                    incs=[("pe", 1)] if c == 7 else ())
        pe_logits = sch.val("pe")
        combcol_done = []

        def vop(emit, extra=()):
            return sch.op("vector", emit,
                          waits=list(extra),
                          hard_waits=[("dve", sch.val("dve"))],
                          incs=[("dve", 1)])

        for st in range(4):
            sc_ = rt[:, 0:8]
            sfc = rt[:, 8:16]
            msk = rt[:, 16:24]
            t8v = rt[:, 24:32]
            oh1 = rt[:, 32:40]
            oh2 = rt[:, 40:48]
            tmp8 = rt[:, 48:56]
            s1 = rt[:, 56:57]
            s2 = rt[:, 57:58]
            m0 = rt[:, 58:59]
            m1 = rt[:, 59:60]
            w1n = rt[:, 60:61]
            w2n = rt[:, 61:62]
            g0 = rt[:, 62:63]
            g1 = rt[:, 63:64]
            ac = sch.op("scalar", lambda e, st=st: e.activation(
                sc_, ps_small[:, st * 8:(st + 1) * 8], AF.Sigmoid),
                waits=[("pe", pe_logits), ("dve", sch.val("dve"))],
                incs=[("act", 1)])
            vop(lambda e: e.tensor_add(sfc, sc_, cbb[:]),
                extra=[("act", ac["act"])])
            for g_, base_ in ((g0, 8), (g1, 12)):
                a_ = rt[:, base_:base_ + 1]
                b_ = rt[:, base_ + 1:base_ + 2]
                c_ = rt[:, base_ + 2:base_ + 3]
                d_ = rt[:, base_ + 3:base_ + 4]
                mx1 = rt[:, 48:49]
                mn1 = rt[:, 49:50]
                mx2 = rt[:, 50:51]
                mn2 = rt[:, 51:52]
                t1_ = rt[:, 52:53]
                t2_ = rt[:, 53:54]
                vop(lambda e, a_=a_, b_=b_, mx1=mx1: e.tensor_max(mx1, a_, b_))
                vop(lambda e, a_=a_, b_=b_, mn1=mn1: e.tensor_tensor(mn1, a_, b_, ALU.min))
                vop(lambda e, c_=c_, d_=d_, mx2=mx2: e.tensor_max(mx2, c_, d_))
                vop(lambda e, c_=c_, d_=d_, mn2=mn2: e.tensor_tensor(mn2, c_, d_, ALU.min))
                vop(lambda e, mn1=mn1, mx2=mx2, t1_=t1_: e.tensor_max(t1_, mn1, mx2))
                vop(lambda e, mn2=mn2, mx1=mx1, t2_=t2_: e.tensor_max(t2_, mn2, mx1))
                vop(lambda e, mx1=mx1, t1_=t1_: e.tensor_add(t1_, mx1, t1_))
                vop(lambda e, mx2=mx2, t2_=t2_: e.tensor_add(t2_, mx2, t2_))
                vop(lambda e, g_=g_, t1_=t1_, t2_=t2_: e.tensor_max(g_, t1_, t2_))
            vop(lambda e: e.tensor_tensor(m0, g0, g1, ALU.is_ge))
            vop(lambda e: e.tensor_scalar(m1, m0, -1.0, 1.0, ALU.mult, ALU.add))
            vop(lambda e: e.tensor_scalar_mul(rt[:, 16:20], rt[:, 8:12], m0))
            vop(lambda e: e.tensor_scalar_mul(rt[:, 20:24], rt[:, 12:16], m1))
            vop(lambda e: e.max(t8v, msk))
            vop(lambda e: e.max_index(rti[:], t8v, msk))
            vop(lambda e: e.tensor_copy(rt[:, 54:56], rti[:, 0:2]))
            i1 = rt[:, 54:55]
            i2 = rt[:, 55:56]
            vop(lambda e: e.tensor_single_scalar(oh1, iota8[:], i1, ALU.is_equal))
            vop(lambda e: e.tensor_single_scalar(oh2, iota8[:], i2, ALU.is_equal))
            vop(lambda e: e.tensor_mul(tmp8, oh1, sc_))
            vop(lambda e: e.reduce_sum(s1, tmp8, axis=mybir.AxisListType.X))
            vop(lambda e: e.tensor_mul(tmp8, oh2, sc_))
            vop(lambda e: e.reduce_sum(s2, tmp8, axis=mybir.AxisListType.X))
            vop(lambda e: e.tensor_add(w1n, s1, s2))
            vop(lambda e: e.tensor_scalar_add(w1n, w1n, 1e-20))
            vop(lambda e: e.reciprocal(w2n, w1n))
            vop(lambda e: e.tensor_mul(w1n, s1, w2n))
            vop(lambda e: e.tensor_mul(w2n, s2, w2n))
            vop(lambda e: e.tensor_scalar_mul(oh1, oh1, w1n))
            vop(lambda e: e.scalar_tensor_tensor(t8v, oh2, w2n, oh1, ALU.mult, ALU.add))
            vop(lambda e, st=st: e.tensor_copy(cw[:, st, :], t8v))
            combcol_done.append(sch.val("dve"))
            if sq == 0 and tb == 0 and st == 0:
                sch.op("gpsimd", lambda e: e.dma_start(
                    out=self.dbg[:, 2560:2624], in_=rt[:]),
                    waits=[("dve", sch.val("dve"))], incs=[("st", 16)])
                vop(lambda e: e.tensor_copy(rt[:, 48:56], rti[:]),
                    extra=[("st", sch.val("st"))])
                sch.op("gpsimd", lambda e: e.dma_start(
                    out=self.dbg[:, 2624:2632], in_=rt[:, 48:56]),
                    waits=[("dve", sch.val("dve"))], incs=[("st", 16)])
                sch.fence()
        sch.fence()

        lat_evac = {0: 0, 1: 0}
        dve_pre = sch.val("dve")
        for lc in range(4):
            for c in range(8):
                pe = sch.op("tensor", lambda e, lc=lc, c=c: e.matmul(
                    ps[lc % 2][:], fc1T[:, c, lc * 128:(lc + 1) * 128],
                    xn[c][:, t0:t0 + 512], start=(c == 0), stop=(c == 7)),
                    waits=[("dve", dve_pre), ("act", lat_evac[lc % 2])]
                    if c == 0 else (),
                    incs=[("pe", 1)] if c == 7 else ())
            a = sch.op("scalar", lambda e, lc=lc: e.copy(lat[:, lc, :], ps[lc % 2][:]),
                       waits=[("pe", pe["pe"])], incs=[("act", 1)])
            lat_evac[lc % 2] = a["act"]

        lat_done = sch.val("act")
        we_ld = {}
        for e_ in range(8):
            par = e_ % 2
            wld = sch.op("gpsimd", lambda e, e_=e_, par=par: e.dma_start(
                out=w1e[par][:], in_=w1T_d[e_].rearrange("(lc p) i -> p lc i", p=128)),
                waits=[("pe", we_ld.get(par, 0))], incs=[("ld_gp", 16)])
            wld2 = sch.op("gpsimd", lambda e, e_=e_, par=par: e.dma_start(
                out=w2e[par][:], in_=w2T_d[e_].rearrange("(ic p) l -> p ic l", p=128)),
                incs=[("ld_gp", 16)])
            for ic in range(4):
                par23 = 2 + ic % 2
                w0 = ([("ld_gp", wld2["ld_gp"]),
                       ("act", max(lat_done, self._h1_evac.get(par23, 0))),
                       ("dve", self._comb_rd.get(par23, 0))] if ic < 2 else
                      [("act", max(lat_done, self._h1_evac.get(par23, 0))),
                       ("dve", self._comb_rd.get(par23, 0))])
                for lc in range(4):
                    pe = sch.op("tensor", lambda e, ic=ic, lc=lc, par=par,
                                par23=par23: e.matmul(
                        ps[par23][:],
                        w1e[par][:, lc, ic * 128:(ic + 1) * 128],
                        lat[:, lc, :], start=(lc == 0), stop=(lc == 3)),
                        waits=(w0 if lc == 0 else ()),
                        incs=[("pe", 1)] if lc == 3 else ())
                a1 = sch.op("scalar", lambda e, ic=ic, par23=par23: e.activation(
                    x2[:, 0:512], ps[par23][:], AF.Relu),
                    waits=[("pe", pe["pe"])], incs=[("act", 1)])
                a2 = sch.op("scalar", lambda e, ic=ic: e.square(h1[:, ic, :], x2[:, 0:512]),
                            incs=[("act", 1)])
                self._h1_evac[par23] = a2["act"]
            act_h1 = sch.val("act")
            for st in range(4):
                par23 = 2 + st % 2
                for ic in range(4):
                    pe = sch.op("tensor", lambda e, st=st, ic=ic, par=par,
                                par23=par23: e.matmul(
                        ps[par23][:],
                        h1[:, ic, st * 128:(st + 1) * 128],
                        w2e[par][:, ic, :], start=(ic == 0), stop=(ic == 3)),
                        waits=([("act", act_h1),
                                ("dve", self._comb_rd.get(par23, 0)),
                                ("act", self._h1_evac.get(par23, 0))]
                               if ic == 0 else ()),
                        incs=[("pe", 1)] if ic == 3 else ())
                if e_ == 0:
                    cc = sch.op("vector", lambda e, st=st, e_=e_, par23=par23:
                                e.tensor_scalar_mul(
                        comb[:, st, :], ps[par23][:], cw[:, st, e_:e_ + 1]),
                        waits=[("pe", pe["pe"])],
                        hard_waits=[("dve", combcol_done[st])], incs=[("dve", 1)])
                else:
                    cc = sch.op("vector", lambda e, st=st, e_=e_, par23=par23:
                                e.scalar_tensor_tensor(
                        comb[:, st, :], ps[par23][:], cw[:, st, e_:e_ + 1],
                        comb[:, st, :], ALU.mult, ALU.add),
                        waits=[("pe", pe["pe"])], incs=[("dve", 1)])
                self._comb_rd[par23] = cc["dve"]
            we_ld[par] = sch.val("pe")
        sch.fence()

        cp_done = {0: 0, 1: 0}
        for st in range(4):
            for lc in range(4):
                par = lc % 2
                pe = sch.op("tensor", lambda e, st=st, lc=lc, par=par: e.transpose(
                    self.ps_tp[par][:, 0:128],
                    comb[:, st, lc * 128:(lc + 1) * 128], sb["c_ident_f32"][:]),
                    waits=[("dve", max(sch.val("dve") if st == 0 and lc == 0 else 0,
                                       cp_done[par]))],
                    incs=[("pe", 1)])
                cpd = sch.op("vector", lambda e, st=st, lc=lc, par=par: e.tensor_copy(
                    comb_fm[:, lc, st * 128:(st + 1) * 128],
                    self.ps_tp[par][:, 0:128]),
                    waits=[("pe", pe["pe"])], incs=[("dve", 1)])
                cp_done[par] = cpd["dve"]
        sch.fence()

        prev_up = {}
        up_evac = {0: 0, 1: 0}
        for fc in range(32):
            par = fc % 2
            srcw = upT_d.rearrange("(c p) f -> p c f", p=128)[
                :, :, fc * 128:(fc + 1) * 128]
            ldu = sch.op("gpsimd", lambda e, srcw=srcw, par=par: e.dma_start(
                out=upw[par][:], in_=srcw),
                waits=[("pe", prev_up[par])] if par in prev_up else (),
                incs=[("ld_gp", 16)])
            for c in range(8):
                pe = sch.op("tensor", lambda e, fc=fc, c=c, par=par: e.matmul(
                    ps[par][:], upw[par][:, c, :],
                    xn[c][:, t0:t0 + 512], start=(c == 0), stop=(c == 7)),
                    waits=([("ld_gp", ldu["ld_gp"]),
                            ("act", up_evac[par])] if c == 0 else ()),
                    incs=[("pe", 1)] if c == 7 else ())
            a1 = sch.op("scalar", lambda e, fc=fc, par=par: e.activation(
                x2[:, 0:512], ps[par][:], AF.Relu),
                waits=[("pe", pe["pe"])], incs=[("act", 1)])
            a2 = sch.op("scalar", lambda e, fc=fc: e.square(h_up[:, fc, :], x2[:, 0:512]),
                        incs=[("act", 1)])
            up_evac[par] = a2["act"]
            prev_up[par] = sch.val("pe")
        sch.fence()

        prev_dn = {}
        for dhalf in range(2):
            for fc in range(32):
                par = fc % 2
                ldd = sch.op("gpsimd", lambda e, fc=fc, par=par, dhalf=dhalf:
                             e.dma_start(
                    out=dnT[par][:],
                    in_=downT_d[fc * 128:(fc + 1) * 128,
                                dhalf * 512:(dhalf + 1) * 512]),
                    waits=[("pe", prev_dn[par])] if par in prev_dn else (),
                    incs=[("ld_gp", 16)])
                for dq in range(4):
                    sch.op("tensor", lambda e, fc=fc, dq=dq, par=par: e.matmul(
                        ps[dq][:], dnT[par][:, dq * 128:(dq + 1) * 128],
                        h_up[:, fc, :], start=(fc == 0), stop=False),
                        waits=([("ld_gp", ldd["ld_gp"]),
                                ("dve", sch.val("dve"))] if dq == 0 and fc == 0
                               else [("ld_gp", ldd["ld_gp"])] if dq == 0 else ()),
                        incs=[("pe", 1)] if dq == 3 else ())
                prev_dn[par] = sch.val("pe")
            for lc in range(4):
                for dq in range(4):
                    pe = sch.op("tensor", lambda e, lc=lc, dq=dq, dhalf=dhalf:
                                e.matmul(
                        ps[dq][:],
                        fc2T[:, lc, :][:, dhalf * 512 + dq * 128:
                                       dhalf * 512 + (dq + 1) * 128],
                        comb_fm[:, lc, :],
                        start=False, stop=(lc == 3)),
                        incs=[("pe", 1)] if lc == 3 and dq == 3 else ())
            pe_all = sch.val("pe")
            for dq in range(4):
                dch = dhalf * 4 + dq
                sch.op("vector", lambda e, dq=dq, dch=dch: e.tensor_add(
                    x_res[dch][:, t0:t0 + 512], ps[dq][:],
                    x_res[dch][:, t0:t0 + 512]),
                    waits=[("pe", pe_all)], incs=[("dve", 1)])
            sch.fence()
        if sq == 0 and tb == 0:
            sch.op("vector", lambda e: e.tensor_copy(x2[:, 0:512], h_up[:, 0, :]),
                   incs=[("dve", 1)])
            sch.op("gpsimd", lambda e: e.dma_start(
                out=self.dbg[:, 0:512], in_=x2[:, 0:512]),
                waits=[("dve", sch.val("dve"))], incs=[("st", 16)])
            sch.op("vector", lambda e: e.tensor_copy(x2[:, 0:512], lat[:, 0, :]),
                   waits=[("st", sch.val("st"))], incs=[("dve", 1)])
            sch.op("gpsimd", lambda e: e.dma_start(
                out=self.dbg[:, 512:1024], in_=x2[:, 0:512]),
                waits=[("dve", sch.val("dve"))], incs=[("st", 16)])
            sch.op("gpsimd", lambda e: e.dma_start(
                out=self.dbg[:, 1024:1536], in_=comb[:, 0, :]),
                incs=[("st", 16)])
            sch.op("gpsimd", lambda e: e.dma_start(
                out=self.dbg[:, 1536:1544], in_=cw[:, 0, :]),
                incs=[("st", 16)])
            sch.op("vector", lambda e: e.tensor_copy(x2[:, 0:512], xn[0][:, 0:512]),
                   waits=[("st", sch.val("st"))], incs=[("dve", 1)])
            sch.op("gpsimd", lambda e: e.dma_start(
                out=self.dbg[:, 2048:2560], in_=x2[:, 0:512]),
                waits=[("dve", sch.val("dve"))], incs=[("st", 16)])
            sch.fence()




class L1FfnProgram:

    CONSTS = ("c_ident_f32", "c_ident_bf", "c_iota8", "c_ones_row")

    def __init__(self, nc, sch=None, pre=""):
        self.nc = nc
        self.pre = pre
        self.sch = sch if sch is not None else Sched(nc)
        self.sb = {}

    def sbuf(self, ctx, name, shape, dt, **kw):
        t = ctx.enter_context(self.nc.sbuf_tensor(self.pre + name, shape, dt, **kw))
        self.sb[name] = t
        return t

    def build(self):
        nc, sch = self.nc, self.sch
        ctx = contextlib.ExitStack()
        cdram = declare_const_dram(nc, self.CONSTS)
        x_in = nc.dram_tensor("x", [SEQ_PER_CORE * S, D], F32, kind="ExternalInput").ap()
        y_out = nc.dram_tensor("y", [SEQ_PER_CORE * S, D], F32, kind="ExternalOutput").ap()
        with ctx:
            block = ctx.enter_context(nc.Block())
            for s_ in ("ld_sync", "ld_gp", "ld_prep", "ldt_prep", "st_prep",
                        "pe", "act", "dve", "pool", "st"):
                sch.add_sem(s_, ctx.enter_context(nc.semaphore(s_)))
            self.prep_stg = [ctx.enter_context(nc.sbuf_tensor(
                self.pre + f"prep_stg{i}", [128, 512], BF16,
                is_transpose_dest=True)) for i in range(2)]
            with contextlib.ExitStack() as pctx:
                self.emit_body(pctx, x_in, y_out, cdram)
                st_total = sch.val("st")
                sch.op("gpsimd", lambda e: e.nop(), waits=[("st", st_total)])
                sch.emit(block)
        return nc

    def declare_w(self):
        if getattr(self, "_declared", False):
            return
        self._declared = True
        nc, pre = self.nc, self.pre
        self.nw_d = nc.dram_tensor(pre + "norm_w", [D], F32, kind="ExternalInput").ap()
        self.wma_d = nc.dram_tensor(pre + "w_macro", [4, D], F32, kind="ExternalInput").ap()
        self.wmi_d = nc.dram_tensor(pre + "w_micro", [4, 2, D], F32, kind="ExternalInput").ap()
        self.w1_d = nc.dram_tensor(pre + "w1", [8, 2048, D], F32, kind="ExternalInput").ap()
        self.w2_d = nc.dram_tensor(pre + "w2", [8, D, 2048], F32, kind="ExternalInput").ap()
        self.w1_bf = nc.dram_tensor(pre + "w1_bf", [8, 2048, D], BF16, kind="Internal").ap()
        self.w2_bf = nc.dram_tensor(pre + "w2_bf", [8, D, 2048], BF16, kind="Internal").ap()
        self.w1T_d = nc.dram_tensor(pre + "w1T", [8, D, 2048], BF16, kind="Internal").ap()
        self.w2T_d = nc.dram_tensor(pre + "w2T", [8, 2048, D], BF16, kind="Internal").ap()

    def emit_prep_casts(self):
        self.declare_w()
        sch = self.sch
        self.casts_emitted = True
        cast_ld = 0
        for (s_, t_) in ((self.w1_d, self.w1_bf), (self.w2_d, self.w2_bf)):
            for e_ in range(8):
                cl = sch.op("gpsimd", lambda e, s_=s_, t_=t_, e_=e_: e.dma_start(
                    out=t_[e_], in_=s_[e_]), incs=[("ld_prep", 16)], nofence=True)
                cast_ld = cl["ld_prep"]
        self.cast_ld = cast_ld

    def emit_prep_T(self):
        if not getattr(self, "casts_emitted", False):
            self.emit_prep_casts()
        sch = self.sch
        self.prep_emitted = True
        state = {"i": 0, 0: sch.val("st_prep"), 1: sch.val("st_prep")}
        for e_ in range(8):
            emit_prep_T(sch, self.prep_stg, self.w1_bf[e_], self.w1T_d[e_], 2048, D,
                        self.cast_ld, state)
            emit_prep_T(sch, self.prep_stg, self.w2_bf[e_], self.w2T_d[e_], D, 2048,
                        self.cast_ld, state)
        flush_prep_T(sch, self.prep_stg, state)
        self.prep_st_total = sch.val("st_prep")

    def emit_body(self, ctx, x_in, y_out, cdram):
        nc, sch, sb = self.nc, self.sch, self.sb
        pre = self.pre
        self.declare_w()
        nw_d, wma_d, wmi_d = self.nw_d, self.wma_d, self.wmi_d
        w1T_d, w2T_d = self.w1T_d, self.w2T_d

        if True:
            for name in self.CONSTS:
                shape, dt = CONST_SPECS[name]
                dram = cdram[name]
                t = self.sbuf(ctx, name + "_sb", list(shape), dt)
                self.sb[name] = t
                sch.op("sync", lambda e, t=t, dram=dram: e.dma_start(
                    out=t[:], in_=dram), incs=[("ld_sync", 16)])

            ones_col = self.sbuf(ctx, "ones_col", [128, 1], F32)
            onesr = self.sbuf(ctx, "onesr", [1, 128], F32)
            sch.op("vector", lambda e: e.memset(ones_col[:], 1.0), incs=[("dve", 1)])
            sch.op("vector", lambda e: e.memset(onesr[:], 1.0), incs=[("dve", 1)])

            z = self.sbuf(ctx, "z", [128, 16, 1024], F32)
            zn_fm = self.sbuf(ctx, "zn_fm", [128, 8, S], BF16)
            w1e = self.sbuf(ctx, "w1e", [128, 8, 2048], BF16)
            w2e = self.sbuf(ctx, "w2e", [128, 16, 1024], BF16)
            h_fm = self.sbuf(ctx, "h_fm", [128, 16, 512], BF16)
            cw8 = self.sbuf(ctx, "cw8", [128, 16, 8], F32)
            nwb = self.sbuf(ctx, "nwb", [128, 1024], F32)
            wmT = self.sbuf(ctx, "wmT", [128, 8, 12], BF16)
            wm_sb = self.sbuf(ctx, "wm_sb", [12, D], F32)
            rt = self.sbuf(ctx, "rt", [128, 64], F32)
            rti = self.sbuf(ctx, "rti", [128, 8], mybir.dt.uint32)
            xs = self.sbuf(ctx, "xs", [128, 1024], F32)
            x2s = self.sbuf(ctx, "x2s", [128, 1024], F32)
            self.sbuf(ctx, "zn_bf", [128, 1024], BF16)
            rows = self.sbuf(ctx, "ln_rows", [1, 512], F32)

            ps = [ctx.enter_context(nc.psum_tensor(self.pre + f"psb{i}", [128, 512], F32))
                  for i in range(4)]
            ps_h = [ctx.enter_context(nc.psum_tensor(self.pre + f"psh{i}", [128, 512], F32))
                    for i in range(2)]
            ps_bt = [ctx.enter_context(nc.psum_tensor(self.pre + f"psbt{i}", [128, 1024], BF16))
                     for i in range(2)]
            self.ps, self.ps_h, self.ps_bt = ps, ps_h, ps_bt

            if not getattr(self, "prep_emitted", False):
                self.emit_prep_T()
            sch.fence_include_val("st_prep", self.prep_st_total)
            sch.op("sync", lambda e: e.dma_start(
                out=wm_sb[0:4, :], in_=wma_d), incs=[("ld_sync", 16)])
            sch.op("sync", lambda e: e.dma_start(
                out=wm_sb[4:12, :], in_=wmi_d.rearrange("g e d -> (g e) d")),
                incs=[("ld_sync", 16)])
            sch.fence()

            for half in range(2):
                ldh = sch.op("sync", lambda e, half=half: e.dma_start(
                    out=rows[0:1, 0:512],
                    in_=nw_d[half * 512:(half + 1) * 512]
                    .rearrange("(o d) -> o d", o=1)),
                    waits=[("pe", sch.val("pe")), ("act", sch.val("act"))],
                    incs=[("ld_sync", 16)])
                pe = sch.op("tensor", lambda e: e.matmul(
                    ps[0][:], onesr[0:1, :], rows[0:1, 0:512], start=True, stop=True),
                    waits=[("ld_sync", ldh["ld_sync"])], incs=[("pe", 1)])
                sch.op("scalar", lambda e, half=half: e.copy(
                    nwb[:, half * 512:(half + 1) * 512], ps[0][:]),
                    waits=[("pe", pe["pe"])], incs=[("act", 1)])
                sch.fence()
            for c in range(8):
                pe = sch.op("tensor", lambda e, c=c: e.transpose(
                    ps[1][0:128, 0:12], wm_sb[0:12, c * 128:(c + 1) * 128],
                    sb["c_ident_f32"][0:12, 0:12]),
                    waits=[("dve", sch.val("dve"))], incs=[("pe", 1)])
                sch.op("vector", lambda e, c=c: e.tensor_copy(
                    wmT[:, c, :], ps[1][0:128, 0:12]),
                    waits=[("pe", pe["pe"])], incs=[("dve", 1)])
                sch.fence()

            for sq in range(SEQ_PER_CORE):
                self.seq(sq, x_in, y_out, w1T_d, w2T_d)


    def vop(self, emit, extra=()):
        sch = self.sch
        return sch.op("vector", emit, waits=list(extra),
                      hard_waits=[("dve", sch.val("dve"))], incs=[("dve", 1)])

    def seq(self, sq, x_in, y_out, w1T_d, w2T_d):
        nc, sch, sb = self.nc, self.sch, self.sb
        z, zn_fm = sb["z"], sb["zn_fm"]
        xs, x2s, rows = sb["xs"], sb["x2s"], sb["ln_rows"]
        nwb, cw8 = sb["nwb"], sb["cw8"]
        ps, ps_h, ps_bt = self.ps, self.ps_h, self.ps_bt
        base = sq * S

        sch.fence()
        for tt in range(16):
            ld = sch.op("sync", lambda e, tt=tt: e.dma_start(
                out=xs[:], in_=x_in[base + tt * 128: base + (tt + 1) * 128, :]),
                waits=[("act", sch.val("act")), ("dve", sch.val("dve"))],
                incs=[("ld_sync", 16)])
            ac = sch.op("scalar", lambda e: e.square(x2s[:], xs[:]),
                        waits=[("ld_sync", ld["ld_sync"])], incs=[("act", 1)])
            v = self.vop(lambda e: e.reduce_sum(rt_col(sb, 0), xs[:],
                                                axis=mybir.AxisListType.X),
                         extra=[("ld_sync", ld["ld_sync"])])
            v = self.vop(lambda e: e.reduce_sum(rt_col(sb, 1), x2s[:],
                                                axis=mybir.AxisListType.X),
                         extra=[("act", ac["act"])])
            self.vop(lambda e: e.tensor_scalar_mul(rt_col(sb, 0), rt_col(sb, 0), 1.0 / D))
            self.vop(lambda e: e.tensor_scalar_mul(rt_col(sb, 1), rt_col(sb, 1), 1.0 / D))
            self.vop(lambda e: e.scalar_tensor_tensor(
                rt_col(sb, 2), rt_col(sb, 0), -1.0, rt_col(sb, 0), ALU.mult, ALU.mult))
            self.vop(lambda e: e.tensor_add(rt_col(sb, 1), rt_col(sb, 1), rt_col(sb, 2)))
            v = self.vop(lambda e: e.tensor_scalar_add(rt_col(sb, 1), rt_col(sb, 1), 1e-5))
            ac = sch.op("scalar", lambda e: e.activation(
                rt_col(sb, 1), rt_col(sb, 1), AF.Sqrt),
                waits=[("dve", v["dve"])], incs=[("act", 1)])
            self.vop(lambda e: e.reciprocal(rt_col(sb, 1), rt_col(sb, 1)),
                     extra=[("act", ac["act"])])
            v = self.vop(lambda e, tt=tt: e.tensor_scalar(
                z[:, tt, :], xs[:], rt_col(sb, 0), rt_col(sb, 1),
                ALU.subtract, ALU.mult))
        sch.fence()

        for lp in range(2):
            self.loop_iter(sq, lp, w1T_d, w2T_d)

        sch.fence()
        for tt in range(16):
            ld = sch.op("sync", lambda e, tt=tt: e.dma_start(
                out=xs[:], in_=x_in[base + tt * 128: base + (tt + 1) * 128, :]),
                waits=[("st", sch.val("st")), ("dve", sch.val("dve"))],
                incs=[("ld_sync", 16)])
            v = sch.op("vector", lambda e, tt=tt: e.tensor_add(
                xs[:], xs[:], z[:, tt, :]),
                waits=[("ld_sync", ld["ld_sync"])], incs=[("dve", 1)])
            sch.op("gpsimd", lambda e, tt=tt: e.dma_start(
                out=y_out[base + tt * 128: base + (tt + 1) * 128, :], in_=xs[:]),
                waits=[("dve", v["dve"])], incs=[("st", 16)])
            sch.fence()

    def loop_iter(self, sq, lp, w1T_d, w2T_d):
        nc, sch, sb = self.nc, self.sch, self.sb
        z, zn_fm = sb["z"], sb["zn_fm"]
        xs, x2s = sb["xs"], sb["x2s"]
        nwb, cw8, wmT = sb["nwb"], sb["cw8"], sb["wmT"]
        rt, rti, iota8 = sb["rt"], sb["rti"], sb["c_iota8"]
        ps, ps_h, ps_bt = self.ps, self.ps_h, self.ps_bt
        vop = self.vop
        sch.fence()

        tp_done = {0: 0, 1: 0}
        for tt in range(16):
            ac = sch.op("scalar", lambda e, tt=tt: e.square(x2s[:], z[:, tt, :]),
                        waits=[("dve", sch.val("dve"))], incs=[("act", 1)])
            v = self.vop(lambda e: e.reduce_sum(rt_col(sb, 0), x2s[:],
                                                axis=mybir.AxisListType.X),
                         extra=[("act", ac["act"])])
            self.vop(lambda e: e.tensor_scalar_mul(rt_col(sb, 0), rt_col(sb, 0), 1.0 / D))
            v = self.vop(lambda e: e.tensor_scalar_add(rt_col(sb, 0), rt_col(sb, 0), 1e-6))
            ac = sch.op("scalar", lambda e: e.activation(
                rt_col(sb, 0), rt_col(sb, 0), AF.Sqrt),
                waits=[("dve", v["dve"])], incs=[("act", 1)])
            self.vop(lambda e: e.reciprocal(rt_col(sb, 0), rt_col(sb, 0)),
                     extra=[("act", ac["act"])])
            self.vop(lambda e, tt=tt: e.tensor_scalar_mul(
                xs[:], z[:, tt, :], rt_col(sb, 0)))
            v = sch.op("vector", lambda e: e.tensor_mul(x2s[:], xs[:], nwb[:]),
                       incs=[("dve", 1)])
            ac = sch.op("scalar", lambda e: e.copy(sb["zn_bf"][:], x2s[:]),
                        waits=[("dve", v["dve"])], incs=[("act", 1)])
            for c in range(8):
                par = c % 2
                pe = sch.op("tensor", lambda e, c=c, par=par: e.transpose(
                    ps_bt[par][:, 0:128], sb["zn_bf"][:, c * 128:(c + 1) * 128],
                    sb["c_ident_bf"][:]),
                    waits=[("act", ac["act"]), ("dve", tp_done[par])],
                    incs=[("pe", 1)])
                cp = sch.op("vector", lambda e, c=c, tt=tt, par=par: e.tensor_copy(
                    zn_fm[:, c, tt * 128:(tt + 1) * 128], ps_bt[par][:, 0:128]),
                    waits=[("pe", pe["pe"])], incs=[("dve", 1)])
                tp_done[par] = cp["dve"]
        sch.fence()

        self._route_copy = 0
        zn_done = sch.val("dve")
        for tt in range(16):
            for c in range(8):
                pe = sch.op("tensor", lambda e, tt=tt, c=c: e.matmul(
                    ps[3][:, 0:12], zn_fm[:, c, tt * 128:(tt + 1) * 128],
                    wmT[:, c, :], start=(c == 0), stop=(c == 7)),
                    waits=[("dve", zn_done), ("act", self._route_copy)]
                    if c == 0 else (),
                    incs=[("pe", 1)] if c == 7 else ())
            self.route_dance(tt, pe["pe"])
        sch.fence()

        silu_evac = {0: 0, 1: 0}
        z_upd = {0: 0, 1: 0}
        h_last_pe = 0
        y_last_pe = 0
        for e_ in range(8):
            ld1 = sch.op("gpsimd", lambda e, e_=e_: e.dma_start(
                out=sb["w1e"][:], in_=w1T_d[e_].rearrange("(c p) f -> p c f", p=128)),
                waits=[("pe", h_last_pe)], incs=[("ld_gp", 16)])
            ld2 = sch.op("gpsimd", lambda e, e_=e_: e.dma_start(
                out=sb["w2e"][:], in_=w2T_d[e_].rearrange("(c p) d -> p c d", p=128)),
                waits=[("pe", y_last_pe)], incs=[("ld_gp", 16)])
            for tb in range(4):
                t0 = tb * 512
                for fc in range(16):
                    par = fc % 2
                    for c in range(8):
                        pe = sch.op("tensor", lambda e, fc=fc, c=c, par=par, t0=t0,
                                    e_=e_: e.matmul(
                            ps_h[par][:], sb["w1e"][:, c, fc * 128:(fc + 1) * 128],
                            zn_fm[:, c, t0:t0 + 512], start=(c == 0), stop=(c == 7)),
                            waits=([("ld_gp", ld1["ld_gp"]),
                                    ("act", silu_evac[par])] if c == 0 else ()),
                            incs=[("pe", 1)] if c == 7 else ())
                    a = sch.op("scalar", lambda e, fc=fc, par=par: e.activation(
                        sb["h_fm"][:, fc, :], ps_h[par][:], AF.Silu),
                        waits=[("pe", pe["pe"])], incs=[("act", 1)])
                    silu_evac[par] = a["act"]
                act_h = sch.val("act")
                if tb == 3:
                    h_last_pe = sch.val("pe")
                for st in range(4):
                    tt = tb * 4 + st
                    b0 = (st % 2) * 2
                    for fc in range(16):
                        pe = sch.op("tensor", lambda e, fc=fc, st=st, e_=e_, b0=b0:
                                    e.matmul(
                            ps[b0][:], sb["h_fm"][:, fc, st * 128:(st + 1) * 128],
                            sb["w2e"][:, fc, 0:512], start=(fc == 0), stop=(fc == 15)),
                            waits=([("act", act_h), ("ld_gp", ld2["ld_gp"]),
                                    ("dve", z_upd[st % 2])]
                                   if fc == 0 else ()),
                            incs=())
                        pe = sch.op("tensor", lambda e, fc=fc, st=st, e_=e_, b0=b0:
                                    e.matmul(
                            ps[b0 + 1][:], sb["h_fm"][:, fc, st * 128:(st + 1) * 128],
                            sb["w2e"][:, fc, 512:1024], start=(fc == 0), stop=(fc == 15)),
                            incs=[("pe", 1)] if fc == 15 else ())
                    for dh in range(2):
                        zu = sch.op("vector", lambda e, tt=tt, dh=dh, e_=e_, b0=b0:
                                    e.scalar_tensor_tensor(
                            z[:, tt, dh * 512:(dh + 1) * 512], ps[b0 + dh][:],
                            cw8[:, tt, e_:e_ + 1], z[:, tt, dh * 512:(dh + 1) * 512],
                            ALU.mult, ALU.add),
                            waits=[("pe", pe["pe"])], incs=[("dve", 1)])
                    z_upd[st % 2] = zu["dve"]
            y_last_pe = sch.val("pe")
        sch.fence()

    def route_dance(self, tt, pe_logits):
        sch, sb = self.sch, self.sb
        rt, rti, iota8 = sb["rt"], sb["rti"], sb["c_iota8"]
        vop = self.vop
        lg = rt[:, 0:12]
        mac = rt[:, 12:16]
        pad8 = rt[:, 16:24]
        t8v = rt[:, 24:32]
        li = rt[:, 32:36]
        oh = rt[:, 40:48]
        mx = rt[:, 48:49]
        ssum = rt[:, 49:50]
        v1 = rt[:, 50:51]
        v2 = rt[:, 51:52]
        i1 = rt[:, 52:53]
        i2 = rt[:, 53:54]
        e1 = rt[:, 54:55]
        e2 = rt[:, 55:56]
        tmp = rt[:, 56:57]
        ac = sch.op("scalar", lambda e: e.copy(lg, self.ps[3][:, 0:12]),
                    waits=[("pe", pe_logits), ("dve", sch.val("dve"))],
                    incs=[("act", 1)])
        self._route_copy = ac["act"]
        vop(lambda e: e.tensor_max(mx, rt[:, 0:1], rt[:, 1:2]),
            extra=[("act", ac["act"])])
        vop(lambda e: e.tensor_max(tmp, rt[:, 2:3], rt[:, 3:4]))
        vop(lambda e: e.tensor_max(mx, mx, tmp))
        v = vop(lambda e: e.tensor_scalar_mul(mx, mx, -1.0))
        ac = sch.op("scalar", lambda e: e.activation(mac, lg[:, 0:4], AF.Exp,
                                                     bias=mx),
                    waits=[("dve", v["dve"])], incs=[("act", 1)])
        vop(lambda e: e.reduce_sum(ssum, mac, axis=mybir.AxisListType.X),
            extra=[("act", ac["act"])])
        vop(lambda e: e.reciprocal(ssum, ssum))
        vop(lambda e: e.tensor_scalar_mul(mac, mac, ssum))
        vop(lambda e: e.memset(rt[:, 20:24], -1.0))
        vop(lambda e: e.tensor_copy(rt[:, 16:20], mac))
        v = vop(lambda e: e.max(t8v, pad8))
        v = vop(lambda e: e.max_index(rti[:], t8v, pad8))
        vop(lambda e: e.tensor_copy(rt[:, 52:54], rti[:, 0:2]))
        vop(lambda e: e.tensor_copy(v1, t8v[:, 0:1]))
        vop(lambda e: e.tensor_copy(v2, t8v[:, 1:2]))
        vop(lambda e: e.tensor_add(tmp, v1, v2))
        vop(lambda e: e.tensor_scalar_add(tmp, tmp, 1e-9))
        vop(lambda e: e.reciprocal(tmp, tmp))
        vop(lambda e: e.tensor_mul(v1, v1, tmp))
        vop(lambda e: e.tensor_mul(v2, v2, tmp))
        for g in range(4):
            vop(lambda e, g=g: e.tensor_tensor(
                rt[:, 36 + g:37 + g], rt[:, 4 + 2 * g:5 + 2 * g],
                rt[:, 5 + 2 * g:6 + 2 * g], ALU.is_lt))
        vop(lambda e: e.tensor_copy(li, rt[:, 36:40]))
        for (i_, e_out) in ((i1, e1), (i2, e2)):
            vop(lambda e, i_=i_: e.tensor_single_scalar(
                rt[:, 60:64], iota8[:, 0:4], i_, ALU.is_equal))
            vop(lambda e: e.tensor_mul(rt[:, 60:64], rt[:, 60:64], li))
            vop(lambda e, e_out=e_out: e.reduce_sum(
                e_out, rt[:, 60:64], axis=mybir.AxisListType.X))
            vop(lambda e, i_=i_, e_out=e_out: e.scalar_tensor_tensor(
                e_out, i_, 2.0, e_out, ALU.mult, ALU.add))
        vop(lambda e: e.tensor_single_scalar(oh, iota8[:], e1, ALU.is_equal))
        vop(lambda e: e.tensor_scalar_mul(oh, oh, v1))
        vop(lambda e: e.tensor_single_scalar(rt[:, 16:24], iota8[:], e2, ALU.is_equal))
        v = vop(lambda e: e.scalar_tensor_tensor(
            rt[:, 16:24], rt[:, 16:24], v2, oh, ALU.mult, ALU.add))
        vop(lambda e, tt=tt: e.tensor_copy(sb["cw8"][:, tt, :], rt[:, 16:24]))


def rt_col(sb, i):
    return sb["rt"][:, i:i + 1]


_CONSTS = None


def _consts():
    global _CONSTS
    if _CONSTS is None:
        _CONSTS = host_constants()
    return _CONSTS


def run_attn(x, inputs, layer):
    shards = np_split_tokens(x)
    consts = _consts()
    nc = bass.Bass("TRN2", target_bir_lowering=False)
    prog = AttnProgram(nc, layer)
    prog.build()
    pre = "L0_attn_" if layer == 0 else "L1_attn_"
    wnames = ("Wq", "Wk", "Wv", "Wo") + (("bq", "bk", "bv", "bo") if layer == 0 else ())
    in_maps = []
    for i in range(NCORES):
        m = {"x": shards[i]}
        for wn in wnames:
            m[wn] = np.asarray(inputs[pre + wn], dtype=np.float32)
        for cn in prog.CONSTS:
            m[cn] = consts[cn]
        in_maps.append(m)
    res = run_bass_kernel_spmd(nc, in_maps, core_ids=list(range(NCORES)))
    return np.concatenate([r["y"].reshape(SEQ_PER_CORE, S, D) for r in res.results],
                          axis=0)


def run_l0_ffn(x, inputs):
    shards = np_split_tokens(x)
    consts = _consts()
    nc = bass.Bass("TRN2", target_bir_lowering=False)
    prog = L0FfnProgram(nc)
    prog.build()
    in_maps = []
    for i in range(NCORES):
        m = {"x": shards[i]}
        for wn in ("gate", "corr_bias", "w1", "w2", "fc1_w", "fc2_w", "up_w", "down_w"):
            m[wn] = np.asarray(inputs["L0_ffn_" + wn], dtype=np.float32)
        for cn in prog.CONSTS:
            m[cn] = consts[cn]
        in_maps.append(m)
    res = run_bass_kernel_spmd(nc, in_maps, core_ids=list(range(NCORES)))
    run_l0_ffn.dbg = res.results[0].get("dbg")
    return np.concatenate([r["y"].reshape(SEQ_PER_CORE, S, D) for r in res.results],
                          axis=0)


def run_l1_ffn(x, inputs):
    shards = np_split_tokens(x)
    consts = _consts()
    nc = bass.Bass("TRN2", target_bir_lowering=False)
    prog = L1FfnProgram(nc)
    prog.build()
    in_maps = []
    for i in range(NCORES):
        m = {"x": shards[i]}
        for wn in ("norm_w", "w_macro", "w_micro", "w1", "w2"):
            m[wn] = np.asarray(inputs["L1_ffn_" + wn], dtype=np.float32)
        for cn in prog.CONSTS:
            m[cn] = consts[cn]
        in_maps.append(m)
    res = run_bass_kernel_spmd(nc, in_maps, core_ids=list(range(NCORES)))
    return np.concatenate([r["y"].reshape(SEQ_PER_CORE, S, D) for r in res.results],
                          axis=0)


def build_mega(nc):
    sch = Sched(nc)
    all_consts = tuple(CONST_SPECS.keys())
    cdram = declare_const_dram(nc, all_consts)
    x_in = nc.dram_tensor("x", [SEQ_PER_CORE * S, D], F32, kind="ExternalInput").ap()
    y_out = nc.dram_tensor("y", [SEQ_PER_CORE * S, D], F32, kind="ExternalOutput").ap()
    xh = [nc.dram_tensor(f"xh{i}", [SEQ_PER_CORE * S, D], F32, kind="Internal").ap()
          for i in range(2)]
    progs = [AttnProgram(nc, 0, sch=sch, pre="L0_attn_"),
             L0FfnProgram(nc, sch=sch, pre="L0_ffn_"),
             AttnProgram(nc, 1, sch=sch, pre="L1_attn_"),
             L1FfnProgram(nc, sch=sch, pre="L1_ffn_")]
    ios = [(x_in, xh[0]), (xh[0], xh[1]), (xh[1], xh[0]), (xh[0], y_out)]
    ctx = contextlib.ExitStack()
    with ctx:
        block = ctx.enter_context(nc.Block())
        for s_ in ("ld_sync", "ld_gp", "ld_prep", "ldt_prep", "st_prep",
                        "pe", "act", "dve", "pool", "st"):
            sch.add_sem(s_, ctx.enter_context(nc.semaphore(s_)))
        prep_stg = [ctx.enter_context(nc.sbuf_tensor(
            f"prep_stg{i}", [128, 512], BF16, is_transpose_dest=True))
            for i in range(2)]
        for p in progs:
            p.prep_stg = prep_stg
        progs[1].emit_prep()
        progs[2].emit_prep()
        for k, (prog, (i_, o_)) in enumerate(zip(progs, ios)):
            with contextlib.ExitStack() as pctx:
                if k == 1:
                    progs[3].emit_prep_casts()
                if k == 2:
                    progs[3].emit_prep_T()
                prog.emit_body(pctx, i_, o_, cdram)
                sch.fence()
                sch.emit(block)
        st_total = sch.val("st")
        sch.op("gpsimd", lambda e: e.nop(), waits=[("st", st_total)])
        sch.emit(block)
    return nc


MEGA_WNAMES = (
    "L0_attn_Wq", "L0_attn_Wk", "L0_attn_Wv", "L0_attn_Wo",
    "L0_attn_bq", "L0_attn_bk", "L0_attn_bv", "L0_attn_bo",
    "L0_ffn_gate", "L0_ffn_corr_bias", "L0_ffn_w1", "L0_ffn_w2",
    "L0_ffn_fc1_w", "L0_ffn_fc2_w", "L0_ffn_up_w", "L0_ffn_down_w",
    "L1_attn_Wq", "L1_attn_Wk", "L1_attn_Wv", "L1_attn_Wo",
    "L1_ffn_norm_w", "L1_ffn_w_macro", "L1_ffn_w_micro",
    "L1_ffn_w1", "L1_ffn_w2",
)


def kernel(**inputs):
    x = np.asarray(inputs["L0_attn_x"], dtype=np.float32)
    shards = np_split_tokens(x)
    consts = _consts()
    nc = bass.Bass("TRN2", target_bir_lowering=False)
    build_mega(nc)
    in_maps = []
    for i in range(NCORES):
        m = {"x": shards[i]}
        for wn in MEGA_WNAMES:
            m[wn] = np.asarray(inputs[wn], dtype=np.float32)
        m.update(consts)
        in_maps.append(m)
    res = run_bass_kernel_spmd(nc, in_maps, core_ids=list(range(NCORES)))
    return np.concatenate(
        [r["y"].reshape(SEQ_PER_CORE, S, D) for r in res.results],
        axis=0).astype(np.float32)


def kernel_unfused(**inputs):
    x = np.asarray(inputs["L0_attn_x"], dtype=np.float32)
    x = run_attn(x, inputs, 0)
    x = run_l0_ffn(x, inputs)
    x = run_attn(x, inputs, 1)
    x = run_l1_ffn(x, inputs)
    return x.astype(np.float32)


# version 12
# speedup vs baseline: 1.1278x; 1.1110x over previous
import contextlib
import numpy as np
import ml_dtypes
import concourse.bass as bass
import concourse.mybir as mybir
from concourse.bass_utils import run_bass_kernel_spmd

F32 = mybir.dt.float32
BF16 = mybir.dt.bfloat16
AF = mybir.ActivationFunctionType
ALU = mybir.AluOpType

D = 1024
S = 2048
SEQ_PER_CORE = 4
NCORES = 8
NH = 16
DH = 64
NEG = -30000.0


class Sched:

    ENGINES = ("sync", "tensor", "scalar", "vector", "gpsimd")

    def __init__(self, nc):
        self.nc = nc
        self.items = []
        self.counts = {}
        self.sems = {}
        self.fence_waits = {}
        self.emitted_upto = 0
        self.e_waited = {e: {} for e in self.ENGINES}
        self.e_selfcnt = {e: {} for e in self.ENGINES}

    def add_sem(self, name, handle):
        self.sems[name] = handle
        self.counts[name] = 0

    def op(self, engine, emit, waits=(), incs=(), hard_waits=(), nofence=False):
        after = {}
        w = {} if nofence else dict(self.fence_waits)
        for semname, val in waits:
            w[semname] = max(w.get(semname, 0), val)
        hw = {}
        for semname, val in hard_waits:
            hw[semname] = max(hw.get(semname, 0), val)
            w.pop(semname, None) if False else None
        inc_list = []
        for semname, amt in incs:
            self.counts[semname] += amt
            after[semname] = self.counts[semname]
            inc_list.append((semname, amt))
        self.items.append((engine, emit, sorted(w.items()), inc_list,
                           sorted(hw.items())))
        return after

    def val(self, semname):
        return self.counts[semname]

    FENCE_EXEMPT = ("ld_prep", "ldt_prep", "st_prep")

    def fence(self):
        self.fence_waits = {k: v for k, v in self.counts.items()
                            if k not in self.FENCE_EXEMPT}
        for s_, v in getattr(self, "sticky_waits", {}).items():
            self.fence_waits[s_] = max(self.fence_waits.get(s_, 0), v)

    def fence_include_val(self, sem, val):
        if not hasattr(self, "sticky_waits"):
            self.sticky_waits = {}
        self.sticky_waits[sem] = max(self.sticky_waits.get(sem, 0), val)
        self.fence_waits[sem] = max(self.fence_waits.get(sem, 0), val)

    def emit(self, block):
        new_items = self.items[self.emitted_upto:]
        self.emitted_upto = len(self.items)
        per_engine = {e: [] for e in self.ENGINES}
        for it in new_items:
            per_engine[it[0]].append(it)
        sems = self.sems

        def mk_body(items, waited, selfcnt):
            def body(eng):
                for (_e, emit, waits, incs, hard) in items:
                    for semname, val in waits:
                        if val <= waited.get(semname, 0) or val <= selfcnt.get(semname, 0):
                            continue
                        eng.wait_ge(sems[semname], val)
                        waited[semname] = val
                    for semname, val in hard:
                        if val <= waited.get(semname, 0):
                            continue
                        eng.wait_ge(sems[semname], val)
                        waited[semname] = val
                    instr = emit(eng)
                    for semname, amt in incs:
                        if instr is None:
                            eng.sem_inc(sems[semname], amt)
                        else:
                            instr = instr.then_inc(sems[semname], amt)
                        selfcnt[semname] = selfcnt.get(semname, 0) + amt
            return body

        for ename in self.ENGINES:
            if per_engine[ename]:
                getattr(block, ename)(mk_body(
                    per_engine[ename], self.e_waited[ename], self.e_selfcnt[ename]))


def host_constants():
    c = {}
    c["c_ident_f32"] = np.eye(128, dtype=np.float32)
    c["c_ident_bf"] = np.eye(128).astype(ml_dtypes.bfloat16)
    c["c_ones_row"] = np.ones((1, 512), dtype=np.float32)
    mask = np.zeros((4, 128, 512), dtype=np.float32)
    for r in range(4):
        for i in range(128):
            mask[r, i, : min(r * 128 + i, 512)] = NEG
    c["c_maskT"] = mask.astype(ml_dtypes.bfloat16)
    c["c_iota8"] = np.tile(np.arange(8, dtype=np.float32), (128, 1))
    inv_freq = 1.0 / (10000.0 ** (np.arange(0, DH, 2, dtype=np.float32) / DH))
    t = np.arange(S, dtype=np.float32)
    freqs = np.outer(t, inv_freq)
    emb = np.concatenate([freqs, freqs], axis=1)
    cos = np.cos(emb).T.astype(np.float32)
    sin = np.sin(emb).T.astype(np.float32)
    c["c_rope_cos"] = np.concatenate([cos, cos], axis=0).astype(ml_dtypes.bfloat16)
    c["c_rope_sin"] = np.concatenate([sin, sin], axis=0).astype(ml_dtypes.bfloat16)
    R = np.zeros((128, 128), dtype=np.float32)
    for h0 in (0, 64):
        for d_ in range(32):
            R[h0 + d_, h0 + 32 + d_] = -1.0
            R[h0 + 32 + d_, h0 + d_] = 1.0
    c["c_R_rot_T"] = R.T.copy().astype(ml_dtypes.bfloat16)
    return c


CONST_SPECS = {
    "c_ident_f32": ([128, 128], F32),
    "c_ident_bf": ([128, 128], BF16),
    "c_ones_row": ([1, 512], F32),
    "c_maskT": ([4, 128, 512], BF16),
    "c_iota8": ([128, 8], F32),
    "c_rope_cos": ([128, S], BF16),
    "c_rope_sin": ([128, S], BF16),
    "c_R_rot_T": ([128, 128], BF16),
}


def emit_prep_T(sch, stg2, src_bf, dst_T, nrow, ncol, cast_ld, state):
    pend = state.get("pend")
    for cc in range(ncol // 128):
        for rr in range(0, nrow, 512):
            rw = min(512, nrow - rr)
            par = state["i"] % 2
            ld = sch.op("sync", lambda e, cc=cc, rr=rr, rw=rw, par=par: e.dma_start(
                out=stg2[par][:, 0:rw],
                in_=src_bf[rr:rr + rw, cc * 128:(cc + 1) * 128],
                transpose=True),
                waits=[("st_prep", state.get(par, 0)), ("ld_prep", cast_ld)],
                incs=[("ldt_prep", 16)], nofence=True)
            if pend is not None:
                (pcc, prr, prw, ppar, pld, pdst) = pend
                st = sch.op("gpsimd", lambda e, pcc=pcc, prr=prr, prw=prw, ppar=ppar,
                            pdst=pdst: e.dma_start(
                    out=pdst[pcc * 128:(pcc + 1) * 128, prr:prr + prw],
                    in_=stg2[ppar][:, 0:prw]),
                    waits=[("ldt_prep", pld)], incs=[("st_prep", 16)], nofence=True)
                state[ppar] = st["st_prep"]
            pend = (cc, rr, rw, par, ld["ldt_prep"], dst_T)
            state["i"] += 1
    state["pend"] = pend


def flush_prep_T(sch, stg2, state):
    pend = state.pop("pend", None)
    if pend is not None:
        (pcc, prr, prw, ppar, pld, pdst) = pend
        st = sch.op("gpsimd", lambda e: e.dma_start(
            out=pdst[pcc * 128:(pcc + 1) * 128, prr:prr + prw],
            in_=stg2[ppar][:, 0:prw]),
            waits=[("ldt_prep", pld)], incs=[("st_prep", 16)], nofence=True)
        state[ppar] = st["st_prep"]


def declare_const_dram(nc, names):
    return {name: nc.dram_tensor(name, CONST_SPECS[name][0], CONST_SPECS[name][1],
                                 kind="ExternalInput").ap()
            for name in names}


def np_split_tokens(x):
    return [np.ascontiguousarray(
        x[i * SEQ_PER_CORE:(i + 1) * SEQ_PER_CORE].reshape(-1, D))
        for i in range(NCORES)]


class AttnProgram:

    def __init__(self, nc, layer, sch=None, pre=""):
        self.nc = nc
        self.layer = layer
        self.pre = pre
        self.CONSTS = ("c_ident_f32", "c_maskT") if layer == 0 else (
            "c_ident_f32", "c_maskT", "c_rope_cos", "c_rope_sin", "c_R_rot_T")
        self.sch = sch if sch is not None else Sched(nc)
        self.sb = {}

    def sbuf(self, ctx, name, shape, dt, **kw):
        t = ctx.enter_context(self.nc.sbuf_tensor(self.pre + name, shape, dt, **kw))
        self.sb[name] = t
        return t

    def build(self):
        nc, sch = self.nc, self.sch
        ctx = contextlib.ExitStack()
        cdram = declare_const_dram(nc, self.CONSTS)
        x_in = nc.dram_tensor("x", [SEQ_PER_CORE * S, D], F32, kind="ExternalInput").ap()
        y_out = nc.dram_tensor("y", [SEQ_PER_CORE * S, D], F32, kind="ExternalOutput").ap()
        with ctx:
            block = ctx.enter_context(nc.Block())
            for s_ in ("ld_sync", "ld_gp", "ld_prep", "ldt_prep", "st_prep",
                        "pe", "act", "dve", "pool", "st"):
                sch.add_sem(s_, ctx.enter_context(nc.semaphore(s_)))
            self.prep_stg = [ctx.enter_context(nc.sbuf_tensor(
                self.pre + f"prep_stg{i}", [128, 512], BF16,
                is_transpose_dest=True)) for i in range(2)]
            with contextlib.ExitStack() as pctx:
                self.emit_body(pctx, x_in, y_out, cdram)
                st_total = sch.val("st")
                sch.op("gpsimd", lambda e: e.nop(), waits=[("st", st_total)])
                sch.emit(block)
        return nc

    def declare_w(self):
        if getattr(self, "_declared", False):
            return
        self._declared = True
        nc, pre = self.nc, self.pre
        self.w_dram = {wn: nc.dram_tensor(pre + wn, [D, D], F32,
                                          kind="ExternalInput").ap()
                       for wn in ("Wq", "Wk", "Wv", "Wo")}
        if self.layer == 0:
            self.b_dram = {bn: nc.dram_tensor(pre + bn, [D], F32,
                                              kind="ExternalInput").ap()
                           for bn in ("bq", "bk", "bv", "bo")}
            self.w_src = self.w_dram
        else:
            self.wT_dram = {wn: nc.dram_tensor(pre + "T_" + wn, [D, D], BF16,
                                               kind="Internal").ap()
                            for wn in ("Wq", "Wk", "Wv", "Wo")}
            self.w_bf = {wn: nc.dram_tensor(pre + "bf_" + wn, [D, D], BF16,
                                            kind="Internal").ap()
                         for wn in ("Wq", "Wk", "Wv", "Wo")}
            self.w_src = self.wT_dram

    def emit_prep(self):
        self.declare_w()
        sch = self.sch
        self.prep_emitted = True
        cast_ld = 0
        for wn in ("Wq", "Wk", "Wv", "Wo"):
            cl = sch.op("gpsimd", lambda e, wn=wn: e.dma_start(
                out=self.w_bf[wn], in_=self.w_dram[wn]), incs=[("ld_prep", 16)],
                nofence=True)
            cast_ld = cl["ld_prep"]
        state = {"i": 0, 0: sch.val("st_prep"), 1: sch.val("st_prep")}
        for wn in ("Wq", "Wk", "Wv", "Wo"):
            emit_prep_T(sch, self.prep_stg, self.w_bf[wn], self.wT_dram[wn],
                        D, D, cast_ld, state)
        flush_prep_T(sch, self.prep_stg, state)
        self.prep_st_total = sch.val("st_prep")

    def emit_body(self, ctx, x_in, y_out, cdram):
        nc, sch, sb = self.nc, self.sch, self.sb
        pre = self.pre
        self.declare_w()
        w_dram = self.w_dram
        if self.layer == 0:
            b_dram = self.b_dram
        if True:
            for name in self.CONSTS:
                shape, dt = CONST_SPECS[name]
                dram = cdram[name]
                if name == "c_maskT":
                    t = self.sbuf(ctx, name + "_sb", [128, 4 * 512], dt)
                    self.sb[name] = t
                    for r in range(4):
                        sch.op("sync", lambda e, t=t, r=r, dram=dram: e.dma_start(
                            out=t[:, r * 512:(r + 1) * 512], in_=dram[r]),
                            incs=[("ld_sync", 16)])
                else:
                    t = self.sbuf(ctx, name + "_sb", list(shape), dt)
                    self.sb[name] = t
                    sch.op("sync", lambda e, t=t, dram=dram: e.dma_start(
                        out=t[:], in_=dram), incs=[("ld_sync", 16)])

            ones_col = self.sbuf(ctx, "ones_col", [128, 1], F32)
            onesr = self.sbuf(ctx, "onesr", [1, 128], F32)
            sch.op("vector", lambda e: e.memset(ones_col[:], 1.0), incs=[("dve", 1)])
            sch.op("vector", lambda e: e.memset(onesr[:], 1.0), incs=[("dve", 1)])

            x_res = [self.sbuf(ctx, f"x_res{c}", [128, S], F32) for c in range(8)]
            xn = [self.sbuf(ctx, f"xn{c}", [128, S], BF16) for c in range(8)]
            K_fm = [self.sbuf(ctx, f"K{c}", [128, S], BF16) for c in range(8)]
            V_tm = self.sbuf(ctx, "V", [128, 16 * NH * 65], BF16)
            Qb = self.sbuf(ctx, "Qb", [128, 8 * 512], BF16, is_transpose_dest=True)
            ctxf = [self.sbuf(ctx, f"ctx{c}", [128, 512], BF16) for c in range(8)]
            npt = 4 if self.layer == 0 else 2
            PT = [self.sbuf(ctx, f"PT{p}", [128, 512], BF16) for p in range(npt)]
            PT = [PT[p % npt] for p in range(4)]
            x2 = self.sbuf(ctx, "ln_x2", [128, 1024], F32)
            if self.layer == 0:
                rb_t = self.sbuf(ctx, "rb", [64, 512], F32)
                rb = rb_t[:, :]
            else:
                rb = x2[0:64, 512:1024]
            rows = self.sbuf(ctx, "ln_rows", [1, 1536], F32)
            w_och = [self.sbuf(ctx, f"w_och{p}", [128, 8, 128], BF16) for p in range(2)]
            if self.layer == 0:
                bias_col = {bn: self.sbuf(ctx, f"{bn}_col", [128, 8], F32)
                            for bn in ("bq", "bk", "bo")}
                bvb = self.sbuf(ctx, "bvb", [128, D], BF16)
            else:
                bias_col = None
                bvb = None
            brow = rows
            stage = x2

            ps_stats = ctx.enter_context(nc.psum_tensor(self.pre + "ps_stats", [33, 512], F32))
            ps_mm = [ctx.enter_context(nc.psum_tensor(self.pre + f"ps_mm{i}", [128, 512], F32))
                     for i in range(2)]
            ps_S = [ctx.enter_context(nc.psum_tensor(self.pre + f"ps_S{i}", [128, 512], F32))
                    for i in range(2)]
            ps_ctx = ctx.enter_context(nc.psum_tensor(self.pre + "ps_ctx", [128, 512], F32))
            ps_t = ctx.enter_context(nc.psum_tensor(self.pre + "ps_t", [128, 512], F32))
            ps_t2 = ctx.enter_context(nc.psum_tensor(self.pre + "ps_t2", [128, 512], F32))
            self.ps_tp = [ps_t, ps_t2]
            self.ps_mm, self.ps_S, self.ps_ctx, self.ps_t = ps_mm, ps_S, ps_ctx, ps_t
            self._stats = ps_stats

            if self.layer == 0:
                for bn in ("bq", "bk", "bo"):
                    for c in range(8):
                        sch.op("sync", lambda e, bn=bn, c=c: e.dma_start(
                            out=bias_col[bn][:, c:c + 1],
                            in_=b_dram[bn][c * 128:(c + 1) * 128]
                            .rearrange("(p o) -> p o", o=1)),
                            incs=[("ld_sync", 16)])
                sch.fence()
                sch.op("vector", lambda e: e.tensor_scalar_mul(
                    bias_col["bq"][:], bias_col["bq"][:], 0.125), incs=[("dve", 1)])
                for half in range(2):
                    ldh = sch.op("sync", lambda e, half=half: e.dma_start(
                        out=brow[0:1, 0:512],
                        in_=b_dram["bv"][half * 512:(half + 1) * 512]
                        .rearrange("(o d) -> o d", o=1)),
                        incs=[("ld_sync", 16)])
                    pe = sch.op("tensor", lambda e: e.matmul(
                        ps_t[:, 0:512], onesr[0:1, :], brow[0:1, 0:512],
                        start=True, stop=True),
                        waits=[("ld_sync", ldh["ld_sync"])], incs=[("pe", 1)])
                    sch.op("scalar", lambda e, half=half: e.copy(
                        bvb[:, half * 512:(half + 1) * 512], ps_t[:, 0:512]),
                        waits=[("pe", pe["pe"])], incs=[("act", 1)])
                    sch.fence()
            else:
                if not getattr(self, "prep_emitted", False):
                    self.emit_prep()
                sch.fence()
                sch.fence_include_val("st_prep", self.prep_st_total)

            for sq in range(SEQ_PER_CORE):
                self.seq(sq, x_in, y_out,
                         x_res, xn, K_fm, V_tm, Qb, ctxf, PT, rows, rb, x2,
                         w_och, bias_col, bvb, stage)


    def ln(self, x_res, xn, t0, W, ps_stats, ps_abc):
        sch, sb = self.sch, self.sb
        ones_col, onesr = sb["ones_col"], sb["onesr"]
        x2, rows = sb["ln_x2"], sb["ln_rows"]
        for c in range(8):
            sch.op("tensor", lambda e, c=c: e.matmul(
                ps_stats[0:1, 0:W], ones_col[:], x_res[c][:, t0:t0 + W],
                start=(c == 0), stop=(c == 7)),
                waits=[("dve", sch.val("dve"))] if c == 0 else (),
                incs=[("pe", 1)] if c == 7 else ())
        pe_entry = sch.val("pe")
        dve_entry = sch.val("dve")
        pe_gate = {}
        for c in range(8):
            half = c % 2
            w = ([("pe", pe_gate[half])] if half in pe_gate
                 else [("pe", pe_entry), ("dve", dve_entry)])
            a = sch.op("scalar", lambda e, c=c, half=half: e.square(
                x2[:, half * 512:half * 512 + W], x_res[c][:, t0:t0 + W]),
                waits=w, incs=[("act", 1)])
            p = sch.op("tensor", lambda e, c=c, half=half: e.matmul(
                ps_stats[32:33, 0:W], ones_col[:], x2[:, half * 512:half * 512 + W],
                start=(c == 0), stop=(c == 7)),
                waits=[("act", a["act"])], incs=[("pe", 1)])
            pe_gate[half] = p["pe"]
        pe_done = sch.val("pe")
        mean_r = rows[0:1, 0:W]
        a_r = rows[0:1, 512:512 + W]
        b_r = rows[0:1, 1024:1024 + W]
        sch.op("vector", lambda e: e.tensor_scalar_mul(
            mean_r, ps_stats[0:1, 0:W], 1.0 / D),
            waits=[("pe", pe_done)], incs=[("dve", 1)])
        sch.op("vector", lambda e: e.tensor_scalar_mul(
            b_r, ps_stats[32:33, 0:W], 1.0 / D), incs=[("dve", 1)])
        sch.op("vector", lambda e: e.scalar_tensor_tensor(
            a_r, mean_r, -1.0, mean_r, ALU.mult, ALU.mult), incs=[("dve", 1)])
        sch.op("vector", lambda e: e.tensor_add(b_r, b_r, a_r), incs=[("dve", 1)])
        v = sch.op("vector", lambda e: e.tensor_scalar_add(b_r, b_r, 1e-5),
                   incs=[("dve", 1)])
        sv = sch.op("scalar", lambda e: e.activation(a_r, b_r, AF.Sqrt),
                    waits=[("dve", v["dve"])], incs=[("act", 1)])
        sch.op("vector", lambda e: e.reciprocal(a_r, a_r),
               waits=[("act", sv["act"])], incs=[("dve", 1)])
        v = sch.op("vector", lambda e: e.scalar_tensor_tensor(
            b_r, mean_r, -1.0, a_r, ALU.mult, ALU.mult), incs=[("dve", 1)])
        dve_rows = v["dve"]
        pa = sch.op("tensor", lambda e: e.matmul(
            ps_abc[0][:, 0:W], onesr[0:1, :], a_r, start=True, stop=True),
            waits=[("dve", dve_rows)], incs=[("pe", 1)])
        pb = sch.op("tensor", lambda e: e.matmul(
            ps_abc[1][:, 0:W], onesr[0:1, :], b_r, start=True, stop=True),
            incs=[("pe", 1)])
        pe_bc = pb["pe"]
        gate = {}
        for c in range(8):
            half = c % 2
            w = [("pe", pe_bc)] + ([("dve", gate[half])] if half in gate else [])
            sch.op("vector", lambda e, c=c, half=half: e.tensor_mul(
                x2[:, half * 512:half * 512 + W], x_res[c][:, t0:t0 + W],
                ps_abc[0][:, 0:W]), waits=w, incs=[("dve", 1)])
            g = sch.op("vector", lambda e, c=c, half=half: e.tensor_tensor(
                xn[c][:, t0:t0 + W], x2[:, half * 512:half * 512 + W],
                ps_abc[1][:, 0:W], ALU.add), incs=[("dve", 1)])
            gate[half] = g["dve"]

    def rope(self, buf, b0, t0, W, evac_act):
        sch, sb = self.sch, self.sb
        x2 = sb["ln_x2"]
        ps = self.ps_tp[0]
        gate = getattr(self, "_rope_gate", 0)
        pe = sch.op("tensor", lambda e: e.matmul(
            ps[:, 0:W], sb["c_R_rot_T"][:], buf[:, b0:b0 + W], start=True, stop=True),
            waits=[("act", evac_act), ("dve", gate)], incs=[("pe", 1)])
        sch.op("vector", lambda e: e.tensor_mul(
            x2[:, 0:W], buf[:, b0:b0 + W], sb["c_rope_cos"][:, t0:t0 + W]),
            waits=[("act", evac_act)], incs=[("dve", 1)])
        sch.op("vector", lambda e: e.tensor_mul(
            x2[:, 512:512 + W], ps[:, 0:W], sb["c_rope_sin"][:, t0:t0 + W]),
            waits=[("pe", pe["pe"])], incs=[("dve", 1)])
        v3 = sch.op("vector", lambda e: e.tensor_add(
            buf[:, b0:b0 + W], x2[:, 0:W], x2[:, 512:512 + W]),
            waits=[("pe", pe["pe"])], incs=[("dve", 1)])
        self._rope_gate = v3["dve"]

    def seq(self, sq, x_in, y_out,
            x_res, xn, K_fm, V_tm, Qb, ctxf, PT, rows, rb, x2,
            w_och, bias_col, bvb, stage):
        nc, sch, sb = self.nc, self.sch, self.sb
        ps_mm, ps_S, ps_ctx, ps_t = self.ps_mm, self.ps_S, self.ps_ctx, self.ps_t
        base = sq * S
        ident = sb["c_ident_f32"]
        rrow = rows
        lay = self.layer

        sch.fence()
        pe_after_tt = 0
        copy_done = {0: 0, 1: 0}
        for tt in range(16):
            ld = sch.op("sync", lambda e, tt=tt: e.dma_start(
                out=stage[:], in_=x_in[base + tt * 128: base + (tt + 1) * 128, :]),
                waits=[("pe", pe_after_tt)], incs=[("ld_sync", 16)])
            for c in range(8):
                par = c % 2
                pe = sch.op("tensor", lambda e, c=c, par=par: e.transpose(
                    self.ps_tp[par][:, 0:128], stage[:, c * 128:(c + 1) * 128],
                    ident[:]),
                    waits=[("ld_sync", ld["ld_sync"]), ("dve", copy_done[par])],
                    incs=[("pe", 1)])
                cp = sch.op("vector", lambda e, c=c, tt=tt, par=par: e.tensor_copy(
                    x_res[c][:, tt * 128:(tt + 1) * 128], self.ps_tp[par][:, 0:128]),
                    waits=[("pe", pe["pe"])], incs=[("dve", 1)])
                copy_done[par] = cp["dve"]
            pe_after_tt = sch.val("pe")
        sch.fence()

        for tch in range(4):
            self.ln(x_res, xn, tch * 512, 512, self._stats, ps_S)
        sch.fence()

        def load_w_cols(wname, och, buf, extra_waits=()):
            srcw = self.w_src[wname].rearrange("(c p) o -> p c o", p=128)[
                :, :, och * 128:(och + 1) * 128]
            return sch.op("gpsimd", lambda e, srcw=srcw, buf=buf: e.dma_start(
                out=buf[:, :, 0:128], in_=srcw),
                waits=list(extra_waits), incs=[("ld_gp", 16)])

        prev_w = {0: 0, 1: 0}
        evac_done = {0: 0, 1: 0}
        for och in range(8):
            par = och % 2
            ld = load_w_cols("Wk", och, w_och[par],
                             extra_waits=[("pe", prev_w[par])])
            for tch in range(4):
                mmpar = tch % 2
                for c in range(8):
                    pe = sch.op("tensor", lambda e, tch=tch, c=c, par=par,
                                mmpar=mmpar: e.matmul(
                        ps_mm[mmpar][:], w_och[par][:, c, 0:128],
                        xn[c][:, tch * 512:(tch + 1) * 512],
                        start=(c == 0), stop=(c == 7)),
                        waits=([("ld_gp", ld["ld_gp"]),
                                ("act", evac_done[mmpar])] if c == 0 else ()),
                        incs=[("pe", 1)] if c == 7 else ())
                if lay == 0:
                    sch.op("scalar", lambda e, och=och, tch=tch, mmpar=mmpar:
                           e.activation(
                        K_fm[och][:, tch * 512:(tch + 1) * 512], ps_mm[mmpar][:],
                        AF.Identity, bias=bias_col["bk"][:, och:och + 1], scale=1.0),
                        waits=[("pe", pe["pe"])], incs=[("act", 1)])
                else:
                    ac = sch.op("scalar", lambda e, och=och, tch=tch, mmpar=mmpar:
                                e.copy(K_fm[och][:, tch * 512:(tch + 1) * 512],
                                       ps_mm[mmpar][:]),
                                waits=[("pe", pe["pe"])], incs=[("act", 1)])
                    self.rope(K_fm[och], tch * 512, tch * 512, 512, ac["act"])
                evac_done[mmpar] = sch.val("act")
            prev_w[par] = sch.val("pe")
        sch.fence()

        vevac = {0: 0, 1: 0}
        for och in range(8):
            par = och % 2
            ld = load_w_cols("Wv", och, w_och[par],
                             extra_waits=[("pe", prev_w[par])])
            for tt in range(16):
                mmpar = tt % 2
                for c in range(8):
                    pe = sch.op("tensor", lambda e, tt=tt, c=c, par=par,
                                mmpar=mmpar: e.matmul(
                        ps_mm[mmpar][:, 0:128], xn[c][:, tt * 128:(tt + 1) * 128],
                        w_och[par][:, c, 0:128], start=(c == 0), stop=(c == 7)),
                        waits=([("ld_gp", ld["ld_gp"]),
                                ("dve", vevac[mmpar])] if c == 0 else ()),
                        incs=[("pe", 1)] if c == 7 else ())
                vap = V_tm[:].rearrange("p (t h x) -> p t h x", t=16, h=NH)[
                    :, tt, 2 * och:2 * och + 2, 0:64]
                psap = ps_mm[mmpar][:, 0:128].rearrange("p (h x) -> p h x", h=2)
                if lay == 0:
                    bvap = bvb[:, och * 128:(och + 1) * 128].rearrange(
                        "p (h x) -> p h x", h=2)
                    sch.op("vector", lambda e, vap=vap, psap=psap, bvap=bvap:
                           e.tensor_tensor(vap, psap, bvap, ALU.add),
                           waits=[("pe", pe["pe"])], incs=[("dve", 1)])
                else:
                    sch.op("vector", lambda e, vap=vap, psap=psap: e.tensor_copy(
                        vap, psap),
                        waits=[("pe", pe["pe"])], incs=[("dve", 1)])
                vevac[mmpar] = sch.val("dve")
            prev_w[par] = sch.val("pe")
        onesap = V_tm[:].rearrange("p (t h x) -> p t h x", t=16, h=NH)[:, :, :, 64:65]
        sch.op("vector", lambda e: e.memset(onesap, 1.0), incs=[("dve", 1)])
        sch.fence()

        psS4 = [ps_S[0], ps_S[1], ps_mm[0], ps_mm[1]]
        ps_S_reader = {0: 0, 1: 0, 2: 0, 3: 0}
        PT_reader = {0: 0, 1: 0, 2: 0, 3: 0}
        ctx_reader_pe = 0
        S_reader_pe = 0
        resid_done = {0: 0, 1: 0}
        q_evac = {0: 0, 1: 0}
        for qb in range(4):
            q0 = qb * 512
            for och in range(8):
                par = och % 2
                ld = load_w_cols("Wq", och, w_och[par],
                                 extra_waits=[("pe", prev_w[par])])
                for c in range(8):
                    pe = sch.op("tensor", lambda e, c=c, par=par, q0=q0: e.matmul(
                        ps_mm[par][:], w_och[par][:, c, 0:128],
                        xn[c][:, q0:q0 + 512], start=(c == 0), stop=(c == 7)),
                        waits=([("ld_gp", ld["ld_gp"]),
                                ("act", max(q_evac[par], ps_S_reader[2 + par])),
                                ("dve", resid_done[par])] if c == 0 else ()),
                        incs=[("pe", 1)] if c == 7 else ())
                if lay == 0:
                    qa = sch.op("scalar", lambda e, och=och, par=par: e.activation(
                        Qb[:, och * 512:(och + 1) * 512], ps_mm[par][:],
                        AF.Identity, bias=bias_col["bq"][:, och:och + 1], scale=0.125),
                        waits=[("pe", pe["pe"]), ("pe", S_reader_pe)],
                        incs=[("act", 1)])
                else:
                    qa = sch.op("scalar", lambda e, och=och, par=par: e.activation(
                        Qb[:, och * 512:(och + 1) * 512], ps_mm[par][:],
                        AF.Copy, scale=0.125),
                        waits=[("pe", pe["pe"]), ("pe", S_reader_pe)],
                        incs=[("act", 1)])
                    self.rope(Qb, och * 512, q0, 512, qa["act"])
                q_evac[par] = sch.val("act")
                prev_w[par] = sch.val("pe")
            q_ready = sch.val("act") if lay == 0 else 0
            q_ready_dve = sch.val("dve")
            ps_S_reader[2] = max(ps_S_reader[2], sch.val("act"))
            ps_S_reader[3] = max(ps_S_reader[3], sch.val("act"))

            nkc = (qb + 1) * 4
            ctx_banks = [ps_ctx, ps_t]
            ctx_free = getattr(self, "_ctx_free", {0: 0, 1: 0})
            self._ctx_free = ctx_free
            for h in range(NH):
                chunk, hb = h // 2, 64 * (h % 2)
                cb = ctx_banks[h % 2]
                exp_acts = {}
                npt = 4 if self.layer == 0 else 2
                LAG = 2

                def emit_pv(kc):
                    ptp = kc % npt
                    pv = sch.op("tensor", lambda e, kc=kc, h=h, ptp=ptp, cb=cb,
                                nkc=nkc: e.matmul(
                        cb[0:65, :],
                        V_tm[:, kc * NH * 65 + h * 65: kc * NH * 65 + (h + 1) * 65],
                        PT[ptp][:], start=(kc == 0), stop=(kc == nkc - 1)),
                        waits=[("act", exp_acts[kc])] + (
                            [("dve", ctx_free[h % 2]), ("pe", ctx_reader_pe)]
                            if kc == 0 else []),
                        incs=[("pe", 1)])
                    PT_reader[ptp] = pv["pe"]

                for kc in range(nkc):
                    par = kc % 4
                    wq = ([("act", q_ready)] if lay == 0 else
                          [("dve", q_ready_dve)])
                    pe = sch.op("tensor", lambda e, kc=kc, chunk=chunk, hb=hb,
                                par=par: e.matmul(
                        psS4[par][:], K_fm[chunk][hb:hb + 64, kc * 128:(kc + 1) * 128],
                        Qb[hb:hb + 64, chunk * 512:(chunk + 1) * 512],
                        start=True, stop=True),
                        waits=[("act", ps_S_reader[par])] + (wq if kc == 0 else [])
                        + ([("dve", resid_done[par - 2])] if par >= 2 else []),
                        incs=[("pe", 1)])
                    waits = [("pe", pe["pe"])]
                    if kc >= nkc - 4:
                        r = kc - (nkc - 4)
                        dv = sch.op("vector", lambda e, par=par, r=r: e.tensor_add(
                            psS4[par][:], psS4[par][:],
                            sb["c_maskT"][:, r * 512:(r + 1) * 512]),
                            waits=waits, incs=[("dve", 1)])
                        waits = [("dve", dv["dve"])]
                    if kc >= LAG:
                        emit_pv(kc - LAG)
                    ptp = kc % npt
                    ac = sch.op("scalar", lambda e, par=par, ptp=ptp: e.activation(
                        PT[ptp][:], psS4[par][:], AF.Exp),
                        waits=waits + [("pe", PT_reader[ptp])], incs=[("act", 1)])
                    ps_S_reader[par] = ac["act"]
                    exp_acts[kc] = ac["act"]
                for kc in range(max(0, nkc - LAG), nkc):
                    emit_pv(kc)
                pv_done = sch.val("pe")
                hr = rrow[0:1, 0:512] if h % 2 == 0 else rrow[0:1, 512:1024]
                dv = sch.op("vector", lambda e, cb=cb, hr=hr: e.reciprocal(
                    hr, cb[64:65, :]),
                    waits=[("pe", pv_done)], incs=[("dve", 1)])
                pe = sch.op("tensor", lambda e, cb=cb, hr=hr: e.matmul(
                    cb[64:128, 0:512], sb["onesr"][0:1, 0:64], hr,
                    start=True, stop=True),
                    waits=[("dve", dv["dve"])], incs=[("pe", 1)])
                hrb = rb[:, 0:512] if h % 2 == 0 else sb["ln_x2"][0:64, 0:512]
                ac = sch.op("scalar", lambda e, cb=cb, hrb=hrb: e.copy(
                    hrb, cb[64:128, 0:512]),
                    waits=[("pe", pe["pe"])], incs=[("act", 1)])
                cf = sch.op("vector", lambda e, chunk=chunk, hb=hb, cb=cb, hrb=hrb:
                            e.tensor_mul(
                    ctxf[chunk][hb:hb + 64, :], cb[0:64, :], hrb),
                    waits=[("act", ac["act"]), ("pe", ctx_reader_pe)],
                    incs=[("dve", 1)])
                ctx_free[h % 2] = cf["dve"]
            S_reader_pe = sch.val("pe")
            ctx_ready = sch.val("dve")

            for och in range(8):
                par = och % 2
                ld = load_w_cols("Wo", och, w_och[par],
                                 extra_waits=[("pe", prev_w[par])])
                for c in range(8):
                    pe = sch.op("tensor", lambda e, c=c, par=par: e.matmul(
                        ps_mm[par][:], w_och[par][:, c, 0:128],
                        ctxf[c][:, 0:512], start=(c == 0), stop=(c == 7)),
                        waits=([("ld_gp", ld["ld_gp"]), ("dve", ctx_ready),
                                ("dve", resid_done[par]),
                                ("act", max(q_evac[par], ps_S_reader[2 + par]))]
                               if c == 0 else ()),
                        incs=[("pe", 1)] if c == 7 else ())
                if lay == 0:
                    sch.op("vector", lambda e, och=och, par=par, q0=q0:
                           e.scalar_tensor_tensor(
                        x_res[och][:, q0:q0 + 512], ps_mm[par][:],
                        bias_col["bo"][:, och:och + 1], x_res[och][:, q0:q0 + 512],
                        ALU.add, ALU.add),
                        waits=[("pe", pe["pe"])], incs=[("dve", 1)])
                else:
                    sch.op("vector", lambda e, och=och, par=par, q0=q0: e.tensor_add(
                        x_res[och][:, q0:q0 + 512], ps_mm[par][:],
                        x_res[och][:, q0:q0 + 512]),
                        waits=[("pe", pe["pe"])], incs=[("dve", 1)])
                resid_done[par] = sch.val("dve")
                prev_w[par] = sch.val("pe")
            ctx_reader_pe = sch.val("pe")

        copy_done = {0: 0, 1: 0}
        for tt in range(16):
            st_before = sch.val("st")
            for c in range(8):
                par = c % 2
                pe = sch.op("tensor", lambda e, c=c, tt=tt, par=par: e.transpose(
                    self.ps_tp[par][:, 0:128], x_res[c][:, tt * 128:(tt + 1) * 128],
                    ident[:]),
                    waits=[("st", st_before), ("dve", copy_done[par])],
                    incs=[("pe", 1)])
                cp = sch.op("vector", lambda e, c=c, par=par: e.tensor_copy(
                    stage[:, c * 128:(c + 1) * 128], self.ps_tp[par][:, 0:128]),
                    waits=[("pe", pe["pe"])], incs=[("dve", 1)])
                copy_done[par] = cp["dve"]
            dv_all = sch.val("dve")
            sch.op("gpsimd", lambda e, tt=tt: e.dma_start(
                out=y_out[base + tt * 128: base + (tt + 1) * 128, :], in_=stage[:]),
                waits=[("dve", dv_all)], incs=[("st", 16)])
            sch.fence()




class L0FfnProgram:

    CONSTS = ("c_ident_f32", "c_ident_bf", "c_iota8", "c_ones_row")

    def __init__(self, nc, sch=None, pre=""):
        self.nc = nc
        self.pre = pre
        self.sch = sch if sch is not None else Sched(nc)
        self.sb = {}

    def sbuf(self, ctx, name, shape, dt, **kw):
        t = ctx.enter_context(self.nc.sbuf_tensor(self.pre + name, shape, dt, **kw))
        self.sb[name] = t
        return t

    def build(self):
        nc, sch = self.nc, self.sch
        ctx = contextlib.ExitStack()
        cdram = declare_const_dram(nc, self.CONSTS)
        x_in = nc.dram_tensor("x", [SEQ_PER_CORE * S, D], F32, kind="ExternalInput").ap()
        y_out = nc.dram_tensor("y", [SEQ_PER_CORE * S, D], F32, kind="ExternalOutput").ap()
        with ctx:
            block = ctx.enter_context(nc.Block())
            for s_ in ("ld_sync", "ld_gp", "ld_prep", "ldt_prep", "st_prep",
                        "pe", "act", "dve", "pool", "st"):
                sch.add_sem(s_, ctx.enter_context(nc.semaphore(s_)))
            self.prep_stg = [ctx.enter_context(nc.sbuf_tensor(
                self.pre + f"prep_stg{i}", [128, 512], BF16,
                is_transpose_dest=True)) for i in range(2)]
            with contextlib.ExitStack() as pctx:
                self.emit_body(pctx, x_in, y_out, cdram)
                st_total = sch.val("st")
                sch.op("gpsimd", lambda e: e.nop(), waits=[("st", st_total)])
                sch.emit(block)
        return nc

    def declare_w(self):
        if getattr(self, "_declared", False):
            return
        self._declared = True
        nc, pre = self.nc, self.pre
        d = {}
        d["gate_d"] = nc.dram_tensor(pre + "gate", [8, D], F32, kind="ExternalInput").ap()
        d["cb_d"] = nc.dram_tensor(pre + "corr_bias", [8], F32, kind="ExternalInput").ap()
        d["w1_d"] = nc.dram_tensor(pre + "w1", [8, 512, 512], F32, kind="ExternalInput").ap()
        d["w2_d"] = nc.dram_tensor(pre + "w2", [8, 512, 512], F32, kind="ExternalInput").ap()
        d["fc1_d"] = nc.dram_tensor(pre + "fc1_w", [512, D], F32, kind="ExternalInput").ap()
        d["fc2_d"] = nc.dram_tensor(pre + "fc2_w", [D, 512], F32, kind="ExternalInput").ap()
        d["up_d"] = nc.dram_tensor(pre + "up_w", [4096, D], F32, kind="ExternalInput").ap()
        d["down_d"] = nc.dram_tensor(pre + "down_w", [D, 4096], F32, kind="ExternalInput").ap()
        d["w1_bf"] = nc.dram_tensor(pre + "w1_bf", [8, 512, 512], BF16, kind="Internal").ap()
        d["w2_bf"] = nc.dram_tensor(pre + "w2_bf", [8, 512, 512], BF16, kind="Internal").ap()
        d["fc1_bf"] = nc.dram_tensor(pre + "fc1_bf", [512, D], BF16, kind="Internal").ap()
        d["fc2_bf"] = nc.dram_tensor(pre + "fc2_bf", [D, 512], BF16, kind="Internal").ap()
        d["up_bf"] = nc.dram_tensor(pre + "up_bf", [4096, D], BF16, kind="Internal").ap()
        d["down_bf"] = nc.dram_tensor(pre + "down_bf", [D, 4096], BF16, kind="Internal").ap()
        d["w1T_d"] = nc.dram_tensor(pre + "w1T", [8, 512, 512], BF16, kind="Internal").ap()
        d["w2T_d"] = nc.dram_tensor(pre + "w2T", [8, 512, 512], BF16, kind="Internal").ap()
        d["fc1T_d"] = nc.dram_tensor(pre + "fc1T", [D, 512], BF16, kind="Internal").ap()
        d["fc2T_d"] = nc.dram_tensor(pre + "fc2T", [512, D], BF16, kind="Internal").ap()
        d["upT_d"] = nc.dram_tensor(pre + "upT", [D, 4096], BF16, kind="Internal").ap()
        d["downT_d"] = nc.dram_tensor(pre + "downT", [4096, D], BF16, kind="Internal").ap()
        self.dw = d

    def emit_prep(self):
        self.declare_w()
        sch, d = self.sch, self.dw
        self.prep_emitted = True
        cast_ld = 0
        for (s_, t_) in (("w1_d", "w1_bf"), ("w2_d", "w2_bf"), ("fc1_d", "fc1_bf"),
                         ("fc2_d", "fc2_bf"), ("up_d", "up_bf"), ("down_d", "down_bf")):
            cl = sch.op("gpsimd", lambda e, s_=s_, t_=t_: e.dma_start(
                out=d[t_], in_=d[s_]), incs=[("ld_prep", 16)], nofence=True)
            cast_ld = cl["ld_prep"]
        state = {"i": 0, 0: sch.val("st_prep"), 1: sch.val("st_prep")}
        emit_prep_T(sch, self.prep_stg, d["fc1_bf"], d["fc1T_d"], 512, D, cast_ld, state)
        emit_prep_T(sch, self.prep_stg, d["fc2_bf"], d["fc2T_d"], D, 512, cast_ld, state)
        emit_prep_T(sch, self.prep_stg, d["up_bf"], d["upT_d"], 4096, D, cast_ld, state)
        emit_prep_T(sch, self.prep_stg, d["down_bf"], d["downT_d"], D, 4096, cast_ld, state)
        for e_ in range(8):
            emit_prep_T(sch, self.prep_stg, d["w1_bf"][e_], d["w1T_d"][e_], 512, 512,
                        cast_ld, state)
            emit_prep_T(sch, self.prep_stg, d["w2_bf"][e_], d["w2T_d"][e_], 512, 512,
                        cast_ld, state)
        flush_prep_T(sch, self.prep_stg, state)
        self.prep_st_total = sch.val("st_prep")

    def emit_body(self, ctx, x_in, y_out, cdram):
        nc, sch, sb = self.nc, self.sch, self.sb
        pre = self.pre
        self.declare_w()
        d = self.dw
        gate_d, cb_d = d["gate_d"], d["cb_d"]
        w1T_d, w2T_d = d["w1T_d"], d["w2T_d"]
        fc1T_d, fc2T_d, upT_d, downT_d = (d["fc1T_d"], d["fc2T_d"],
                                          d["upT_d"], d["downT_d"])
        self.dbg = nc.dram_tensor(pre + "dbg", [128, 4096], F32,
                                  kind="Internal").ap()

        if True:
            for name in self.CONSTS:
                shape, dt = CONST_SPECS[name]
                dram = cdram[name]
                t = self.sbuf(ctx, name + "_sb", list(shape), dt)
                self.sb[name] = t
                sch.op("sync", lambda e, t=t, dram=dram: e.dma_start(
                    out=t[:], in_=dram), incs=[("ld_sync", 16)])

            ones_col = self.sbuf(ctx, "ones_col", [128, 1], F32)
            onesr = self.sbuf(ctx, "onesr", [1, 128], F32)
            sch.op("vector", lambda e: e.memset(ones_col[:], 1.0), incs=[("dve", 1)])
            sch.op("vector", lambda e: e.memset(onesr[:], 1.0), incs=[("dve", 1)])

            x_res = [self.sbuf(ctx, f"x_res{c}", [128, S], F32) for c in range(8)]
            xn = [self.sbuf(ctx, f"xn{c}", [128, S], BF16) for c in range(8)]
            fc1T = self.sbuf(ctx, "fc1T_sb", [128, 8, 512], BF16)
            fc2T = self.sbuf(ctx, "fc2T_sb", [128, 4, 1024], BF16)
            w1e = [self.sbuf(ctx, f"w1e{p}", [128, 4, 512], BF16) for p in range(2)]
            w2e = [self.sbuf(ctx, f"w2e{p}", [128, 4, 512], BF16) for p in range(2)]
            dnT = [self.sbuf(ctx, f"dnT{p}", [128, 512], BF16) for p in range(2)]
            upw = [self.sbuf(ctx, f"upw{p}", [128, 8, 128], BF16) for p in range(2)]
            h_up = self.sbuf(ctx, "h_up", [128, 32, 512], BF16)
            lat = self.sbuf(ctx, "lat", [128, 4, 512], BF16)
            h1 = self.sbuf(ctx, "h1", [128, 4, 512], BF16)
            comb = self.sbuf(ctx, "comb", [128, 4, 512], F32)
            comb_fm = self.sbuf(ctx, "comb_fm", [128, 4, 512], BF16)
            cw = self.sbuf(ctx, "cw", [128, 4, 8], F32)
            gateT = self.sbuf(ctx, "gateT", [128, 8, 8], BF16)
            cbb = self.sbuf(ctx, "cbb", [128, 8], F32)
            rt = self.sbuf(ctx, "rt", [128, 64], F32)
            rti = self.sbuf(ctx, "rti", [128, 8], mybir.dt.uint32)
            gate_sb = self.sbuf(ctx, "gate_sb", [8, D], F32)
            x2 = self.sbuf(ctx, "ln_x2", [128, 1024], F32)
            rows = self.sbuf(ctx, "ln_rows", [1, 1536], F32)
            stage = x2

            ps = [ctx.enter_context(nc.psum_tensor(self.pre + f"psb{i}", [128, 512], F32))
                  for i in range(6)]
            ps_stats = ctx.enter_context(nc.psum_tensor(self.pre + "ps_stats", [33, 512], F32))
            ps_small = ctx.enter_context(nc.psum_tensor(self.pre + "ps_small", [128, 512], F32))
            self._stats = ps_stats
            self.ps_tp = [ps[4], ps[5]]

            if not getattr(self, "prep_emitted", False):
                self.emit_prep()
            sch.fence_include_val("st_prep", self.prep_st_total)
            sch.op("sync", lambda e: e.dma_start(out=gate_sb[:], in_=gate_d),
                   incs=[("ld_sync", 16)])
            sch.fence()
            sch.fence_include_val("st_prep", self.prep_st_total)

            for c in range(8):
                sch.op("sync", lambda e, c=c: e.dma_start(
                    out=fc1T[:, c, :],
                    in_=fc1T_d[c * 128:(c + 1) * 128, :]), incs=[("ld_sync", 16)])
            for lc in range(4):
                sch.op("sync", lambda e, lc=lc: e.dma_start(
                    out=fc2T[:, lc, :],
                    in_=fc2T_d[lc * 128:(lc + 1) * 128, :]), incs=[("ld_sync", 16)])
            ld = sch.val("ld_sync")
            for c in range(8):
                pe = sch.op("tensor", lambda e, c=c: e.transpose(
                    ps_small[0:128, 0:8], gate_sb[0:8, c * 128:(c + 1) * 128],
                    sb["c_ident_f32"][0:8, 0:8]),
                    waits=[("ld_sync", ld), ("dve", sch.val("dve"))],
                    incs=[("pe", 1)])
                sch.op("vector", lambda e, c=c: e.tensor_copy(
                    gateT[:, c, :], ps_small[0:128, 0:8]),
                    waits=[("pe", pe["pe"])], incs=[("dve", 1)])
                sch.fence()
            ldc = sch.op("sync", lambda e: e.dma_start(
                out=rows[0:1, 0:8], in_=cb_d.rearrange("(o d) -> o d", o=1)),
                incs=[("ld_sync", 16)])
            pe = sch.op("tensor", lambda e: e.matmul(
                ps_small[:, 0:8], onesr[0:1, :], rows[0:1, 0:8],
                start=True, stop=True),
                waits=[("ld_sync", ldc["ld_sync"])], incs=[("pe", 1)])
            sch.op("scalar", lambda e: e.copy(cbb[:], ps_small[:, 0:8]),
                   waits=[("pe", pe["pe"])], incs=[("act", 1)])
            sch.fence()

            for sq in range(SEQ_PER_CORE):
                self.seq(sq, x_in, y_out, w1T_d, w2T_d, upT_d, downT_d,
                         x_res, xn, fc1T, fc2T, w1e, w2e, dnT, upw,
                         h_up, lat, h1, comb, comb_fm, gateT, cbb, rt, rti,
                         x2, rows, stage, ps, ps_small)


    ln = AttnProgram.ln
    sbuf2 = None

    def seq(self, sq, x_in, y_out, w1T_d, w2T_d, upT_d, downT_d,
            x_res, xn, fc1T, fc2T, w1e, w2e, dnT, upw,
            h_up, lat, h1, comb, comb_fm, gateT, cbb, rt, rti,
            x2, rows, stage, ps, ps_small):
        nc, sch, sb = self.nc, self.sch, self.sb
        base = sq * S
        ident = sb["c_ident_f32"]
        iota8 = sb["c_iota8"]

        sch.fence()
        pe_after_tt = 0
        copy_done = {0: 0, 1: 0}
        for tt in range(16):
            ld = sch.op("sync", lambda e, tt=tt: e.dma_start(
                out=stage[:], in_=x_in[base + tt * 128: base + (tt + 1) * 128, :]),
                waits=[("pe", pe_after_tt)], incs=[("ld_sync", 16)])
            for c in range(8):
                par = c % 2
                pe = sch.op("tensor", lambda e, c=c, par=par: e.transpose(
                    self.ps_tp[par][:, 0:128], stage[:, c * 128:(c + 1) * 128],
                    ident[:]),
                    waits=[("ld_sync", ld["ld_sync"]), ("dve", copy_done[par])],
                    incs=[("pe", 1)])
                cp = sch.op("vector", lambda e, c=c, tt=tt, par=par: e.tensor_copy(
                    x_res[c][:, tt * 128:(tt + 1) * 128], self.ps_tp[par][:, 0:128]),
                    waits=[("pe", pe["pe"])], incs=[("dve", 1)])
                copy_done[par] = cp["dve"]
            pe_after_tt = sch.val("pe")
        sch.fence()

        for tch in range(4):
            self.ln(x_res, xn, tch * 512, 512, self._stats, ps[0:2])
        sch.fence()

        for tb in range(4):
            self.tile(sq, tb, y_out, w1T_d, w2T_d, upT_d, downT_d,
                      x_res, xn, fc1T, fc2T, w1e, w2e, dnT, upw,
                      h_up, lat, h1, comb, comb_fm, gateT, cbb, rt, rti,
                      ps, ps_small)

        copy_done = {0: 0, 1: 0}
        for tt in range(16):
            st_before = sch.val("st")
            for c in range(8):
                par = c % 2
                pe = sch.op("tensor", lambda e, c=c, tt=tt, par=par: e.transpose(
                    self.ps_tp[par][:, 0:128], x_res[c][:, tt * 128:(tt + 1) * 128],
                    ident[:]),
                    waits=[("st", st_before), ("dve", copy_done[par])],
                    incs=[("pe", 1)])
                cp = sch.op("vector", lambda e, c=c, par=par: e.tensor_copy(
                    stage[:, c * 128:(c + 1) * 128], self.ps_tp[par][:, 0:128]),
                    waits=[("pe", pe["pe"])], incs=[("dve", 1)])
                copy_done[par] = cp["dve"]
            dv_all = sch.val("dve")
            sch.op("gpsimd", lambda e, tt=tt: e.dma_start(
                out=y_out[base + tt * 128: base + (tt + 1) * 128, :], in_=stage[:]),
                waits=[("dve", dv_all)], incs=[("st", 16)])
            sch.fence()

    def tile(self, sq, tb, y_out, w1T_d, w2T_d, upT_d, downT_d,
             x_res, xn, fc1T, fc2T, w1e, w2e, dnT, upw,
             h_up, lat, h1, comb, comb_fm, gateT, cbb, rt, rti,
             ps, ps_small):
        sch, sb = self.sch, self.sb
        iota8 = sb["c_iota8"]
        x2 = sb["ln_x2"]
        cw = sb["cw"]
        t0 = tb * 512
        self._h1_evac = {}
        self._comb_rd = {}
        sch.fence()

        for st in range(4):
            for c in range(8):
                sch.op("tensor", lambda e, st=st, c=c: e.matmul(
                    ps_small[:, st * 8:(st + 1) * 8],
                    xn[c][:, t0 + st * 128: t0 + (st + 1) * 128],
                    gateT[:, c, :], start=(c == 0), stop=(c == 7)),
                    incs=[("pe", 1)] if c == 7 else ())
        pe_logits = sch.val("pe")
        combcol_done = []

        def vop(emit, extra=()):
            return sch.op("vector", emit,
                          waits=list(extra),
                          hard_waits=[("dve", sch.val("dve"))],
                          incs=[("dve", 1)])

        for st in range(4):
            sc_ = rt[:, 0:8]
            sfc = rt[:, 8:16]
            msk = rt[:, 16:24]
            t8v = rt[:, 24:32]
            oh1 = rt[:, 32:40]
            oh2 = rt[:, 40:48]
            tmp8 = rt[:, 48:56]
            s1 = rt[:, 56:57]
            s2 = rt[:, 57:58]
            m0 = rt[:, 58:59]
            m1 = rt[:, 59:60]
            w1n = rt[:, 60:61]
            w2n = rt[:, 61:62]
            g0 = rt[:, 62:63]
            g1 = rt[:, 63:64]
            ac = sch.op("scalar", lambda e, st=st: e.activation(
                sc_, ps_small[:, st * 8:(st + 1) * 8], AF.Sigmoid),
                waits=[("pe", pe_logits), ("dve", sch.val("dve"))],
                incs=[("act", 1)])
            vop(lambda e: e.tensor_add(sfc, sc_, cbb[:]),
                extra=[("act", ac["act"])])
            for g_, base_ in ((g0, 8), (g1, 12)):
                a_ = rt[:, base_:base_ + 1]
                b_ = rt[:, base_ + 1:base_ + 2]
                c_ = rt[:, base_ + 2:base_ + 3]
                d_ = rt[:, base_ + 3:base_ + 4]
                mx1 = rt[:, 48:49]
                mn1 = rt[:, 49:50]
                mx2 = rt[:, 50:51]
                mn2 = rt[:, 51:52]
                t1_ = rt[:, 52:53]
                t2_ = rt[:, 53:54]
                vop(lambda e, a_=a_, b_=b_, mx1=mx1: e.tensor_max(mx1, a_, b_))
                vop(lambda e, a_=a_, b_=b_, mn1=mn1: e.tensor_tensor(mn1, a_, b_, ALU.min))
                vop(lambda e, c_=c_, d_=d_, mx2=mx2: e.tensor_max(mx2, c_, d_))
                vop(lambda e, c_=c_, d_=d_, mn2=mn2: e.tensor_tensor(mn2, c_, d_, ALU.min))
                vop(lambda e, mn1=mn1, mx2=mx2, t1_=t1_: e.tensor_max(t1_, mn1, mx2))
                vop(lambda e, mn2=mn2, mx1=mx1, t2_=t2_: e.tensor_max(t2_, mn2, mx1))
                vop(lambda e, mx1=mx1, t1_=t1_: e.tensor_add(t1_, mx1, t1_))
                vop(lambda e, mx2=mx2, t2_=t2_: e.tensor_add(t2_, mx2, t2_))
                vop(lambda e, g_=g_, t1_=t1_, t2_=t2_: e.tensor_max(g_, t1_, t2_))
            vop(lambda e: e.tensor_tensor(m0, g0, g1, ALU.is_ge))
            vop(lambda e: e.tensor_scalar(m1, m0, -1.0, 1.0, ALU.mult, ALU.add))
            vop(lambda e: e.tensor_scalar_mul(rt[:, 16:20], rt[:, 8:12], m0))
            vop(lambda e: e.tensor_scalar_mul(rt[:, 20:24], rt[:, 12:16], m1))
            vop(lambda e: e.max(t8v, msk))
            vop(lambda e: e.max_index(rti[:], t8v, msk))
            vop(lambda e: e.tensor_copy(rt[:, 54:56], rti[:, 0:2]))
            i1 = rt[:, 54:55]
            i2 = rt[:, 55:56]
            vop(lambda e: e.tensor_single_scalar(oh1, iota8[:], i1, ALU.is_equal))
            vop(lambda e: e.tensor_single_scalar(oh2, iota8[:], i2, ALU.is_equal))
            vop(lambda e: e.tensor_mul(tmp8, oh1, sc_))
            vop(lambda e: e.reduce_sum(s1, tmp8, axis=mybir.AxisListType.X))
            vop(lambda e: e.tensor_mul(tmp8, oh2, sc_))
            vop(lambda e: e.reduce_sum(s2, tmp8, axis=mybir.AxisListType.X))
            vop(lambda e: e.tensor_add(w1n, s1, s2))
            vop(lambda e: e.tensor_scalar_add(w1n, w1n, 1e-20))
            vop(lambda e: e.reciprocal(w2n, w1n))
            vop(lambda e: e.tensor_mul(w1n, s1, w2n))
            vop(lambda e: e.tensor_mul(w2n, s2, w2n))
            vop(lambda e: e.tensor_scalar_mul(oh1, oh1, w1n))
            vop(lambda e: e.scalar_tensor_tensor(t8v, oh2, w2n, oh1, ALU.mult, ALU.add))
            vop(lambda e, st=st: e.tensor_copy(cw[:, st, :], t8v))
            combcol_done.append(sch.val("dve"))
            if sq == 0 and tb == 0 and st == 0:
                sch.op("gpsimd", lambda e: e.dma_start(
                    out=self.dbg[:, 2560:2624], in_=rt[:]),
                    waits=[("dve", sch.val("dve"))], incs=[("st", 16)])
                vop(lambda e: e.tensor_copy(rt[:, 48:56], rti[:]),
                    extra=[("st", sch.val("st"))])
                sch.op("gpsimd", lambda e: e.dma_start(
                    out=self.dbg[:, 2624:2632], in_=rt[:, 48:56]),
                    waits=[("dve", sch.val("dve"))], incs=[("st", 16)])
                sch.fence()
        sch.fence()

        lat_evac = {0: 0, 1: 0}
        dve_pre = sch.val("dve")
        for lc in range(4):
            for c in range(8):
                pe = sch.op("tensor", lambda e, lc=lc, c=c: e.matmul(
                    ps[lc % 2][:], fc1T[:, c, lc * 128:(lc + 1) * 128],
                    xn[c][:, t0:t0 + 512], start=(c == 0), stop=(c == 7)),
                    waits=[("dve", dve_pre), ("act", lat_evac[lc % 2])]
                    if c == 0 else (),
                    incs=[("pe", 1)] if c == 7 else ())
            a = sch.op("scalar", lambda e, lc=lc: e.copy(lat[:, lc, :], ps[lc % 2][:]),
                       waits=[("pe", pe["pe"])], incs=[("act", 1)])
            lat_evac[lc % 2] = a["act"]

        lat_done = sch.val("act")
        we_ld = {}
        for e_ in range(8):
            par = e_ % 2
            wld = sch.op("gpsimd", lambda e, e_=e_, par=par: e.dma_start(
                out=w1e[par][:], in_=w1T_d[e_].rearrange("(lc p) i -> p lc i", p=128)),
                waits=[("pe", we_ld.get(par, 0))], incs=[("ld_gp", 16)])
            wld2 = sch.op("gpsimd", lambda e, e_=e_, par=par: e.dma_start(
                out=w2e[par][:], in_=w2T_d[e_].rearrange("(ic p) l -> p ic l", p=128)),
                incs=[("ld_gp", 16)])
            for ic in range(4):
                par23 = 2 + ic % 2
                w0 = ([("ld_gp", wld2["ld_gp"]),
                       ("act", max(lat_done, self._h1_evac.get(par23, 0))),
                       ("dve", self._comb_rd.get(par23, 0))] if ic < 2 else
                      [("act", max(lat_done, self._h1_evac.get(par23, 0))),
                       ("dve", self._comb_rd.get(par23, 0))])
                for lc in range(4):
                    pe = sch.op("tensor", lambda e, ic=ic, lc=lc, par=par,
                                par23=par23: e.matmul(
                        ps[par23][:],
                        w1e[par][:, lc, ic * 128:(ic + 1) * 128],
                        lat[:, lc, :], start=(lc == 0), stop=(lc == 3)),
                        waits=(w0 if lc == 0 else ()),
                        incs=[("pe", 1)] if lc == 3 else ())
                a1 = sch.op("scalar", lambda e, ic=ic, par23=par23: e.activation(
                    x2[:, 0:512], ps[par23][:], AF.Relu),
                    waits=[("pe", pe["pe"])], incs=[("act", 1)])
                a2 = sch.op("scalar", lambda e, ic=ic: e.square(h1[:, ic, :], x2[:, 0:512]),
                            incs=[("act", 1)])
                self._h1_evac[par23] = a2["act"]
            act_h1 = sch.val("act")
            for st in range(4):
                par23 = 2 + st % 2
                for ic in range(4):
                    pe = sch.op("tensor", lambda e, st=st, ic=ic, par=par,
                                par23=par23: e.matmul(
                        ps[par23][:],
                        h1[:, ic, st * 128:(st + 1) * 128],
                        w2e[par][:, ic, :], start=(ic == 0), stop=(ic == 3)),
                        waits=([("act", act_h1),
                                ("dve", self._comb_rd.get(par23, 0)),
                                ("act", self._h1_evac.get(par23, 0))]
                               if ic == 0 else ()),
                        incs=[("pe", 1)] if ic == 3 else ())
                if e_ == 0:
                    cc = sch.op("vector", lambda e, st=st, e_=e_, par23=par23:
                                e.tensor_scalar_mul(
                        comb[:, st, :], ps[par23][:], cw[:, st, e_:e_ + 1]),
                        waits=[("pe", pe["pe"])],
                        hard_waits=[("dve", combcol_done[st])], incs=[("dve", 1)])
                else:
                    cc = sch.op("vector", lambda e, st=st, e_=e_, par23=par23:
                                e.scalar_tensor_tensor(
                        comb[:, st, :], ps[par23][:], cw[:, st, e_:e_ + 1],
                        comb[:, st, :], ALU.mult, ALU.add),
                        waits=[("pe", pe["pe"])], incs=[("dve", 1)])
                self._comb_rd[par23] = cc["dve"]
            we_ld[par] = sch.val("pe")
        sch.fence()

        cp_done = {0: 0, 1: 0}
        for st in range(4):
            for lc in range(4):
                par = lc % 2
                pe = sch.op("tensor", lambda e, st=st, lc=lc, par=par: e.transpose(
                    self.ps_tp[par][:, 0:128],
                    comb[:, st, lc * 128:(lc + 1) * 128], sb["c_ident_f32"][:]),
                    waits=[("dve", max(sch.val("dve") if st == 0 and lc == 0 else 0,
                                       cp_done[par]))],
                    incs=[("pe", 1)])
                cpd = sch.op("vector", lambda e, st=st, lc=lc, par=par: e.tensor_copy(
                    comb_fm[:, lc, st * 128:(st + 1) * 128],
                    self.ps_tp[par][:, 0:128]),
                    waits=[("pe", pe["pe"])], incs=[("dve", 1)])
                cp_done[par] = cpd["dve"]
        sch.fence()

        prev_up = {}
        up_evac = {0: 0, 1: 0}
        for fc in range(32):
            par = fc % 2
            srcw = upT_d.rearrange("(c p) f -> p c f", p=128)[
                :, :, fc * 128:(fc + 1) * 128]
            ldu = sch.op("gpsimd", lambda e, srcw=srcw, par=par: e.dma_start(
                out=upw[par][:], in_=srcw),
                waits=[("pe", prev_up[par])] if par in prev_up else (),
                incs=[("ld_gp", 16)])
            for c in range(8):
                pe = sch.op("tensor", lambda e, fc=fc, c=c, par=par: e.matmul(
                    ps[par][:], upw[par][:, c, :],
                    xn[c][:, t0:t0 + 512], start=(c == 0), stop=(c == 7)),
                    waits=([("ld_gp", ldu["ld_gp"]),
                            ("act", up_evac[par])] if c == 0 else ()),
                    incs=[("pe", 1)] if c == 7 else ())
            a1 = sch.op("scalar", lambda e, fc=fc, par=par: e.activation(
                x2[:, 0:512], ps[par][:], AF.Relu),
                waits=[("pe", pe["pe"])], incs=[("act", 1)])
            a2 = sch.op("scalar", lambda e, fc=fc: e.square(h_up[:, fc, :], x2[:, 0:512]),
                        incs=[("act", 1)])
            up_evac[par] = a2["act"]
            prev_up[par] = sch.val("pe")
        sch.fence()

        prev_dn = {}
        for dhalf in range(2):
            for fc in range(32):
                par = fc % 2
                ldd = sch.op("gpsimd", lambda e, fc=fc, par=par, dhalf=dhalf:
                             e.dma_start(
                    out=dnT[par][:],
                    in_=downT_d[fc * 128:(fc + 1) * 128,
                                dhalf * 512:(dhalf + 1) * 512]),
                    waits=[("pe", prev_dn[par])] if par in prev_dn else (),
                    incs=[("ld_gp", 16)])
                for dq in range(4):
                    sch.op("tensor", lambda e, fc=fc, dq=dq, par=par: e.matmul(
                        ps[dq][:], dnT[par][:, dq * 128:(dq + 1) * 128],
                        h_up[:, fc, :], start=(fc == 0), stop=False),
                        waits=([("ld_gp", ldd["ld_gp"]),
                                ("dve", sch.val("dve"))] if dq == 0 and fc == 0
                               else [("ld_gp", ldd["ld_gp"])] if dq == 0 else ()),
                        incs=[("pe", 1)] if dq == 3 else ())
                prev_dn[par] = sch.val("pe")
            for lc in range(4):
                for dq in range(4):
                    pe = sch.op("tensor", lambda e, lc=lc, dq=dq, dhalf=dhalf:
                                e.matmul(
                        ps[dq][:],
                        fc2T[:, lc, :][:, dhalf * 512 + dq * 128:
                                       dhalf * 512 + (dq + 1) * 128],
                        comb_fm[:, lc, :],
                        start=False, stop=(lc == 3)),
                        incs=[("pe", 1)] if lc == 3 and dq == 3 else ())
            pe_all = sch.val("pe")
            for dq in range(4):
                dch = dhalf * 4 + dq
                sch.op("vector", lambda e, dq=dq, dch=dch: e.tensor_add(
                    x_res[dch][:, t0:t0 + 512], ps[dq][:],
                    x_res[dch][:, t0:t0 + 512]),
                    waits=[("pe", pe_all)], incs=[("dve", 1)])
            sch.fence()
        if sq == 0 and tb == 0:
            sch.op("vector", lambda e: e.tensor_copy(x2[:, 0:512], h_up[:, 0, :]),
                   incs=[("dve", 1)])
            sch.op("gpsimd", lambda e: e.dma_start(
                out=self.dbg[:, 0:512], in_=x2[:, 0:512]),
                waits=[("dve", sch.val("dve"))], incs=[("st", 16)])
            sch.op("vector", lambda e: e.tensor_copy(x2[:, 0:512], lat[:, 0, :]),
                   waits=[("st", sch.val("st"))], incs=[("dve", 1)])
            sch.op("gpsimd", lambda e: e.dma_start(
                out=self.dbg[:, 512:1024], in_=x2[:, 0:512]),
                waits=[("dve", sch.val("dve"))], incs=[("st", 16)])
            sch.op("gpsimd", lambda e: e.dma_start(
                out=self.dbg[:, 1024:1536], in_=comb[:, 0, :]),
                incs=[("st", 16)])
            sch.op("gpsimd", lambda e: e.dma_start(
                out=self.dbg[:, 1536:1544], in_=cw[:, 0, :]),
                incs=[("st", 16)])
            sch.op("vector", lambda e: e.tensor_copy(x2[:, 0:512], xn[0][:, 0:512]),
                   waits=[("st", sch.val("st"))], incs=[("dve", 1)])
            sch.op("gpsimd", lambda e: e.dma_start(
                out=self.dbg[:, 2048:2560], in_=x2[:, 0:512]),
                waits=[("dve", sch.val("dve"))], incs=[("st", 16)])
            sch.fence()




class L1FfnProgram:

    CONSTS = ("c_ident_f32", "c_ident_bf", "c_iota8", "c_ones_row")

    def __init__(self, nc, sch=None, pre=""):
        self.nc = nc
        self.pre = pre
        self.sch = sch if sch is not None else Sched(nc)
        self.sb = {}

    def sbuf(self, ctx, name, shape, dt, **kw):
        t = ctx.enter_context(self.nc.sbuf_tensor(self.pre + name, shape, dt, **kw))
        self.sb[name] = t
        return t

    def build(self):
        nc, sch = self.nc, self.sch
        ctx = contextlib.ExitStack()
        cdram = declare_const_dram(nc, self.CONSTS)
        x_in = nc.dram_tensor("x", [SEQ_PER_CORE * S, D], F32, kind="ExternalInput").ap()
        y_out = nc.dram_tensor("y", [SEQ_PER_CORE * S, D], F32, kind="ExternalOutput").ap()
        with ctx:
            block = ctx.enter_context(nc.Block())
            for s_ in ("ld_sync", "ld_gp", "ld_prep", "ldt_prep", "st_prep",
                        "pe", "act", "dve", "pool", "st"):
                sch.add_sem(s_, ctx.enter_context(nc.semaphore(s_)))
            self.prep_stg = [ctx.enter_context(nc.sbuf_tensor(
                self.pre + f"prep_stg{i}", [128, 512], BF16,
                is_transpose_dest=True)) for i in range(2)]
            with contextlib.ExitStack() as pctx:
                self.emit_body(pctx, x_in, y_out, cdram)
                st_total = sch.val("st")
                sch.op("gpsimd", lambda e: e.nop(), waits=[("st", st_total)])
                sch.emit(block)
        return nc

    def declare_w(self):
        if getattr(self, "_declared", False):
            return
        self._declared = True
        nc, pre = self.nc, self.pre
        self.nw_d = nc.dram_tensor(pre + "norm_w", [D], F32, kind="ExternalInput").ap()
        self.wma_d = nc.dram_tensor(pre + "w_macro", [4, D], F32, kind="ExternalInput").ap()
        self.wmi_d = nc.dram_tensor(pre + "w_micro", [4, 2, D], F32, kind="ExternalInput").ap()
        self.w1_d = nc.dram_tensor(pre + "w1", [8, 2048, D], F32, kind="ExternalInput").ap()
        self.w2_d = nc.dram_tensor(pre + "w2", [8, D, 2048], F32, kind="ExternalInput").ap()
        self.w1_bf = nc.dram_tensor(pre + "w1_bf", [8, 2048, D], BF16, kind="Internal").ap()
        self.w2_bf = nc.dram_tensor(pre + "w2_bf", [8, D, 2048], BF16, kind="Internal").ap()
        self.w1T_d = nc.dram_tensor(pre + "w1T", [8, D, 2048], BF16, kind="Internal").ap()
        self.w2T_d = nc.dram_tensor(pre + "w2T", [8, 2048, D], BF16, kind="Internal").ap()

    def emit_prep_casts(self):
        self.declare_w()
        sch = self.sch
        self.casts_emitted = True
        cast_ld = 0
        for (s_, t_) in ((self.w1_d, self.w1_bf), (self.w2_d, self.w2_bf)):
            for e_ in range(8):
                cl = sch.op("gpsimd", lambda e, s_=s_, t_=t_, e_=e_: e.dma_start(
                    out=t_[e_], in_=s_[e_]), incs=[("ld_prep", 16)], nofence=True)
                cast_ld = cl["ld_prep"]
        self.cast_ld = cast_ld

    def emit_prep_T(self):
        if not getattr(self, "casts_emitted", False):
            self.emit_prep_casts()
        sch = self.sch
        self.prep_emitted = True
        state = {"i": 0, 0: sch.val("st_prep"), 1: sch.val("st_prep")}
        for e_ in range(8):
            emit_prep_T(sch, self.prep_stg, self.w1_bf[e_], self.w1T_d[e_], 2048, D,
                        self.cast_ld, state)
            emit_prep_T(sch, self.prep_stg, self.w2_bf[e_], self.w2T_d[e_], D, 2048,
                        self.cast_ld, state)
        flush_prep_T(sch, self.prep_stg, state)
        self.prep_st_total = sch.val("st_prep")

    def emit_body(self, ctx, x_in, y_out, cdram):
        nc, sch, sb = self.nc, self.sch, self.sb
        pre = self.pre
        self.declare_w()
        nw_d, wma_d, wmi_d = self.nw_d, self.wma_d, self.wmi_d
        w1T_d, w2T_d = self.w1T_d, self.w2T_d

        if True:
            for name in self.CONSTS:
                shape, dt = CONST_SPECS[name]
                dram = cdram[name]
                t = self.sbuf(ctx, name + "_sb", list(shape), dt)
                self.sb[name] = t
                sch.op("sync", lambda e, t=t, dram=dram: e.dma_start(
                    out=t[:], in_=dram), incs=[("ld_sync", 16)])

            ones_col = self.sbuf(ctx, "ones_col", [128, 1], F32)
            onesr = self.sbuf(ctx, "onesr", [1, 128], F32)
            sch.op("vector", lambda e: e.memset(ones_col[:], 1.0), incs=[("dve", 1)])
            sch.op("vector", lambda e: e.memset(onesr[:], 1.0), incs=[("dve", 1)])

            z = self.sbuf(ctx, "z", [128, 16, 1024], F32)
            zn_fm = self.sbuf(ctx, "zn_fm", [128, 8, S], BF16)
            w1e = self.sbuf(ctx, "w1e", [128, 8, 2048], BF16)
            w2e = self.sbuf(ctx, "w2e", [128, 16, 1024], BF16)
            h_fm = self.sbuf(ctx, "h_fm", [128, 16, 512], BF16)
            cw8 = self.sbuf(ctx, "cw8", [128, 16, 8], F32)
            nwb = self.sbuf(ctx, "nwb", [128, 1024], F32)
            wmT = self.sbuf(ctx, "wmT", [128, 8, 12], BF16)
            wm_sb = self.sbuf(ctx, "wm_sb", [12, D], F32)
            rt = self.sbuf(ctx, "rt", [128, 64], F32)
            rti = self.sbuf(ctx, "rti", [128, 8], mybir.dt.uint32)
            xs = self.sbuf(ctx, "xs", [128, 1024], F32)
            x2s = self.sbuf(ctx, "x2s", [128, 1024], F32)
            self.sbuf(ctx, "zn_bf", [128, 1024], BF16)
            rows = self.sbuf(ctx, "ln_rows", [1, 512], F32)

            ps = [ctx.enter_context(nc.psum_tensor(self.pre + f"psb{i}", [128, 512], F32))
                  for i in range(4)]
            ps_h = [ctx.enter_context(nc.psum_tensor(self.pre + f"psh{i}", [128, 512], F32))
                    for i in range(2)]
            ps_bt = [ctx.enter_context(nc.psum_tensor(self.pre + f"psbt{i}", [128, 1024], BF16))
                     for i in range(2)]
            self.ps, self.ps_h, self.ps_bt = ps, ps_h, ps_bt

            if not getattr(self, "prep_emitted", False):
                self.emit_prep_T()
            sch.fence_include_val("st_prep", self.prep_st_total)
            sch.op("sync", lambda e: e.dma_start(
                out=wm_sb[0:4, :], in_=wma_d), incs=[("ld_sync", 16)])
            sch.op("sync", lambda e: e.dma_start(
                out=wm_sb[4:12, :], in_=wmi_d.rearrange("g e d -> (g e) d")),
                incs=[("ld_sync", 16)])
            sch.fence()

            for half in range(2):
                ldh = sch.op("sync", lambda e, half=half: e.dma_start(
                    out=rows[0:1, 0:512],
                    in_=nw_d[half * 512:(half + 1) * 512]
                    .rearrange("(o d) -> o d", o=1)),
                    waits=[("pe", sch.val("pe")), ("act", sch.val("act"))],
                    incs=[("ld_sync", 16)])
                pe = sch.op("tensor", lambda e: e.matmul(
                    ps[0][:], onesr[0:1, :], rows[0:1, 0:512], start=True, stop=True),
                    waits=[("ld_sync", ldh["ld_sync"])], incs=[("pe", 1)])
                sch.op("scalar", lambda e, half=half: e.copy(
                    nwb[:, half * 512:(half + 1) * 512], ps[0][:]),
                    waits=[("pe", pe["pe"])], incs=[("act", 1)])
                sch.fence()
            for c in range(8):
                pe = sch.op("tensor", lambda e, c=c: e.transpose(
                    ps[1][0:128, 0:12], wm_sb[0:12, c * 128:(c + 1) * 128],
                    sb["c_ident_f32"][0:12, 0:12]),
                    waits=[("dve", sch.val("dve"))], incs=[("pe", 1)])
                sch.op("vector", lambda e, c=c: e.tensor_copy(
                    wmT[:, c, :], ps[1][0:128, 0:12]),
                    waits=[("pe", pe["pe"])], incs=[("dve", 1)])
                sch.fence()

            for sq in range(SEQ_PER_CORE):
                self.seq(sq, x_in, y_out, w1T_d, w2T_d)


    def vop(self, emit, extra=()):
        sch = self.sch
        return sch.op("vector", emit, waits=list(extra),
                      hard_waits=[("dve", sch.val("dve"))], incs=[("dve", 1)])

    def seq(self, sq, x_in, y_out, w1T_d, w2T_d):
        nc, sch, sb = self.nc, self.sch, self.sb
        z, zn_fm = sb["z"], sb["zn_fm"]
        xs, x2s, rows = sb["xs"], sb["x2s"], sb["ln_rows"]
        nwb, cw8 = sb["nwb"], sb["cw8"]
        ps, ps_h, ps_bt = self.ps, self.ps_h, self.ps_bt
        base = sq * S

        sch.fence()
        for tt in range(16):
            ld = sch.op("sync", lambda e, tt=tt: e.dma_start(
                out=xs[:], in_=x_in[base + tt * 128: base + (tt + 1) * 128, :]),
                waits=[("act", sch.val("act")), ("dve", sch.val("dve"))],
                incs=[("ld_sync", 16)])
            ac = sch.op("scalar", lambda e: e.square(x2s[:], xs[:]),
                        waits=[("ld_sync", ld["ld_sync"])], incs=[("act", 1)])
            v = self.vop(lambda e: e.reduce_sum(rt_col(sb, 0), xs[:],
                                                axis=mybir.AxisListType.X),
                         extra=[("ld_sync", ld["ld_sync"])])
            v = self.vop(lambda e: e.reduce_sum(rt_col(sb, 1), x2s[:],
                                                axis=mybir.AxisListType.X),
                         extra=[("act", ac["act"])])
            self.vop(lambda e: e.tensor_scalar_mul(rt_col(sb, 0), rt_col(sb, 0), 1.0 / D))
            self.vop(lambda e: e.tensor_scalar_mul(rt_col(sb, 1), rt_col(sb, 1), 1.0 / D))
            self.vop(lambda e: e.scalar_tensor_tensor(
                rt_col(sb, 2), rt_col(sb, 0), -1.0, rt_col(sb, 0), ALU.mult, ALU.mult))
            self.vop(lambda e: e.tensor_add(rt_col(sb, 1), rt_col(sb, 1), rt_col(sb, 2)))
            v = self.vop(lambda e: e.tensor_scalar_add(rt_col(sb, 1), rt_col(sb, 1), 1e-5))
            ac = sch.op("scalar", lambda e: e.activation(
                rt_col(sb, 1), rt_col(sb, 1), AF.Sqrt),
                waits=[("dve", v["dve"])], incs=[("act", 1)])
            self.vop(lambda e: e.reciprocal(rt_col(sb, 1), rt_col(sb, 1)),
                     extra=[("act", ac["act"])])
            v = self.vop(lambda e, tt=tt: e.tensor_scalar(
                z[:, tt, :], xs[:], rt_col(sb, 0), rt_col(sb, 1),
                ALU.subtract, ALU.mult))
        sch.fence()

        for lp in range(2):
            self.loop_iter(sq, lp, w1T_d, w2T_d)

        sch.fence()
        for tt in range(16):
            ld = sch.op("sync", lambda e, tt=tt: e.dma_start(
                out=xs[:], in_=x_in[base + tt * 128: base + (tt + 1) * 128, :]),
                waits=[("st", sch.val("st")), ("dve", sch.val("dve"))],
                incs=[("ld_sync", 16)])
            v = sch.op("vector", lambda e, tt=tt: e.tensor_add(
                xs[:], xs[:], z[:, tt, :]),
                waits=[("ld_sync", ld["ld_sync"])], incs=[("dve", 1)])
            sch.op("gpsimd", lambda e, tt=tt: e.dma_start(
                out=y_out[base + tt * 128: base + (tt + 1) * 128, :], in_=xs[:]),
                waits=[("dve", v["dve"])], incs=[("st", 16)])
            sch.fence()

    def loop_iter(self, sq, lp, w1T_d, w2T_d):
        nc, sch, sb = self.nc, self.sch, self.sb
        z, zn_fm = sb["z"], sb["zn_fm"]
        xs, x2s = sb["xs"], sb["x2s"]
        nwb, cw8, wmT = sb["nwb"], sb["cw8"], sb["wmT"]
        rt, rti, iota8 = sb["rt"], sb["rti"], sb["c_iota8"]
        ps, ps_h, ps_bt = self.ps, self.ps_h, self.ps_bt
        vop = self.vop
        sch.fence()

        tp_done = {0: 0, 1: 0}
        for tt in range(16):
            ac = sch.op("scalar", lambda e, tt=tt: e.square(x2s[:], z[:, tt, :]),
                        waits=[("dve", sch.val("dve"))], incs=[("act", 1)])
            v = self.vop(lambda e: e.reduce_sum(rt_col(sb, 0), x2s[:],
                                                axis=mybir.AxisListType.X),
                         extra=[("act", ac["act"])])
            self.vop(lambda e: e.tensor_scalar_mul(rt_col(sb, 0), rt_col(sb, 0), 1.0 / D))
            v = self.vop(lambda e: e.tensor_scalar_add(rt_col(sb, 0), rt_col(sb, 0), 1e-6))
            ac = sch.op("scalar", lambda e: e.activation(
                rt_col(sb, 0), rt_col(sb, 0), AF.Sqrt),
                waits=[("dve", v["dve"])], incs=[("act", 1)])
            self.vop(lambda e: e.reciprocal(rt_col(sb, 0), rt_col(sb, 0)),
                     extra=[("act", ac["act"])])
            self.vop(lambda e, tt=tt: e.tensor_scalar_mul(
                xs[:], z[:, tt, :], rt_col(sb, 0)))
            v = sch.op("vector", lambda e: e.tensor_mul(x2s[:], xs[:], nwb[:]),
                       incs=[("dve", 1)])
            ac = sch.op("scalar", lambda e: e.copy(sb["zn_bf"][:], x2s[:]),
                        waits=[("dve", v["dve"])], incs=[("act", 1)])
            for c in range(8):
                par = c % 2
                pe = sch.op("tensor", lambda e, c=c, par=par: e.transpose(
                    ps_bt[par][:, 0:128], sb["zn_bf"][:, c * 128:(c + 1) * 128],
                    sb["c_ident_bf"][:]),
                    waits=[("act", ac["act"]), ("dve", tp_done[par])],
                    incs=[("pe", 1)])
                cp = sch.op("vector", lambda e, c=c, tt=tt, par=par: e.tensor_copy(
                    zn_fm[:, c, tt * 128:(tt + 1) * 128], ps_bt[par][:, 0:128]),
                    waits=[("pe", pe["pe"])], incs=[("dve", 1)])
                tp_done[par] = cp["dve"]
        sch.fence()

        self._route_copy = 0
        zn_done = sch.val("dve")
        for tt in range(16):
            for c in range(8):
                pe = sch.op("tensor", lambda e, tt=tt, c=c: e.matmul(
                    ps[3][:, 0:12], zn_fm[:, c, tt * 128:(tt + 1) * 128],
                    wmT[:, c, :], start=(c == 0), stop=(c == 7)),
                    waits=[("dve", zn_done), ("act", self._route_copy)]
                    if c == 0 else (),
                    incs=[("pe", 1)] if c == 7 else ())
            self.route_dance(tt, pe["pe"])
        sch.fence()

        silu_evac = {0: 0, 1: 0}
        z_upd = {0: 0, 1: 0}
        h_last_pe = 0
        y_last_pe = 0
        for e_ in range(8):
            ld1 = sch.op("gpsimd", lambda e, e_=e_: e.dma_start(
                out=sb["w1e"][:], in_=w1T_d[e_].rearrange("(c p) f -> p c f", p=128)),
                waits=[("pe", h_last_pe)], incs=[("ld_gp", 16)])
            ld2 = sch.op("gpsimd", lambda e, e_=e_: e.dma_start(
                out=sb["w2e"][:], in_=w2T_d[e_].rearrange("(c p) d -> p c d", p=128)),
                waits=[("pe", y_last_pe)], incs=[("ld_gp", 16)])
            for tb in range(4):
                t0 = tb * 512
                for fc in range(16):
                    par = fc % 2
                    for c in range(8):
                        pe = sch.op("tensor", lambda e, fc=fc, c=c, par=par, t0=t0,
                                    e_=e_: e.matmul(
                            ps_h[par][:], sb["w1e"][:, c, fc * 128:(fc + 1) * 128],
                            zn_fm[:, c, t0:t0 + 512], start=(c == 0), stop=(c == 7)),
                            waits=([("ld_gp", ld1["ld_gp"]),
                                    ("act", silu_evac[par])] if c == 0 else ()),
                            incs=[("pe", 1)] if c == 7 else ())
                    a = sch.op("scalar", lambda e, fc=fc, par=par: e.activation(
                        sb["h_fm"][:, fc, :], ps_h[par][:], AF.Silu),
                        waits=[("pe", pe["pe"])], incs=[("act", 1)])
                    silu_evac[par] = a["act"]
                act_h = sch.val("act")
                if tb == 3:
                    h_last_pe = sch.val("pe")
                for st in range(4):
                    tt = tb * 4 + st
                    b0 = (st % 2) * 2
                    for fc in range(16):
                        pe = sch.op("tensor", lambda e, fc=fc, st=st, e_=e_, b0=b0:
                                    e.matmul(
                            ps[b0][:], sb["h_fm"][:, fc, st * 128:(st + 1) * 128],
                            sb["w2e"][:, fc, 0:512], start=(fc == 0), stop=(fc == 15)),
                            waits=([("act", act_h), ("ld_gp", ld2["ld_gp"]),
                                    ("dve", z_upd[st % 2])]
                                   if fc == 0 else ()),
                            incs=())
                        pe = sch.op("tensor", lambda e, fc=fc, st=st, e_=e_, b0=b0:
                                    e.matmul(
                            ps[b0 + 1][:], sb["h_fm"][:, fc, st * 128:(st + 1) * 128],
                            sb["w2e"][:, fc, 512:1024], start=(fc == 0), stop=(fc == 15)),
                            incs=[("pe", 1)] if fc == 15 else ())
                    for dh in range(2):
                        zu = sch.op("vector", lambda e, tt=tt, dh=dh, e_=e_, b0=b0:
                                    e.scalar_tensor_tensor(
                            z[:, tt, dh * 512:(dh + 1) * 512], ps[b0 + dh][:],
                            cw8[:, tt, e_:e_ + 1], z[:, tt, dh * 512:(dh + 1) * 512],
                            ALU.mult, ALU.add),
                            waits=[("pe", pe["pe"])], incs=[("dve", 1)])
                    z_upd[st % 2] = zu["dve"]
            y_last_pe = sch.val("pe")
        sch.fence()

    def route_dance(self, tt, pe_logits):
        sch, sb = self.sch, self.sb
        rt, rti, iota8 = sb["rt"], sb["rti"], sb["c_iota8"]
        vop = self.vop
        lg = rt[:, 0:12]
        mac = rt[:, 12:16]
        pad8 = rt[:, 16:24]
        t8v = rt[:, 24:32]
        li = rt[:, 32:36]
        oh = rt[:, 40:48]
        mx = rt[:, 48:49]
        ssum = rt[:, 49:50]
        v1 = rt[:, 50:51]
        v2 = rt[:, 51:52]
        i1 = rt[:, 52:53]
        i2 = rt[:, 53:54]
        e1 = rt[:, 54:55]
        e2 = rt[:, 55:56]
        tmp = rt[:, 56:57]
        ac = sch.op("scalar", lambda e: e.copy(lg, self.ps[3][:, 0:12]),
                    waits=[("pe", pe_logits), ("dve", sch.val("dve"))],
                    incs=[("act", 1)])
        self._route_copy = ac["act"]
        vop(lambda e: e.tensor_max(mx, rt[:, 0:1], rt[:, 1:2]),
            extra=[("act", ac["act"])])
        vop(lambda e: e.tensor_max(tmp, rt[:, 2:3], rt[:, 3:4]))
        vop(lambda e: e.tensor_max(mx, mx, tmp))
        v = vop(lambda e: e.tensor_scalar_mul(mx, mx, -1.0))
        ac = sch.op("scalar", lambda e: e.activation(mac, lg[:, 0:4], AF.Exp,
                                                     bias=mx),
                    waits=[("dve", v["dve"])], incs=[("act", 1)])
        vop(lambda e: e.reduce_sum(ssum, mac, axis=mybir.AxisListType.X),
            extra=[("act", ac["act"])])
        vop(lambda e: e.reciprocal(ssum, ssum))
        vop(lambda e: e.tensor_scalar_mul(mac, mac, ssum))
        vop(lambda e: e.memset(rt[:, 20:24], -1.0))
        vop(lambda e: e.tensor_copy(rt[:, 16:20], mac))
        v = vop(lambda e: e.max(t8v, pad8))
        v = vop(lambda e: e.max_index(rti[:], t8v, pad8))
        vop(lambda e: e.tensor_copy(rt[:, 52:54], rti[:, 0:2]))
        vop(lambda e: e.tensor_copy(v1, t8v[:, 0:1]))
        vop(lambda e: e.tensor_copy(v2, t8v[:, 1:2]))
        vop(lambda e: e.tensor_add(tmp, v1, v2))
        vop(lambda e: e.tensor_scalar_add(tmp, tmp, 1e-9))
        vop(lambda e: e.reciprocal(tmp, tmp))
        vop(lambda e: e.tensor_mul(v1, v1, tmp))
        vop(lambda e: e.tensor_mul(v2, v2, tmp))
        for g in range(4):
            vop(lambda e, g=g: e.tensor_tensor(
                rt[:, 36 + g:37 + g], rt[:, 4 + 2 * g:5 + 2 * g],
                rt[:, 5 + 2 * g:6 + 2 * g], ALU.is_lt))
        vop(lambda e: e.tensor_copy(li, rt[:, 36:40]))
        for (i_, e_out) in ((i1, e1), (i2, e2)):
            vop(lambda e, i_=i_: e.tensor_single_scalar(
                rt[:, 60:64], iota8[:, 0:4], i_, ALU.is_equal))
            vop(lambda e: e.tensor_mul(rt[:, 60:64], rt[:, 60:64], li))
            vop(lambda e, e_out=e_out: e.reduce_sum(
                e_out, rt[:, 60:64], axis=mybir.AxisListType.X))
            vop(lambda e, i_=i_, e_out=e_out: e.scalar_tensor_tensor(
                e_out, i_, 2.0, e_out, ALU.mult, ALU.add))
        vop(lambda e: e.tensor_single_scalar(oh, iota8[:], e1, ALU.is_equal))
        vop(lambda e: e.tensor_scalar_mul(oh, oh, v1))
        vop(lambda e: e.tensor_single_scalar(rt[:, 16:24], iota8[:], e2, ALU.is_equal))
        v = vop(lambda e: e.scalar_tensor_tensor(
            rt[:, 16:24], rt[:, 16:24], v2, oh, ALU.mult, ALU.add))
        vop(lambda e, tt=tt: e.tensor_copy(sb["cw8"][:, tt, :], rt[:, 16:24]))


def rt_col(sb, i):
    return sb["rt"][:, i:i + 1]


_CONSTS = None


def _consts():
    global _CONSTS
    if _CONSTS is None:
        _CONSTS = host_constants()
    return _CONSTS


def run_attn(x, inputs, layer):
    shards = np_split_tokens(x)
    consts = _consts()
    nc = bass.Bass("TRN2", target_bir_lowering=False)
    prog = AttnProgram(nc, layer)
    prog.build()
    pre = "L0_attn_" if layer == 0 else "L1_attn_"
    wnames = ("Wq", "Wk", "Wv", "Wo") + (("bq", "bk", "bv", "bo") if layer == 0 else ())
    in_maps = []
    for i in range(NCORES):
        m = {"x": shards[i]}
        for wn in wnames:
            m[wn] = np.asarray(inputs[pre + wn], dtype=np.float32)
        for cn in prog.CONSTS:
            m[cn] = consts[cn]
        in_maps.append(m)
    res = run_bass_kernel_spmd(nc, in_maps, core_ids=list(range(NCORES)))
    return np.concatenate([r["y"].reshape(SEQ_PER_CORE, S, D) for r in res.results],
                          axis=0)


def run_l0_ffn(x, inputs):
    shards = np_split_tokens(x)
    consts = _consts()
    nc = bass.Bass("TRN2", target_bir_lowering=False)
    prog = L0FfnProgram(nc)
    prog.build()
    in_maps = []
    for i in range(NCORES):
        m = {"x": shards[i]}
        for wn in ("gate", "corr_bias", "w1", "w2", "fc1_w", "fc2_w", "up_w", "down_w"):
            m[wn] = np.asarray(inputs["L0_ffn_" + wn], dtype=np.float32)
        for cn in prog.CONSTS:
            m[cn] = consts[cn]
        in_maps.append(m)
    res = run_bass_kernel_spmd(nc, in_maps, core_ids=list(range(NCORES)))
    run_l0_ffn.dbg = res.results[0].get("dbg")
    return np.concatenate([r["y"].reshape(SEQ_PER_CORE, S, D) for r in res.results],
                          axis=0)


def run_l1_ffn(x, inputs):
    shards = np_split_tokens(x)
    consts = _consts()
    nc = bass.Bass("TRN2", target_bir_lowering=False)
    prog = L1FfnProgram(nc)
    prog.build()
    in_maps = []
    for i in range(NCORES):
        m = {"x": shards[i]}
        for wn in ("norm_w", "w_macro", "w_micro", "w1", "w2"):
            m[wn] = np.asarray(inputs["L1_ffn_" + wn], dtype=np.float32)
        for cn in prog.CONSTS:
            m[cn] = consts[cn]
        in_maps.append(m)
    res = run_bass_kernel_spmd(nc, in_maps, core_ids=list(range(NCORES)))
    return np.concatenate([r["y"].reshape(SEQ_PER_CORE, S, D) for r in res.results],
                          axis=0)


def build_mega(nc):
    sch = Sched(nc)
    all_consts = tuple(CONST_SPECS.keys())
    cdram = declare_const_dram(nc, all_consts)
    x_in = nc.dram_tensor("x", [SEQ_PER_CORE * S, D], F32, kind="ExternalInput").ap()
    y_out = nc.dram_tensor("y", [SEQ_PER_CORE * S, D], F32, kind="ExternalOutput").ap()
    xh = [nc.dram_tensor(f"xh{i}", [SEQ_PER_CORE * S, D], F32, kind="Internal").ap()
          for i in range(2)]
    progs = [AttnProgram(nc, 0, sch=sch, pre="L0_attn_"),
             L0FfnProgram(nc, sch=sch, pre="L0_ffn_"),
             AttnProgram(nc, 1, sch=sch, pre="L1_attn_"),
             L1FfnProgram(nc, sch=sch, pre="L1_ffn_")]
    ios = [(x_in, xh[0]), (xh[0], xh[1]), (xh[1], xh[0]), (xh[0], y_out)]
    ctx = contextlib.ExitStack()
    with ctx:
        block = ctx.enter_context(nc.Block())
        for s_ in ("ld_sync", "ld_gp", "ld_prep", "ldt_prep", "st_prep",
                        "pe", "act", "dve", "pool", "st"):
            sch.add_sem(s_, ctx.enter_context(nc.semaphore(s_)))
        prep_stg = [ctx.enter_context(nc.sbuf_tensor(
            f"prep_stg{i}", [128, 512], BF16, is_transpose_dest=True))
            for i in range(2)]
        for p in progs:
            p.prep_stg = prep_stg
        progs[1].emit_prep()
        progs[2].emit_prep()
        for k, (prog, (i_, o_)) in enumerate(zip(progs, ios)):
            with contextlib.ExitStack() as pctx:
                if k == 1:
                    progs[3].emit_prep_casts()
                if k == 2:
                    progs[3].emit_prep_T()
                prog.emit_body(pctx, i_, o_, cdram)
                sch.fence()
                sch.emit(block)
        st_total = sch.val("st")
        sch.op("gpsimd", lambda e: e.nop(), waits=[("st", st_total)])
        sch.emit(block)
    return nc


MEGA_WNAMES = (
    "L0_attn_Wq", "L0_attn_Wk", "L0_attn_Wv", "L0_attn_Wo",
    "L0_attn_bq", "L0_attn_bk", "L0_attn_bv", "L0_attn_bo",
    "L0_ffn_gate", "L0_ffn_corr_bias", "L0_ffn_w1", "L0_ffn_w2",
    "L0_ffn_fc1_w", "L0_ffn_fc2_w", "L0_ffn_up_w", "L0_ffn_down_w",
    "L1_attn_Wq", "L1_attn_Wk", "L1_attn_Wv", "L1_attn_Wo",
    "L1_ffn_norm_w", "L1_ffn_w_macro", "L1_ffn_w_micro",
    "L1_ffn_w1", "L1_ffn_w2",
)


def kernel(**inputs):
    x = np.asarray(inputs["L0_attn_x"], dtype=np.float32)
    shards = np_split_tokens(x)
    consts = _consts()
    nc = bass.Bass("TRN2", target_bir_lowering=False)
    build_mega(nc)
    in_maps = []
    for i in range(NCORES):
        m = {"x": shards[i]}
        for wn in MEGA_WNAMES:
            m[wn] = np.asarray(inputs[wn], dtype=np.float32)
        m.update(consts)
        in_maps.append(m)
    res = run_bass_kernel_spmd(nc, in_maps, core_ids=list(range(NCORES)))
    return np.concatenate(
        [r["y"].reshape(SEQ_PER_CORE, S, D) for r in res.results],
        axis=0).astype(np.float32)


def kernel_unfused(**inputs):
    x = np.asarray(inputs["L0_attn_x"], dtype=np.float32)
    x = run_attn(x, inputs, 0)
    x = run_l0_ffn(x, inputs)
    x = run_attn(x, inputs, 1)
    x = run_l1_ffn(x, inputs)
    return x.astype(np.float32)
